# version 07
# speedup vs baseline: 1.0254x; 1.0254x over previous
import resource

try:
    resource.setrlimit(resource.RLIMIT_STACK,
                       (resource.RLIM_INFINITY, resource.RLIM_INFINITY))
except (ValueError, OSError):
    try:
        _soft, _hard = resource.getrlimit(resource.RLIMIT_STACK)
        resource.setrlimit(resource.RLIMIT_STACK, (_hard, _hard))
    except (ValueError, OSError):
        pass

import numpy as np
import concourse.bass as bass
import concourse.mybir as mybir
from concourse.bass_utils import run_bass_kernel_spmd

F32 = mybir.dt.float32
BF16 = mybir.dt.bfloat16
I32 = mybir.dt.int32
AF = mybir.ActivationFunctionType
ALU = mybir.AluOpType
AX = mybir.AxisListType

NCORES = 8
D = 1024
_SEQ_MARKS = []



class Prog:
    ENGINES = ("sync", "scalar", "vector", "tensor", "gpsimd")

    """Eager per-engine emission with build-time semaphore counting.

    DMA increments can be multiplied when bass splits one dma_start into
    several hardware DMAs; we read back the actual total from the emitted
    instructions.
    """

    def __init__(self, nc):
        self.nc = nc
        self.sems = {}
        self.cnt = {}
        self._block = nc.Block()
        self._block.__enter__()

    def sem(self, name):
        if name not in self.sems:
            self.sems[name] = self.nc.alloc_semaphore(name=name)
            self.cnt[name] = 0
        return self.sems[name]

    def val(self, name):
        self.sem(name)
        return self.cnt[name]

    def do(self, eng, fn, wait=(), inc=None, amt=1):
        wait = list(wait)
        if inc is not None and amt >= 16 and self.val(inc) > 0:
            wait.append((inc, self.cnt[inc]))
        wl = tuple((self.sem(n), v) for (n, v) in wait if v > 0)
        if inc is not None and amt >= 16:
            n_before = self.nc.n_instructions()
        out = {}

        def body(engine, fn=fn, wl=wl, out=out):
            for (sh, v) in wl:
                engine.wait_ge(sh, v)
            out["r"] = fn(engine)
        getattr(self._block, eng)(body)
        if inc is None:
            return None
        s = self.sem(inc)
        out["r"].then_inc(s, amt)
        if amt >= 16:
            names = self.nc.inst_names()
            ndma = sum(1 for n in names[n_before:]
                       if self.nc.get_inst_by_name(n).opcode == "DMACopy")
            total = amt * max(1, ndma)
        else:
            total = amt
        self.cnt[inc] += total
        return self.cnt[inc]

    def chain(self, eng, fns, wait=(), inc=None, amt=1, linked=False):
        wait = list(wait)
        if inc is not None and amt >= 16 and self.val(inc) > 0:
            wait.append((inc, self.cnt[inc]))
        wl = tuple((self.sem(n), v) for (n, v) in wait if v > 0)
        out = {}
        if linked:
            ls = self.sem("lnk")
            base = self.cnt["lnk"]
            self.cnt["lnk"] += len(fns) - 1
        else:
            ls, base = None, 0

        def body(engine, fns=fns, wl=wl, out=out, ls=ls, base=base):
            for (sh, v) in wl:
                engine.wait_ge(sh, v)
            r = None
            for i, fn in enumerate(fns):
                if ls is not None and i > 0:
                    r.then_inc(ls, 1)
                    engine.wait_ge(ls, base + i)
                r = fn(engine)
            out["r"] = r
        getattr(self._block, eng)(body)
        if inc is None:
            return None
        s = self.sem(inc)
        out["r"].then_inc(s, amt)
        self.cnt[inc] += amt
        return self.cnt[inc]

    def wait_only(self, eng, wait):
        wl = tuple((self.sem(n), v) for (n, v) in wait if v > 0)

        def body(engine, wl=wl):
            for (sh, v) in wl:
                engine.wait_ge(sh, v)
        getattr(self._block, eng)(body)

    def finish(self):
        self._block.__exit__(None, None, None)


def _ident_f32(p, identf, ones128):
    c = p.do("vector", lambda e, t=ones128: e.memset(t[:, :], 1.0), inc="v")
    p.do("gpsimd",
         lambda e, o=identf, i=ones128: e.affine_select(
             o[:, :], i[:, :], [[1, 128]], ALU.is_equal, 0.0,
             base=0, channel_multiplier=-1),
         wait=[("v", c)], inc="g")


def _emit_consts(p, tile, values):
    cols = []
    for i, v in enumerate(values):
        p.do("vector", lambda e, i=i, v=v, t=tile: e.memset(t[:, i:i + 1], v),
             inc="v")
        cols.append(tile[:, i:i + 1])
    return cols


def _emit_ln(p, xt, xn, stats, scr, tt, ld_cnt, eps_col, neghalf_col, cconst, prev_v=0, prev_s=0, prev_store=0):
    ssum = stats[:, 0:1]
    sq = stats[:, 1:2]
    negmean = stats[:, 2:3]
    qbar = stats[:, 3:4]
    msq = stats[:, 4:5]
    var = stats[:, 5:6]
    lnv = stats[:, 6:7]
    rstd = stats[:, 7:8]

    c0 = p.do("vector", lambda e, a=ssum, b=xt: e.reduce_sum(a, b[:, :], axis=AX.X),
         wait=[("ld", ld_cnt), ("v", prev_v)], inc="v")
    cs = p.do("scalar", lambda e, o=scr, b=xt, a=sq: e.activation(
        o[:, :], b[:, :], AF.Square, accum_out=a),
        wait=[("ld", ld_cnt), ("s", prev_s), ("v", prev_v)], inc="s")
    c1 = p.do("vector", lambda e, o=negmean, a=ssum: e.tensor_scalar(
        o, a, -1.0 / 1024.0, None, ALU.mult), wait=[("v", c0)], inc="v")
    c2 = p.do("vector", lambda e, o=qbar, a=sq: e.tensor_scalar(
        o, a, 1.0 / 1024.0, None, ALU.mult), wait=[("s", cs)], inc="v")
    c3 = p.do("vector", lambda e, o=msq, a=negmean: e.tensor_mul(o, a, a),
              wait=[("v", c1)], inc="v")
    cv = p.do("vector", lambda e, o=var, a=qbar, b=msq: e.tensor_sub(o, a, b),
              wait=[("v", c3)], inc="v")
    cln = p.do("scalar", lambda e, o=lnv, a=var, b=eps_col: e.activation(
        o, a, AF.Ln, bias=b), wait=[("v", cv), ("v", cconst)], inc="s")
    cr = p.do("scalar", lambda e, o=rstd, a=lnv, sc=neghalf_col: e.activation(
        o, a, AF.Exp, scale=sc), wait=[("s", cln)], inc="s")
    cxn = p.do("vector", lambda e, o=xn, b=xt, m=negmean, r=rstd: e.tensor_scalar(
        o[:, :], b[:, :], m, r, ALU.add, ALU.mult),
        wait=[("s", cr), ("v", cv), ("xnst", prev_store)], inc="v")
    return cxn



def build_ffn0():
    nc = bass.Bass("TRN2", target_bir_lowering=False)
    T = 2048
    NTT = T // 128
    E, H = 8, 4096
    NS = 32
    GS = 8
    NG = NS // GS

    x_in = nc.dram_tensor("x", [T, D], F32, kind="ExternalInput").ap()
    wg_in = nc.dram_tensor("wg", [D, E], F32, kind="ExternalInput").ap()
    w1_in = nc.dram_tensor("w1", [E, D, H], F32, kind="ExternalInput").ap()
    b1_in = nc.dram_tensor("b1", [E, H], F32, kind="ExternalInput").ap()
    w2_in = nc.dram_tensor("w2", [E, H, D], F32, kind="ExternalInput").ap()
    b2_in = nc.dram_tensor("b2", [E, D], F32, kind="ExternalInput").ap()
    y_out = nc.dram_tensor("y", [T, D], F32, kind="ExternalOutput").ap()
    xn_hbm = nc.dram_tensor("xn_hbm", [T, D], BF16).ap()

    p = Prog(nc)

    from contextlib import ExitStack
    with ExitStack() as ctx:
        sb = lambda n, sh, dt: ctx.enter_context(nc.sbuf_tensor(n, sh, dt))
        ps = lambda n, sh: ctx.enter_context(nc.psum_tensor(n, sh, F32))
        xt = sb("xt", [128, 1024], F32)
        xn = sb("xn", [128, 1024], BF16)
        scr = sb("scr", [128, 1024], F32)
        stats = sb("stats", [128, 8], F32)
        xnT = sb("xnT", [128, 8, T], BF16)
        hts = sb("hts", [128, GS, T], BF16)
        acc = sb("acc", [128, NTT, 1024], F32)
        w2g = sb("w2g", [128, GS, 1024], BF16)
        w1s = sb("w1s", [128, 2, 8, 128], BF16)
        wgsb = sb("wgsb", [128, 8, 8], BF16)
        b1nat = sb("b1nat", [8, H], F32)
        b1T = sb("b1T", [128, NS, 8], F32)
        b2sb = sb("b2sb", [8, 1024], BF16)
        mw = sb("mw", [128, NTT, 8], F32)
        mwTb = sb("mwTb", [8, T], BF16)
        gatef = sb("gatef", [128, 8], F32)
        m8g = sb("m8g", [128, 8], F32)
        gcol = sb("gcol", [128, 8], F32)
        eg = sb("eg", [128, 8], F32)
        consts = sb("consts", [128, 4], F32)
        identf = sb("identf", [128, 128], F32)
        ones128 = sb("ones128", [128, 128], F32)
        ph0 = ps("ph0", [128, 512]); ph1 = ps("ph1", [128, 512])
        pO0 = ps("pO0", [128, 512]); pO1 = ps("pO1", [128, 512])
        pO2 = ps("pO2", [128, 512]); pO3 = ps("pO3", [128, 512])
        pta = ps("pta", [128, 512]); ptb = ps("ptb", [128, 512])

        pO = [pO0, pO1, pO2, pO3]
        ph = [ph0, ph1]

        _ident_f32(p, identf, ones128)
        ident_done = p.val("g")
        ccols = _emit_consts(p, consts, [1e-5, -0.5])
        cconst = p.val("v")

        p.do("gpsimd", lambda e: e.dma_start(
            out=wgsb[:, :, :],
            in_=wg_in.rearrange("(kt kp) f -> kp kt f", kp=128)),
            inc="wld", amt=16)
        p.do("gpsimd", lambda e: e.dma_start(out=b2sb[:, :], in_=b2_in),
             inc="wld", amt=16)
        cb1 = p.do("sync", lambda e: e.dma_start(out=b1nat[:, :], in_=b1_in),
             inc="wldh", amt=16)
        wld0 = p.val("wld")

        for s in range(NS):
            ct = p.do("tensor", lambda e, s=s: e.transpose(
                pta[0:128, 0:8] if s % 2 == 0 else ptb[0:128, 0:8],
                b1nat[0:8, s * 128:(s + 1) * 128], identf[0:8, 0:8]),
                wait=[("wldh", cb1), ("g", ident_done), ("v", p.val("v"))],
                inc="mmt")
            p.do("vector", lambda e, s=s, ct=ct: e.tensor_copy(
                b1T[:, s, :], (pta if s % 2 == 0 else ptb)[0:128, 0:8]),
                wait=[("mmt", ct)], inc="v")
            if s >= 1:
                pass
        b1T_done_v = p.val("v")

        xn_sem_counts = []
        for tt in range(NTT):
            xap = x_in[tt * 128:(tt + 1) * 128, :]
            w = [("v", xn_sem_counts[-1])] if xn_sem_counts else []
            cl = p.do("sync", lambda e, a=xap: e.dma_start(out=xt[:, :], in_=a),
                      wait=w, inc="ld", amt=16)
            cxn = _emit_ln(p, xt, xn, stats, scr, tt, cl, ccols[0], ccols[1],
                           cconst, prev_v=(xn_sem_counts[-1] if xn_sem_counts else cconst),
                           prev_s=p.val("s"), prev_store=p.val("xnst"))
            p.do("sync", lambda e, o=xn_hbm[tt * 128:(tt + 1) * 128, :]:
                 e.dma_start(out=o, in_=xn[:, :]), wait=[("v", cxn)],
                 inc="xnst", amt=16)
            xn_sem_counts.append(cxn)
        xnst_all = p.val("xnst")

        for dt in range(8):
            p.do("sync", lambda e, dt=dt: e.dma_start(
                out=xnT[:, dt, :], in_=xn_hbm[:, dt * 128:(dt + 1) * 128],
                transpose=True),
                wait=[("xnst", xnst_all)], inc="xnT", amt=16)
        xnT_all = p.val("xnT")

        for tt in range(NTT):
            for kt in range(8):
                ct = p.do("tensor", lambda e, kt=kt, tt=tt: e.matmul(
                    pta[0:128, 0:8], xnT[:, kt, tt * 128:(tt + 1) * 128],
                    wgsb[:, kt, :], start=(kt == 0), stop=(kt == 7)),
                    wait=([("xnT", xnT_all), ("wld", wld0),
                           ("v", p.val("v"))] if kt == 0 else []),
                    inc=("mmt" if kt == 7 else None))
            cg = p.do("vector", lambda e: e.tensor_copy(gatef[:, :], pta[0:128, 0:8]),
                      wait=[("mmt", ct)], inc="v")
            cm = p.do("vector", lambda e: e.max(out=m8g[:, :], in_=gatef[:, :]),
                      wait=[("v", cg)], inc="v")
            cn = p.do("vector", lambda e: e.tensor_scalar(
                gcol[:, 0:1], m8g[:, 0:1], -1.0, None, ALU.mult),
                wait=[("v", cm)], inc="v")
            ce = p.do("scalar", lambda e: e.activation(
                eg[:, :], gatef[:, :], AF.Exp, bias=gcol[:, 0:1]),
                wait=[("v", cn)], inc="s")
            cz = p.do("vector", lambda e: e.scalar_tensor_tensor(
                eg[:, :], gatef[:, :], m8g[:, 1:2], eg[:, :],
                ALU.is_ge, ALU.mult, accum_out=gcol[:, 1:2]),
                wait=[("s", ce), ("v", cn)], inc="v")
            cz8 = p.do("vector", lambda e: e.tensor_scalar(
                gcol[:, 2:3], gcol[:, 1:2], 8.0, None, ALU.mult),
                wait=[("v", cz)], inc="v")
            crc = p.do("vector", lambda e: e.reciprocal(gcol[:, 3:4], gcol[:, 2:3]),
                       wait=[("v", cz8)], inc="v")
            p.do("vector", lambda e, tt=tt: e.tensor_scalar(
                mw[:, tt, :], eg[:, :], gcol[:, 3:4], None, ALU.mult),
                wait=[("v", crc)], inc="v")
        mw_done = p.val("v")

        for tt in range(NTT):
            ct = p.do("tensor", lambda e, tt=tt: e.transpose(
                ptb[0:8, 0:128], mw[:, tt, :], identf[:, :]),
                wait=[("v", mw_done), ("s", p.val("s"))], inc="mmt")
            p.do("scalar", lambda e, tt=tt, ct=ct: e.activation(
                mwTb[0:8, tt * 128:(tt + 1) * 128], ptb[0:8, 0:128],
                AF.Copy), wait=[("mmt", ct)], inc="s")
        mwT_done = p.val("s")
        for tt in range(NTT):
            for dc in range(2):
                ct = p.do("tensor", lambda e, tt=tt, dc=dc: e.matmul(
                    pO[dc][0:128, 0:512],
                    mwTb[0:8, tt * 128:(tt + 1) * 128],
                    b2sb[0:8, dc * 512:(dc + 1) * 512],
                    start=True, stop=True),
                    wait=[("s", mwT_done), ("wld", wld0),
                          ("vacc", p.val("vacc"))], inc="mmt")
                p.do("vector", lambda e, tt=tt, dc=dc, ct=ct: e.tensor_copy(
                    acc[:, tt, dc * 512:(dc + 1) * 512], pO[dc][0:128, 0:512]),
                    wait=[("mmt", ct)], inc="vacc")
        acc_init = p.val("vacc")

        hts_free_at = 0
        ph_evac = {}
        ph_iter = 0
        w2_free_at = 0
        for e_i in range(E):
            for g in range(NG):
                cw2 = p.do("gpsimd", lambda en, e_i=e_i, g=g: en.dma_start(
                    out=w2g[:, :, :],
                    in_=w2_in[e_i, g * (GS * 128):(g + 1) * (GS * 128), :]
                    .rearrange("(st hp) d -> hp st d", hp=128)),
                    wait=[("mmo", w2_free_at)], inc="w2ld", amt=16)
                for s_l in range(GS):
                    s = g * GS + s_l
                    cw1 = p.do("gpsimd", lambda en, e_i=e_i, s=s: en.dma_start(
                        out=w1s[:, s % 2, :, :],
                        in_=w1_in[e_i, :, s * 128:(s + 1) * 128]
                        .rearrange("(kt kp) f -> kp kt f", kp=128)),
                        wait=[("mmh", max(0, p.val("mmh") - 4))],
                        inc="w1ld", amt=16)
                    for blk in range(4):
                        bank = ph[ph_iter % 2]
                        wv = [("sh", ph_evac.get(ph_iter - 2, 0))]
                        for kt in range(8):
                            first = (kt == 0)
                            cmh = p.do("tensor", lambda en, kt=kt, s=s, blk=blk,
                                       bank=bank: en.matmul(
                                           bank[0:128, 0:512],
                                           w1s[:, s % 2, kt, :],
                                           xnT[:, kt, blk * 512:(blk + 1) * 512],
                                           start=(kt == 0), stop=(kt == 7)),
                                       wait=([("w1ld", cw1), ("xnT", xnT_all),
                                              ("vacc", hts_free_at)] + wv
                                             if first else []),
                                       inc=("mmh" if kt == 7 else None))
                        ce = p.do("scalar", lambda en, s=s, s_l=s_l, blk=blk,
                                  e_i=e_i, bank=bank, cmh=cmh: en.activation(
                                      hts[:, s_l, blk * 512:(blk + 1) * 512],
                                      bank[0:128, 0:512], AF.Relu,
                                      bias=b1T[:, s, e_i:e_i + 1]),
                                  wait=[("mmh", cmh),
                                        ("v", b1T_done_v)], inc="sh")
                        ph_evac[ph_iter] = ce
                        ph_iter += 1
                f1_done = p.val("sh")
                for tb in range(8):
                    for s_l in range(GS):
                        for sub in range(2):
                            for dc in range(2):
                                bi = sub * 2 + dc
                                first = (s_l == 0)
                                last = (s_l == GS - 1)
                                cmo = p.do("tensor", lambda en, s_l=s_l, tb=tb,
                                           sub=sub, dc=dc, bi=bi: en.matmul(
                                               pO[bi][0:128, 0:512],
                                               hts[:, s_l,
                                                   tb * 256 + sub * 128:
                                                   tb * 256 + sub * 128 + 128],
                                               w2g[:, s_l, dc * 512:(dc + 1) * 512],
                                               start=first, stop=last),
                                           wait=([("sh", f1_done),
                                                  ("w2ld", cw2),
                                                  ("vacc", p.val("vacc"))]
                                                 if (first and s_l == 0 and sub == 0
                                                     and dc == 0) else []),
                                           inc=("mmo" if last else None))
                    for sub in range(2):
                        for dc in range(2):
                            bi = sub * 2 + dc
                            tt_i = tb * 2 + sub
                            p.do("vector", lambda en, bi=bi, tt_i=tt_i, dc=dc,
                                 e_i=e_i, cmo=cmo: en.scalar_tensor_tensor(
                                     acc[:, tt_i, dc * 512:(dc + 1) * 512],
                                     pO[bi][0:128, 0:512],
                                     mw[:, tt_i, e_i:e_i + 1],
                                     acc[:, tt_i, dc * 512:(dc + 1) * 512],
                                     ALU.mult, ALU.add),
                                 wait=[("mmo", cmo), ("v", mw_done),
                                       ("vacc", p.val("vacc"))], inc="vacc")
                hts_free_at = p.val("vacc")
                w2_free_at = p.val("mmo")

        acc_done = p.val("vacc")

        for tt in range(NTT):
            wv = [("vacc", acc_done)] if tt == 0 else []
            cl = p.do("sync", lambda e, tt=tt: e.dma_start(
                out=xt[:, :], in_=x_in[tt * 128:(tt + 1) * 128, :]),
                wait=wv + [("vG", p.val("vG"))], inc="ld2", amt=16)
            cv = p.do("vector", lambda e, tt=tt: e.tensor_add(
                scr[:, :], xt[:, :], acc[:, tt, :]),
                wait=[("ld2", cl), ("st", p.val("st"))], inc="vG")
            p.do("sync", lambda e, tt=tt: e.dma_start(
                out=y_out[tt * 128:(tt + 1) * 128, :], in_=scr[:, :]),
                wait=[("vG", cv)], inc="st", amt=16)
        p.wait_only("gpsimd", [("st", p.val("st"))])

        p.finish()
    return nc


_PROG_CACHE = {}


def _get(name, builder):
    if name not in _PROG_CACHE:
        _PROG_CACHE[name] = builder()
    return _PROG_CACHE[name]


def run_ffn0(x, inputs):
    nc = _get("ffn0", build_ffn0)
    shards = [x[c * 2048:(c + 1) * 2048] for c in range(NCORES)]
    w = dict(
        wg=np.ascontiguousarray(inputs["L0_ffn_W_gate"]),
        w1=np.ascontiguousarray(inputs["L0_ffn_W1"]),
        b1=np.ascontiguousarray(inputs["L0_ffn_b1"]),
        w2=np.ascontiguousarray(inputs["L0_ffn_W2"]),
        b2=np.ascontiguousarray(inputs["L0_ffn_b2"]),
    )
    in_maps = [dict(x=np.ascontiguousarray(s), **w) for s in shards]
    r = run_bass_kernel_spmd(nc, in_maps, core_ids=list(range(NCORES)))
    return np.concatenate([r.results[c]["y"] for c in range(NCORES)], axis=0)


def kernel(**inputs):
    x = np.ascontiguousarray(inputs["L0_attn_a_x"], dtype=np.float32)
    B, S, Dm = x.shape
    xf = x.reshape(B * S, Dm)
    xf = run_fused(xf, inputs)
    return xf.reshape(B, S, Dm)



def build_ffn1():
    nc = bass.Bass("TRN2", target_bir_lowering=False)
    T = 2048
    NTT = T // 128
    E, HR = 64, 256
    ES, HS = 2, 2048

    x_in = nc.dram_tensor("x", [T, D], F32, kind="ExternalInput").ap()
    rw_in = nc.dram_tensor("rw", [D, E], F32, kind="ExternalInput").ap()
    rb_in = nc.dram_tensor("rb", [1, E], F32, kind="ExternalInput").ap()
    w1_in = nc.dram_tensor("w1", [E, D, HR], F32, kind="ExternalInput").ap()
    b1_in = nc.dram_tensor("b1", [E, HR], F32, kind="ExternalInput").ap()
    w2_in = nc.dram_tensor("w2", [E, HR, D], F32, kind="ExternalInput").ap()
    b2_in = nc.dram_tensor("b2", [E, D], F32, kind="ExternalInput").ap()
    sw1_in = nc.dram_tensor("sw1", [ES, D, HS], F32, kind="ExternalInput").ap()
    sb1_in = nc.dram_tensor("sb1", [ES, HS], F32, kind="ExternalInput").ap()
    sw2_in = nc.dram_tensor("sw2", [ES, HS, D], F32, kind="ExternalInput").ap()
    sb2_in = nc.dram_tensor("sb2", [ES, D], F32, kind="ExternalInput").ap()
    y_out = nc.dram_tensor("y", [T, D], F32, kind="ExternalOutput").ap()
    xn_hbm = nc.dram_tensor("xn_hbm", [T, D], BF16).ap()

    p = Prog(nc)
    from contextlib import ExitStack
    with ExitStack() as ctx:
        sb = lambda n, sh, dt: ctx.enter_context(nc.sbuf_tensor(n, sh, dt))
        ps = lambda n, sh: ctx.enter_context(nc.psum_tensor(n, sh, F32))
        xt = sb("xt", [128, 1024], F32)
        xn = sb("xn", [128, 1024], BF16)
        scr = sb("scr", [128, 1024], F32)
        stats = sb("stats", [128, 8], F32)
        xnT = sb("xnT", [128, 8, T], BF16)
        acc = sb("acc", [128, NTT, 1024], F32)
        u = sb("u", [128, 40960], BF16)
        rwsb = sb("rwsb", [128, 8, 64], BF16)
        rbsb = sb("rbsb", [1, 64], BF16)
        b1rT = sb("b1rT", [128, 2, 64], F32)
        sb1T = sb("sb1T", [128, 16, 2], F32)
        logits = sb("logits", [128, 64], F32)
        m8r = sb("m8r", [128, 8], F32)
        rcol = sb("rcol", [128, 8], F32)
        er = sb("er", [128, 64], F32)
        cwn = sb("cwn", [128, NTT, 64], F32)
        cwTb = sb("cwTb", [64, T], BF16)
        ones64 = sb("ones64", [64, 128], BF16)
        sel64 = sb("sel64", [64, 128], BF16)
        onesb = sb("onesb", [1, 128], BF16)
        identf = sb("identf", [128, 128], F32)
        ones128 = sb("ones128", [128, 128], F32)
        consts = sb("consts", [128, 4], F32)
        ph0 = ps("ph0", [128, 512]); ph1 = ps("ph1", [128, 512])
        pO0 = ps("pO0", [128, 512]); pO1 = ps("pO1", [128, 512])
        pO2 = ps("pO2", [128, 512]); pO3 = ps("pO3", [128, 512])
        pta = ps("pta", [128, 512]); ptb = ps("ptb", [128, 512])
        pO = [pO0, pO1, pO2, pO3]
        ph = [ph0, ph1]

        sb1a_t = u[0:1, 0:4096].bitcast(F32)
        sb1b_t = u[0:1, 4096:8192].bitcast(F32)
        b1nat = u[0:64, 8192:8704].bitcast(F32)
        b2sb = u[0:64, 8704:9728]
        sb2a = u[0:1, 9728:10752]
        sb2b = u[0:1, 10752:11776]
        w1g = u[:, 0:8192].rearrange("p (e kt f) -> p e kt f", e=4, kt=8)
        w2g = u[:, 8192:16384].rearrange("p (e ht f) -> p e ht f", e=4, ht=2)
        cwb = u[:, 16384:24576].rearrange("p (e f) -> p e f", e=4)
        hseg = u[:, 24576:40960].rearrange("p (e ht f) -> p e ht f", e=4, ht=2)
        s_hts = u[:, 0:16384].rearrange("p (s f) -> p s f", s=8)
        s_w2g = u[:, 16384:24576].rearrange("p (s f) -> p s f", s=8)
        s_w1s = u[:, 24576:26624].rearrange("p (b kt f) -> p b kt f", b=2, kt=8)

        _ident_f32(p, identf, ones128)
        ident_done = p.val("g")
        ccols = _emit_consts(p, consts, [1e-5, -0.5])
        p.do("vector", lambda e: e.memset(onesb[:, :], 1.0), inc="v")
        cones = p.do("vector", lambda e: e.memset(ones64[:, :], 1.0), inc="v")
        cconst = p.val("v")

        p.do("gpsimd", lambda e: e.dma_start(
            out=rwsb[:, :, :], in_=rw_in.rearrange("(kt kp) f -> kp kt f", kp=128)),
            inc="wld", amt=16)
        p.do("gpsimd", lambda e: e.dma_start(out=rbsb[:, :], in_=rb_in),
             inc="wld", amt=16)
        p.do("gpsimd", lambda e: e.dma_start(out=b2sb[:, :], in_=b2_in),
             inc="wld", amt=16)
        p.do("gpsimd", lambda e: e.dma_start(out=sb2a[:, :], in_=sb2_in[0:1, :]),
             inc="wld", amt=16)
        p.do("gpsimd", lambda e: e.dma_start(out=sb2b[:, :], in_=sb2_in[1:2, :]),
             inc="wld", amt=16)
        wld0 = p.val("wld")
        cb1 = p.do("sync", lambda e: e.dma_start(out=b1nat[:, :], in_=b1_in),
                   inc="wldh", amt=16)
        cb1s = p.do("sync", lambda e: e.dma_start(out=sb1a_t[:, :], in_=sb1_in[0:1, :]),
                    inc="wldh", amt=16)
        cb1s = p.do("sync", lambda e: e.dma_start(out=sb1b_t[:, :], in_=sb1_in[1:2, :]),
                    inc="wldh", amt=16)

        for ht in range(2):
            ct = p.do("tensor", lambda e, ht=ht: e.transpose(
                (pta if ht == 0 else ptb)[0:128, 0:64],
                b1nat[0:64, ht * 128:(ht + 1) * 128], identf[0:64, 0:64]),
                wait=[("wldh", cb1), ("g", ident_done), ("v", p.val("v"))],
                inc="mmt")
            p.do("vector", lambda e, ht=ht, ct=ct: e.tensor_copy(
                b1rT[:, ht, :], (pta if ht == 0 else ptb)[0:128, 0:64]),
                wait=[("mmt", ct)], inc="v")
        for se in range(2):
            for st_i in range(16):
                bank = pta if (st_i % 2 == 0) else ptb
                src = (sb1a_t if se == 0 else sb1b_t)
                ct = p.do("tensor", lambda e, src=src, st_i=st_i, bank=bank:
                          e.transpose(bank[0:128, 0:1],
                                      src[0:1, st_i * 128:(st_i + 1) * 128],
                                      identf[0:1, 0:1]),
                          wait=[("wldh", cb1s), ("v", p.val("v"))], inc="mmt")
                p.do("vector", lambda e, se=se, st_i=st_i, ct=ct, bank=bank:
                     e.tensor_copy(sb1T[:, st_i, se:se + 1], bank[0:128, 0:1]),
                     wait=[("mmt", ct)], inc="v")
        prep_v = p.val("v")

        xn_sem_counts = []
        for tt in range(NTT):
            xap = x_in[tt * 128:(tt + 1) * 128, :]
            w = [("v", xn_sem_counts[-1])] if xn_sem_counts else []
            cl = p.do("sync", lambda e, a=xap: e.dma_start(out=xt[:, :], in_=a),
                      wait=w, inc="ld", amt=16)
            cxn = _emit_ln(p, xt, xn, stats, scr, tt, cl, ccols[0], ccols[1],
                           cconst, prev_v=(xn_sem_counts[-1] if xn_sem_counts else cconst),
                           prev_s=p.val("s"), prev_store=p.val("xnst"))
            p.do("sync", lambda e, o=xn_hbm[tt * 128:(tt + 1) * 128, :]:
                 e.dma_start(out=o, in_=xn[:, :]), wait=[("v", cxn)],
                 inc="xnst", amt=16)
            xn_sem_counts.append(cxn)
        xnst_all = p.val("xnst")

        for dt in range(8):
            p.do("sync", lambda e, dt=dt: e.dma_start(
                out=xnT[:, dt, :], in_=xn_hbm[:, dt * 128:(dt + 1) * 128],
                transpose=True),
                wait=[("xnst", xnst_all)], inc="xnT", amt=16)
        xnT_all = p.val("xnT")

        sb1T_sel_busy = prep_v
        for tt in range(NTT):
            p.do("tensor", lambda e: e.matmul(
                pta[0:128, 0:64], onesb[0:1, :], rbsb[0:1, :],
                start=True, stop=False),
                wait=[("xnT", xnT_all), ("wld", wld0), ("v", p.val("v"))],
                inc=None)
            for kt in range(8):
                ct = p.do("tensor", lambda e, kt=kt, tt=tt: e.matmul(
                    pta[0:128, 0:64], xnT[:, kt, tt * 128:(tt + 1) * 128],
                    rwsb[:, kt, :], start=False, stop=(kt == 7)),
                    inc=("mmt" if kt == 7 else None))
            cg = p.do("vector", lambda e: e.tensor_copy(logits[:, :], pta[0:128, 0:64]),
                      wait=[("mmt", ct)], inc="v")
            cm = p.do("vector", lambda e: e.max(out=m8r[:, :], in_=logits[:, :]),
                      wait=[("v", cg)], inc="v")
            cn = p.do("vector", lambda e: e.tensor_scalar(
                rcol[:, 0:1], m8r[:, 0:1], -1.0, None, ALU.mult),
                wait=[("v", cm)], inc="v")
            ce = p.do("scalar", lambda e: e.activation(
                er[:, :], logits[:, :], AF.Exp, bias=rcol[:, 0:1]),
                wait=[("v", cn)], inc="s")
            cz = p.do("vector", lambda e: e.scalar_tensor_tensor(
                er[:, :], logits[:, :], m8r[:, 7:8], er[:, :],
                ALU.is_ge, ALU.mult, accum_out=rcol[:, 1:2]),
                wait=[("s", ce), ("v", cn)], inc="v")
            crc = p.do("vector", lambda e: e.reciprocal(rcol[:, 2:3], rcol[:, 1:2]),
                       wait=[("v", cz)], inc="v")
            p.do("vector", lambda e, tt=tt: e.tensor_scalar(
                cwn[:, tt, :], er[:, :], rcol[:, 2:3], None, ALU.mult),
                wait=[("v", crc)], inc="v")
        cw_done = p.val("v")

        prev_evac = 0
        for tt in range(NTT):
            ct = p.do("tensor", lambda e, tt=tt: e.transpose(
                ptb[0:64, 0:128], cwn[:, tt, :], identf[:, :]),
                wait=[("v", cw_done), ("s", prev_evac)], inc="mmt")
            prev_evac = p.do("scalar", lambda e, tt=tt: e.activation(
                cwTb[0:64, tt * 128:(tt + 1) * 128], ptb[0:64, 0:128],
                AF.Copy), wait=[("mmt", ct)], inc="s")
        cwT_done = prev_evac
        for tt in range(NTT):
            for dc in range(2):
                ct = p.do("tensor", lambda e, tt=tt, dc=dc: e.matmul(
                    pO[dc][0:128, 0:512],
                    cwTb[0:64, tt * 128:(tt + 1) * 128],
                    b2sb[0:64, dc * 512:(dc + 1) * 512],
                    start=True, stop=False),
                    wait=[("s", cwT_done), ("wld", wld0),
                          ("vacc", p.val("vacc")), ("v", cones)], inc=None)
                p.do("tensor", lambda e, dc=dc: e.matmul(
                    pO[dc][0:128, 0:512], onesb[0:1, :],
                    sb2a[0:1, dc * 512:(dc + 1) * 512],
                    start=False, stop=False), inc=None)
                ct = p.do("tensor", lambda e, dc=dc: e.matmul(
                    pO[dc][0:128, 0:512], onesb[0:1, :],
                    sb2b[0:1, dc * 512:(dc + 1) * 512],
                    start=False, stop=True), inc="mmt")
                p.do("vector", lambda e, tt=tt, dc=dc, ct=ct: e.tensor_copy(
                    acc[:, tt, dc * 512:(dc + 1) * 512], pO[dc][0:128, 0:512]),
                    wait=[("mmt", ct)], inc="vacc")
        acc_init = p.val("vacc")

        ph_evac = {}
        ph_iter = 0
        u_free_at = 0
        for grp in range(16):
            ldw = []
            for i in range(4):
                e_g = grp * 4 + i
                c1 = p.do("gpsimd", lambda en, e_g=e_g, i=i: en.dma_start(
                    out=w1g[:, i, :, :],
                    in_=w1_in[e_g].rearrange("(kt kp) f -> kp kt f", kp=128)),
                    wait=[("mmo", u_free_at), ("vacc", acc_init)],
                    inc="wg1", amt=16)
                c2 = p.do("gpsimd", lambda en, e_g=e_g, i=i: en.dma_start(
                    out=w2g[:, i, :, :],
                    in_=w2_in[e_g].rearrange("(ht hp) d -> hp ht d", hp=128)),
                    inc="wg2", amt=16)
                ldw.append((c1, c2))
            cwb_done = 0
            for i in range(4):
                e_g = grp * 4 + i
                ca = p.do("gpsimd", lambda en, e_g=e_g: en.affine_select(
                    sel64[:, :], ones64[:, :], [[0, 128]], ALU.is_equal, 0.0,
                    base=-e_g, channel_multiplier=1),
                    wait=[("v", cones), ("mmt", p.val("mmt"))], inc="g")
                for ch in range(4):
                    ct = p.do("tensor", lambda e, i=i, ch=ch: e.matmul(
                        pta[0:128, 0:512], sel64[:, :],
                        cwTb[0:64, ch * 512:(ch + 1) * 512],
                        start=True, stop=True),
                        wait=[("g", ca), ("s", max(cwT_done, p.val("s"))),
                              ("sh", p.val("sh"))], inc="mmt")
                    cwb_done = p.do("scalar", lambda e, i=i, ch=ch, ct=ct:
                                    e.activation(
                                        cwb[:, i, ch * 512:(ch + 1) * 512],
                                        pta[0:128, 0:512], AF.Copy),
                                    wait=[("mmt", ct)], inc="sh")
            for i in range(4):
                for blk in range(4):
                    for ht in range(2):
                        bank = ph[ph_iter % 2]
                        wv = [("vh", ph_evac.get(ph_iter - 2, 0))]
                        for kt in range(8):
                            cmh = p.do("tensor", lambda en, i=i, blk=blk, ht=ht,
                                       kt=kt, bank=bank: en.matmul(
                                           bank[0:128, 0:512],
                                           w1g[:, i, kt, ht * 128:(ht + 1) * 128],
                                           xnT[:, kt, blk * 512:(blk + 1) * 512],
                                           start=(kt == 0), stop=(kt == 7)),
                                       wait=([("wg1", ldw[i][0]),
                                              ("xnT", xnT_all)] + wv
                                             if kt == 0 else []),
                                       inc=("mmh" if kt == 7 else None))
                        cse = p.do("scalar", lambda en, i=i, blk=blk, ht=ht,
                                   grp=grp, bank=bank, cmh=cmh: en.activation(
                                       hseg[:, i, ht, blk * 512:(blk + 1) * 512],
                                       bank[0:128, 0:512], AF.Silu,
                                       bias=b1rT[:, ht, grp * 4 + i:grp * 4 + i + 1]),
                                   wait=[("mmh", cmh), ("v", prep_v)], inc="sh")
                        cvh = p.do("vector", lambda en, i=i, blk=blk, ht=ht,
                                   cse=cse: en.tensor_tensor(
                                       hseg[:, i, ht, blk * 512:(blk + 1) * 512],
                                       hseg[:, i, ht, blk * 512:(blk + 1) * 512],
                                       cwb[:, i, blk * 512:(blk + 1) * 512],
                                       ALU.mult),
                                   wait=[("sh", max(cse, cwb_done))], inc="vh")
                        ph_evac[ph_iter] = cvh
                        ph_iter += 1
            w1_done = p.val("vh")
            for tb in range(8):
                for i in range(4):
                    for ht in range(2):
                        for sub in range(2):
                            for dc in range(2):
                                bi = sub * 2 + dc
                                first = (i == 0 and ht == 0)
                                last = (i == 3 and ht == 1)
                                cmo = p.do("tensor", lambda en, i=i, ht=ht,
                                           tb=tb, sub=sub, dc=dc, bi=bi:
                                           en.matmul(
                                               pO[bi][0:128, 0:512],
                                               hseg[:, i, ht,
                                                    tb * 256 + sub * 128:
                                                    tb * 256 + sub * 128 + 128],
                                               w2g[:, i, ht, dc * 512:(dc + 1) * 512],
                                               start=first, stop=last),
                                           wait=([("vh", w1_done),
                                                  ("wg2", ldw[3][1]),
                                                  ("vacc", p.val("vacc"))]
                                                 if (first and sub == 0 and dc == 0)
                                                 else []),
                                           inc=("mmo" if last else None))
                for sub in range(2):
                    for dc in range(2):
                        bi = sub * 2 + dc
                        tt_i = tb * 2 + sub
                        p.do("vector", lambda en, bi=bi, tt_i=tt_i, dc=dc,
                             cmo=cmo: en.tensor_tensor(
                                 acc[:, tt_i, dc * 512:(dc + 1) * 512],
                                 acc[:, tt_i, dc * 512:(dc + 1) * 512],
                                 pO[bi][0:128, 0:512], ALU.add),
                             wait=[("mmo", cmo), ("vacc", p.val("vacc"))],
                             inc="vacc")
            u_free_at = p.val("mmo")

        routed_done_v = p.val("vacc")
        routed_done_mmo = p.val("mmo")

        ph_evac2 = {}
        for se in range(2):
            for g in range(2):
                cw2 = p.do("gpsimd", lambda en, se=se, g=g: en.dma_start(
                    out=s_w2g[:, :, :],
                    in_=sw2_in[se, g * 1024:(g + 1) * 1024, :]
                    .rearrange("(st hp) d -> hp st d", hp=128)),
                    wait=[("mmo", max(routed_done_mmo, p.val("mmo")))],
                    inc="wg2", amt=16)
                for s_l in range(8):
                    s_g = g * 8 + s_l
                    cw1 = p.do("gpsimd", lambda en, se=se, s_g=s_g, s_l=s_l:
                               en.dma_start(
                                   out=s_w1s[:, s_l % 2, :, :],
                                   in_=sw1_in[se, :, s_g * 128:(s_g + 1) * 128]
                                   .rearrange("(kt kp) f -> kp kt f", kp=128)),
                               wait=[("mmh", max(0, p.val("mmh") - 4)),
                                     ("mmo", routed_done_mmo)],
                               inc="wg1", amt=16)
                    for blk in range(4):
                        bank = ph[ph_iter % 2]
                        wv = [("sh", ph_evac2.get(ph_iter - 2, 0)),
                              ("vh", ph_evac.get(ph_iter - 2, 0) if (ph_iter - 2) in ph_evac else 0)]
                        for kt in range(8):
                            cmh = p.do("tensor", lambda en, s_l=s_l, blk=blk,
                                       kt=kt, bank=bank: en.matmul(
                                           bank[0:128, 0:512],
                                           s_w1s[:, s_l % 2, kt, :],
                                           xnT[:, kt, blk * 512:(blk + 1) * 512],
                                           start=(kt == 0), stop=(kt == 7)),
                                       wait=([("wg1", cw1), ("vacc", routed_done_v)]
                                             + wv if kt == 0 else []),
                                       inc=("mmh" if kt == 7 else None))
                        cse = p.do("scalar", lambda en, se=se, s_g=s_g, s_l=s_l,
                                   blk=blk, bank=bank, cmh=cmh: en.activation(
                                       s_hts[:, s_l, blk * 512:(blk + 1) * 512],
                                       bank[0:128, 0:512], AF.Silu,
                                       bias=sb1T[:, s_g, se:se + 1]),
                                   wait=[("mmh", cmh), ("v", prep_v)], inc="sh")
                        ph_evac2[ph_iter] = cse
                        ph_iter += 1
                f1_done = p.val("sh")
                for tb in range(8):
                    for s_l in range(8):
                        for sub in range(2):
                            for dc in range(2):
                                bi = sub * 2 + dc
                                cmo = p.do("tensor", lambda en, s_l=s_l, tb=tb,
                                           sub=sub, dc=dc, bi=bi: en.matmul(
                                               pO[bi][0:128, 0:512],
                                               s_hts[:, s_l,
                                                     tb * 256 + sub * 128:
                                                     tb * 256 + sub * 128 + 128],
                                               s_w2g[:, s_l, dc * 512:(dc + 1) * 512],
                                               start=(s_l == 0), stop=(s_l == 7)),
                                           wait=([("sh", f1_done),
                                                  ("wg2", cw2),
                                                  ("vacc", p.val("vacc"))]
                                                 if (s_l == 0 and sub == 0
                                                     and dc == 0) else []),
                                           inc=("mmo" if s_l == 7 else None))
                    for sub in range(2):
                        for dc in range(2):
                            bi = sub * 2 + dc
                            tt_i = tb * 2 + sub
                            p.do("vector", lambda en, bi=bi, tt_i=tt_i, dc=dc,
                                 cmo=cmo: en.tensor_tensor(
                                     acc[:, tt_i, dc * 512:(dc + 1) * 512],
                                     acc[:, tt_i, dc * 512:(dc + 1) * 512],
                                     pO[bi][0:128, 0:512], ALU.add),
                                 wait=[("mmo", cmo), ("vacc", p.val("vacc"))],
                                 inc="vacc")

        acc_done = p.val("vacc")

        for tt in range(NTT):
            wv = [("vacc", acc_done)] if tt == 0 else []
            cl = p.do("sync", lambda e, tt=tt: e.dma_start(
                out=xt[:, :], in_=x_in[tt * 128:(tt + 1) * 128, :]),
                wait=wv + [("vG", p.val("vG"))], inc="ld2", amt=16)
            cv = p.do("vector", lambda e, tt=tt: e.tensor_add(
                scr[:, :], xt[:, :], acc[:, tt, :]),
                wait=[("ld2", cl), ("st", p.val("st")), ("vacc", acc_done)],
                inc="vG")
            p.do("sync", lambda e, tt=tt: e.dma_start(
                out=y_out[tt * 128:(tt + 1) * 128, :], in_=scr[:, :]),
                wait=[("vG", cv)], inc="st", amt=16)
        p.wait_only("gpsimd", [("st", p.val("st"))])
        p.finish()
    return nc


def run_ffn1(x, inputs):
    nc = _get("ffn1", build_ffn1)
    shards = [x[c * 2048:(c + 1) * 2048] for c in range(NCORES)]
    w = dict(
        rw=np.ascontiguousarray(inputs["L1_ffn_router_w"]),
        rb=np.ascontiguousarray(inputs["L1_ffn_router_b"]).reshape(1, 64),
        w1=np.ascontiguousarray(inputs["L1_ffn_w1"]),
        b1=np.ascontiguousarray(inputs["L1_ffn_b1"]),
        w2=np.ascontiguousarray(inputs["L1_ffn_w2"]),
        b2=np.ascontiguousarray(inputs["L1_ffn_b2"]),
        sw1=np.ascontiguousarray(inputs["L1_ffn_sw1"]),
        sb1=np.ascontiguousarray(inputs["L1_ffn_sb1"]),
        sw2=np.ascontiguousarray(inputs["L1_ffn_sw2"]),
        sb2=np.ascontiguousarray(inputs["L1_ffn_sb2"]),
    )
    in_maps = [dict(x=np.ascontiguousarray(s), **w) for s in shards]
    r = run_bass_kernel_spmd(nc, in_maps, core_ids=list(range(NCORES)))
    return np.concatenate([r.results[c]["y"] for c in range(NCORES)], axis=0)



def build_mamba1():
    nc = bass.Bass("TRN2", target_bir_lowering=False)
    T = 4096
    NCH = 8
    CT = 512
    NSTATE = 16

    x_in = nc.dram_tensor("x", [T, D], F32, kind="ExternalInput").ap()
    al_in = nc.dram_tensor("al", [D, NSTATE], F32, kind="ExternalInput").ap()
    dp_in = nc.dram_tensor("dp", [1, D], F32, kind="ExternalInput").ap()
    wxp_in = nc.dram_tensor("wxp", [D, 1056], F32, kind="ExternalInput").ap()
    bxp_in = nc.dram_tensor("bxp", [1, 1056], F32, kind="ExternalInput").ap()
    wo_in = nc.dram_tensor("wo", [D, D], F32, kind="ExternalInput").ap()
    bo_in = nc.dram_tensor("bo", [1, D], F32, kind="ExternalInput").ap()
    y_out = nc.dram_tensor("y", [T, D], F32, kind="ExternalOutput").ap()
    xn_hbm = nc.dram_tensor("xn_hbm", [T, D], BF16).ap()

    p = Prog(nc)
    from contextlib import ExitStack
    with ExitStack() as ctx:
        sb = lambda n, sh, dt: ctx.enter_context(nc.sbuf_tensor(n, sh, dt))
        ps = lambda n, sh: ctx.enter_context(nc.psum_tensor(n, sh, F32))
        xt = sb("xt", [128, 1024], F32)
        xn = sb("xn", [128, 1024], BF16)
        scr = sb("scr", [128, 1024], F32)
        scr2 = sb("scr2", [128, 1024], F32)
        stats = sb("stats", [128, 8], F32)
        xnT = sb("xnT", [128, 8, CT], BF16)
        delta = sb("delta", [128, 8, CT], F32)
        ubuf = sb("ubuf", [128, 8, CT], F32)
        ybuf = sb("ybuf", [128, 8, CT], F32)
        yb = sb("yb", [128, 8, CT], BF16)
        hseq = sb("hseq", [128, CT], F32)
        ab = sb("ab", [128, 2, CT], F32)
        bx = sb("bx", [128, 2, CT], F32)
        bcB = sb("bcB", [128, CT], F32)
        bcC = sb("bcC", [128, CT], F32)
        BC = sb("BC", [32, CT], F32)
        state = sb("state", [128, 8, NSTATE], F32)
        wxpsb = sb("wxpsb", [128, 8, 1056], BF16)
        wosb = sb("wosb", [128, 8, 1024], BF16)
        bosb = sb("bosb", [1, 1024], BF16)
        ALsb = sb("ALsb", [128, 8, NSTATE], F32)
        Asb = sb("Asb", [128, 8, NSTATE], F32)
        bxpT = sb("bxpT", [128, 9], F32)
        bxp_r = sb("bxp_r", [1, 1056], F32)
        dp_r = sb("dp_r", [1, 1024], F32)
        DpT = sb("DpT", [128, 8], F32)
        ones32 = sb("ones32", [32, 128], F32)
        sel32 = sb("sel32", [32, 128], F32)
        onesb = sb("onesb", [1, 128], BF16)
        identf = sb("identf", [128, 128], F32)
        ones128 = sb("ones128", [128, 128], F32)
        consts = sb("consts", [128, 4], F32)
        pa = ps("pa", [128, 512]); pb = ps("pb", [128, 512])
        pc = ps("pc", [128, 512]); pd = ps("pd", [128, 512])

        _ident_f32(p, identf, ones128)
        ident_done = p.val("g")
        ccols = _emit_consts(p, consts, [1e-5, -0.5])
        c32 = p.do("vector", lambda e: e.memset(ones32[:, :], 1.0), inc="v")
        p.do("vector", lambda e: e.memset(onesb[:, :], 1.0), inc="v")
        cconst = p.val("v")

        p.do("gpsimd", lambda e: e.dma_start(
            out=wxpsb[:, :, :], in_=wxp_in.rearrange("(kt kp) f -> kp kt f", kp=128)),
            inc="wld", amt=16)
        p.do("gpsimd", lambda e: e.dma_start(
            out=wosb[:, :, :], in_=wo_in.rearrange("(kt kp) f -> kp kt f", kp=128)),
            inc="wld", amt=16)
        p.do("gpsimd", lambda e: e.dma_start(out=bosb[:, :], in_=bo_in),
             inc="wld", amt=16)
        wld0 = p.val("wld")
        p.do("sync", lambda e: e.dma_start(
            out=ALsb[:, :, :], in_=al_in.rearrange("(dt dp2) n -> dp2 dt n", dp2=128)),
            inc="wldh", amt=16)
        p.do("sync", lambda e: e.dma_start(out=bxp_r[:, :], in_=bxp_in),
             inc="wldh", amt=16)
        cwh = p.do("sync", lambda e: e.dma_start(out=dp_r[:, :], in_=dp_in),
                   inc="wldh", amt=16)
        ca1 = p.do("scalar", lambda e: e.activation(
            Asb[:, :, :], ALsb[:, :, :], AF.Exp), wait=[("wldh", cwh)], inc="s")
        cAneg = p.do("vector", lambda e: e.tensor_scalar(
            Asb[:, :, :], Asb[:, :, :], -1.0, None, ALU.mult),
            wait=[("s", ca1)], inc="v")
        prev_cp = cconst
        for f in range(9):
            w = 128 if f < 8 else 32
            ctp = p.do("tensor", lambda e, f=f, w=w: e.transpose(
                pa[0:w, 0:1], bxp_r[0:1, f * 128:f * 128 + w], identf[0:1, 0:1]),
                wait=[("wldh", cwh), ("g", ident_done), ("v", prev_cp)],
                inc="mmt")
            prev_cp = p.do("vector", lambda e, f=f, w=w, ctp=ctp: e.tensor_copy(
                bxpT[0:w, f:f + 1], pa[0:w, 0:1]), wait=[("mmt", ctp)], inc="v")
        for f in range(8):
            ctp = p.do("tensor", lambda e, f=f: e.transpose(
                pb[0:128, 0:1], dp_r[0:1, f * 128:(f + 1) * 128],
                identf[0:1, 0:1]),
                wait=[("wldh", cwh), ("g", ident_done), ("v", prev_cp)],
                inc="mmt")
            prev_cp = p.do("vector", lambda e, f=f, ctp=ctp: e.tensor_copy(
                DpT[0:128, f:f + 1], pb[0:128, 0:1]), wait=[("mmt", ctp)], inc="v")
        prep_v = p.val("v")
        cst0 = p.do("vector", lambda e: e.memset(state[:, :, :], 0.0), inc="v")

        state_cnt = {}
        ymm_done = 0
        for ch in range(NCH):
            bv = p.val("v"); bs = p.val("s"); bg = p.val("vG")
            p.wait_only("sync", [("vG", bg), ("st", p.val("st"))])
            p.wait_only("vector", [("vG", bg), ("st", p.val("st"))])
            p.wait_only("scalar", [("vG", bg), ("st", p.val("st"))])
            xn_counts = []
            for ttl in range(CT // 128):
                tt = ch * (CT // 128) + ttl
                w = [("v", xn_counts[-1] if xn_counts else bv)]
                cl = p.do("sync", lambda e, tt=tt: e.dma_start(
                    out=xt[:, :], in_=x_in[tt * 128:(tt + 1) * 128, :]),
                    wait=w, inc="ld", amt=16)
                cxn = _emit_ln(p, xt, xn, stats, scr, tt, cl, ccols[0], ccols[1],
                               cconst,
                               prev_v=(xn_counts[-1] if xn_counts else bv),
                               prev_s=p.val("s"), prev_store=p.val("xnst"))
                p.do("sync", lambda e, tt=tt: e.dma_start(
                    out=xn_hbm[tt * 128:(tt + 1) * 128, :], in_=xn[:, :]),
                    wait=[("v", cxn)], inc="xnst", amt=16)
                xn_counts.append(cxn)
            xnst_ch = p.val("xnst")
            for dt in range(8):
                p.do("sync", lambda e, dt=dt, ch=ch: e.dma_start(
                    out=xnT[:, dt, :],
                    in_=xn_hbm[ch * CT:(ch + 1) * CT, dt * 128:(dt + 1) * 128],
                    transpose=True),
                    wait=[("xnst", xnst_ch), ("vG", bg),
                          ("mmh", p.val("mmh"))], inc="xnT", amt=16)
            xnT_ch = p.val("xnT")

            ph_pair = [pa, pb]
            for f in range(9):
                w = 128 if f < 8 else 32
                for tc in range(CT // 512):
                    bank = ph_pair[(f * 2 + tc) % 2]
                    for kt in range(8):
                        cmh = p.do("tensor", lambda e, f=f, tc=tc, kt=kt, w=w,
                                   bank=bank: e.matmul(
                                       bank[0:w, 0:512],
                                       wxpsb[:, kt, f * 128:f * 128 + w],
                                       xnT[:, kt, tc * 512:(tc + 1) * 512],
                                       start=(kt == 0), stop=(kt == 7)),
                                   wait=([("xnT", xnT_ch), ("wld", wld0),
                                          ("v", p.val("v")),
                                          ("s", p.val("s"))] if kt == 0 else []),
                                   inc=("mmh" if kt == 7 else None))
                    if f < 8:
                        cm1 = p.do("vector", lambda e, f=f, tc=tc, bank=bank,
                                   cmh=cmh: e.tensor_scalar(
                                       scr[:, tc * 512:(tc + 1) * 512],
                                       bank[0:128, 0:512], bxpT[:, f:f + 1], 0.0,
                                       ALU.add, ALU.max),
                                   wait=[("mmh", cmh), ("v", prep_v)], inc="v")
                        cs1 = p.do("scalar", lambda e, f=f, tc=tc, bank=bank,
                                   cmh=cmh: e.activation(
                                       scr2[:, tc * 512:(tc + 1) * 512],
                                       bank[0:128, 0:512], AF.Abs,
                                       bias=bxpT[:, f:f + 1]),
                                   wait=[("mmh", cmh), ("v", prep_v),
                                         ("s", p.val("s"))], inc="s")
                        cs2 = p.do("scalar", lambda e, tc=tc: e.activation(
                            scr2[:, tc * 512:(tc + 1) * 512],
                            scr2[:, tc * 512:(tc + 1) * 512], AF.Exp,
                            scale=-1.0), wait=[("s", cs1)], inc="s")
                        cs3 = p.do("scalar", lambda e, tc=tc: e.activation(
                            scr2[:, tc * 512:(tc + 1) * 512],
                            scr2[:, tc * 512:(tc + 1) * 512], AF.Ln,
                            bias=1.0), wait=[("s", cs2)], inc="s")
                        p.do("vector", lambda e, f=f, tc=tc: e.tensor_add(
                            delta[:, f, tc * 512:(tc + 1) * 512],
                            scr[:, tc * 512:(tc + 1) * 512],
                            scr2[:, tc * 512:(tc + 1) * 512]),
                            wait=[("s", cs3), ("v", cm1)], inc="v")
                    else:
                        p.do("vector", lambda e, tc=tc, bank=bank, cmh=cmh:
                             e.tensor_scalar(
                                 BC[0:32, tc * 512:(tc + 1) * 512],
                                 bank[0:32, 0:512], bxpT[0:32, 8:9], None,
                                 ALU.add),
                             wait=[("mmh", cmh), ("v", p.val("v")),
                                   ("mmt", p.val("mmt"))], inc="v")
            delta_done = p.val("v")
            bc_done = p.val("v")

            for dt in range(8):
                p.do("vector", lambda e, dt=dt: e.tensor_mul(
                    ubuf[:, dt, :], delta[:, dt, :], xnT[:, dt, :]),
                    wait=[("v", p.val("v"))], inc="v")
            u_done = p.val("v")
            for dt in range(8):
                p.do("vector", lambda e, dt=dt: e.memset(ybuf[:, dt, :], 0.0),
                     wait=[("v", p.val("v"))], inc="v")
            y0 = p.val("v")

            for n in range(NSTATE):
                ca = p.do("gpsimd", lambda e, n=n: e.affine_select(
                    sel32[:, :], ones32[:, :], [[0, 128]], ALU.is_equal, 0.0,
                    base=-n, channel_multiplier=1),
                    wait=[("v", c32), ("mmt", p.val("mmt"))], inc="g")
                for half in range(CT // 512):
                    ctb = p.do("tensor", lambda e, half=half: e.matmul(
                        pc[0:128, 0:512], sel32[:, :],
                        BC[0:32, half * 512:(half + 1) * 512],
                        start=True, stop=True),
                        wait=[("g", ca), ("v", max(bc_done, p.val("v"))),
                              ("vG", bg)], inc="mmt")
                    p.do("vector", lambda e, half=half, ctb=ctb: e.tensor_copy(
                        bcB[:, half * 512:(half + 1) * 512], pc[0:128, 0:512]),
                        wait=[("mmt", ctb), ("v", p.val("v"))], inc="v")
                ca2_ = p.do("gpsimd", lambda e, n=n: e.affine_select(
                    sel32[:, :], ones32[:, :], [[0, 128]], ALU.is_equal, 0.0,
                    base=-(16 + n), channel_multiplier=1),
                    wait=[("mmt", p.val("mmt"))], inc="g")
                for half in range(CT // 512):
                    ctc = p.do("tensor", lambda e, half=half: e.matmul(
                        pd[0:128, 0:512], sel32[:, :],
                        BC[0:32, half * 512:(half + 1) * 512],
                        start=True, stop=True),
                        wait=[("g", ca2_), ("v", p.val("v")), ("vG", bg)],
                        inc="mmt")
                    p.do("vector", lambda e, half=half, ctc=ctc: e.tensor_copy(
                        bcC[:, half * 512:(half + 1) * 512], pd[0:128, 0:512]),
                        wait=[("mmt", ctc), ("v", p.val("v"))], inc="v")
                bc_built = p.val("v")
                for dt in range(8):
                    pp = (n * 8 + dt) % 2
                    cab = p.do("scalar", lambda e, dt=dt, n=n, pp=pp: e.activation(
                        ab[:, pp, :], delta[:, dt, :], AF.Exp,
                        scale=Asb[:, dt, n:n + 1]),
                        wait=[("v", max(delta_done, cAneg,
                                        state_cnt.get(("ab", pp), 0))),
                              ("s", p.val("s"))], inc="s")
                    cbx = p.do("vector", lambda e, dt=dt, pp=pp: e.tensor_mul(
                        bx[:, pp, :], ubuf[:, dt, :], bcB[:, :]),
                        wait=[("v", p.val("v"))], inc="v")
                    csc = p.do("vector", lambda e, dt=dt, n=n, pp=pp:
                               e.tensor_tensor_scan(
                                   hseq[:, :], ab[:, pp, :], bx[:, pp, :],
                                   state[:, dt, n:n + 1], ALU.mult, ALU.add),
                               wait=[("s", cab), ("v", p.val("v"))], inc="v")
                    state_cnt[("ab", pp)] = csc
                    cst = p.do("vector", lambda e, dt=dt, n=n: e.tensor_copy(
                        state[:, dt, n:n + 1], hseq[:, CT - 1:CT]),
                        wait=[("v", p.val("v"))], inc="v")
                    state_cnt[(dt, n)] = cst
                    cm_ = p.do("vector", lambda e: e.tensor_mul(
                        hseq[:, :], hseq[:, :], bcC[:, :]),
                        wait=[("v", p.val("v"))], inc="v")
                    p.do("vector", lambda e, dt=dt: e.tensor_add(
                        ybuf[:, dt, :], ybuf[:, dt, :], hseq[:, :]),
                        wait=[("v", p.val("v"))], inc="v")
            scan_done = p.val("v")

            for dt in range(8):
                cyd = p.do("vector", lambda e, dt=dt: e.scalar_tensor_tensor(
                    ybuf[:, dt, :], xnT[:, dt, :], DpT[:, dt:dt + 1],
                    ybuf[:, dt, :], ALU.mult, ALU.add),
                    wait=[("v", p.val("v"))], inc="v")
                p.do("scalar", lambda e, dt=dt, cyd=cyd: e.activation(
                    yb[:, dt, :], ybuf[:, dt, :], AF.Copy),
                    wait=[("v", cyd), ("mmo", ymm_done), ("s", p.val("s"))],
                    inc="s")
            yb_done = p.val("s")

            for ttl in range(CT // 128):
                tt = ch * (CT // 128) + ttl
                for dc in range(2):
                    bank = pc if dc == 0 else pd
                    for dt in range(8):
                        p.do("tensor", lambda e, ttl=ttl, dc=dc, dt=dt,
                             bank=bank: e.matmul(
                                 bank[0:128, 0:512],
                                 yb[:, dt, ttl * 128:(ttl + 1) * 128],
                                 wosb[:, dt, dc * 512:(dc + 1) * 512],
                                 start=(dt == 0), stop=False),
                             wait=([("s", yb_done), ("wld", wld0),
                                    ("vG", p.val("vG")),
                                    ("v", p.val("v"))]
                                   if dt == 0 else []),
                             inc=None)
                    cmo = p.do("tensor", lambda e, dc=dc, bank=bank: e.matmul(
                        bank[0:128, 0:512], onesb[0:1, :],
                        bosb[0:1, dc * 512:(dc + 1) * 512],
                        start=False, stop=True), inc="mmo")
                cl2 = p.do("sync", lambda e, tt=tt: e.dma_start(
                    out=xt[:, :], in_=x_in[tt * 128:(tt + 1) * 128, :]),
                    wait=[("vG", p.val("vG")), ("v", xn_counts[-1])],
                    inc="ld2", amt=16)
                cv1 = p.do("vector", lambda e, cmo=cmo, cl2=cl2: e.tensor_add(
                    scr[:, 0:512], xt[:, 0:512], pc[0:128, 0:512]),
                    wait=[("mmo", cmo), ("ld2", cl2), ("st", p.val("st")),
                          ("v", p.val("v"))], inc="vG")
                cv2 = p.do("vector", lambda e: e.tensor_add(
                    scr[:, 512:1024], xt[:, 512:1024], pd[0:128, 0:512]),
                    wait=[("vG", cv1)], inc="vG")
                p.do("sync", lambda e, tt=tt: e.dma_start(
                    out=y_out[tt * 128:(tt + 1) * 128, :], in_=scr[:, :]),
                    wait=[("vG", cv2)], inc="st", amt=16)
            ymm_done = p.val("mmo")

        p.wait_only("gpsimd", [("st", p.val("st"))])
        p.finish()
    return nc


def run_mamba1(x, inputs):
    nc = _get("mamba1", build_mamba1)
    w = dict(
        al=np.ascontiguousarray(inputs["L0_attn_a_A_log"]),
        dp=np.ascontiguousarray(inputs["L0_attn_a_Dp"]).reshape(1, D),
        wxp=np.ascontiguousarray(inputs["L0_attn_a_W_xproj"]),
        bxp=np.ascontiguousarray(inputs["L0_attn_a_b_xproj"]).reshape(1, 1056),
        wo=np.ascontiguousarray(inputs["L0_attn_a_W_out"]),
        bo=np.ascontiguousarray(inputs["L0_attn_a_b_out"]).reshape(1, D),
    )
    in_maps = [dict(x=np.ascontiguousarray(x[(c // 2) * 4096:(c // 2 + 1) * 4096]),
                    **w) for c in range(NCORES)]
    r = run_bass_kernel_spmd(nc, in_maps, core_ids=list(range(NCORES)))
    out = np.empty_like(x)
    for c in range(NCORES):
        b, h = c // 2, c % 2
        rows = slice(b * 4096 + h * 2048, b * 4096 + (h + 1) * 2048)
        out[rows] = r.results[c]["y"][h * 2048:(h + 1) * 2048]
    return out



def build_mamba2():
    nc = bass.Bass("TRN2", target_bir_lowering=False)
    T = 4096
    NCH = 8
    CT = 512
    NS = 16
    NET = 16

    x_in = nc.dram_tensor("x", [T, D], F32, kind="ExternalInput").ap()
    win_in = nc.dram_tensor("win", [4096, D], F32, kind="ExternalInput").ap()
    al_in = nc.dram_tensor("al", [2048, NS], F32, kind="ExternalInput").ap()
    wb_in = nc.dram_tensor("wb", [NS, 2048], F32, kind="ExternalInput").ap()
    wc_in = nc.dram_tensor("wc", [NS, 2048], F32, kind="ExternalInput").ap()
    wdt_in = nc.dram_tensor("wdt", [2048, 2048], F32, kind="ExternalInput").ap()
    bdt_in = nc.dram_tensor("bdt", [1, 2048], F32, kind="ExternalInput").ap()
    dd_in = nc.dram_tensor("dd", [1, 2048], F32, kind="ExternalInput").ap()
    og_in = nc.dram_tensor("og", [1, 1], F32, kind="ExternalInput").ap()
    wo_in = nc.dram_tensor("wo", [D, 2048], F32, kind="ExternalInput").ap()
    y_out = nc.dram_tensor("y", [T, D], F32, kind="ExternalOutput").ap()
    xn_hbm = nc.dram_tensor("xn_hbm", [T, D], BF16).ap()
    win_bf = nc.dram_tensor("win_bf", [4096, D], BF16).ap()
    wdt_bf = nc.dram_tensor("wdt_bf", [2048, 2048], BF16).ap()
    wo_bf = nc.dram_tensor("wo_bf", [D, 2048], BF16).ap()
    wb_bf = nc.dram_tensor("wb_bf", [NS, 2048], BF16).ap()
    wc_bf = nc.dram_tensor("wc_bf", [NS, 2048], BF16).ap()
    winT_hbm = nc.dram_tensor("winT_hbm", [D, 4096], BF16).ap()
    wdtT_hbm = nc.dram_tensor("wdtT_hbm", [2048, 2048], BF16).ap()
    woT_hbm = nc.dram_tensor("woT_hbm", [2048, D], BF16).ap()

    p = Prog(nc)
    from contextlib import ExitStack
    with ExitStack() as ctx:
        sb = lambda n, sh, dt: ctx.enter_context(nc.sbuf_tensor(n, sh, dt))
        ps = lambda n, sh: ctx.enter_context(nc.psum_tensor(n, sh, F32))
        xt = sb("xt", [128, 1024], F32)
        xn = sb("xn", [128, 1024], BF16)
        scr = sb("scr", [128, 1024], F32)
        scr2 = sb("scr2", [128, 1024], F32)
        stats = sb("stats", [128, 8], F32)
        xnT = sb("xnT", [128, 8, CT], BF16)
        xssmT = sb("xssmT", [128, NET, CT], BF16)
        zraw = sb("zraw", [128, NET, CT], BF16)
        dtbuf = sb("dtbuf", [128, NET, CT], F32)
        bcBa = sb("bcBa", [128, NS, CT], BF16)
        bcCa = sb("bcCa", [128, NS, CT], BF16)
        ygb = sb("ygb", [128, NET, CT], BF16)
        uet = sb("uet", [128, CT], F32)
        yet = sb("yet", [128, CT], F32)
        zsil = sb("zsil", [128, CT], F32)
        hseq = sb("hseq", [128, CT], F32)
        ab = sb("ab", [128, CT], F32)
        bx = sb("bx", [128, CT], F32)
        Bm = sb("Bm", [16, CT], F32)
        Cm = sb("Cm", [16, CT], F32)
        state = sb("state", [128, NET, NS], F32)
        wstage = sb("wstage", [128, 4096], BF16)
        wgrp = sb("wgrp", [128, NET, 512], BF16)
        wingrp = sb("wingrp", [128, 8, 512], BF16)
        wbT = sb("wbT", [128, NET, NS], BF16)
        wcT = sb("wcT", [128, NET, NS], BF16)
        ALsb = sb("ALsb", [128, NET, NS], F32)
        Asb = sb("Asb", [128, NET, NS], F32)
        brow = sb("brow", [1, 2048], F32)
        bdtT = sb("bdtT", [128, NET], F32)
        DT = sb("DT", [128, NET], F32)
        sgcol = sb("sgcol", [128, 1], F32)
        ogsb = sb("ogsb", [1, 1], F32)
        ogsg = sb("ogsg", [1, 1], F32)
        ogsgb = sb("ogsgb", [1, 1], BF16)
        ones32 = sb("ones32", [32, 128], F32)
        sel32 = sb("sel32", [32, 128], F32)
        onesb = sb("onesb", [1, 128], BF16)
        identf = sb("identf", [128, 128], F32)
        ones128 = sb("ones128", [128, 128], F32)
        consts = sb("consts", [128, 4], F32)
        pa = ps("pa", [128, 512]); pb = ps("pb", [128, 512])
        pc = ps("pc", [128, 512]); pd = ps("pd", [128, 512])
        pe_ = ps("pe_", [128, 512]); pf_ = ps("pf_", [128, 512])

        _ident_f32(p, identf, ones128)
        ident_done = p.val("g")
        ccols = _emit_consts(p, consts, [1e-5, -0.5])
        c32 = p.do("vector", lambda e: e.memset(ones32[:, :], 1.0), inc="v")
        p.do("vector", lambda e: e.memset(onesb[:, :], 1.0), inc="v")
        cconst = p.val("v")

        p.do("gpsimd", lambda e: e.dma_start(out=win_bf, in_=win_in),
             inc="wld", amt=16)
        p.do("gpsimd", lambda e: e.dma_start(out=wdt_bf, in_=wdt_in),
             inc="wld", amt=16)
        p.do("gpsimd", lambda e: e.dma_start(out=wo_bf, in_=wo_in),
             inc="wld", amt=16)
        p.do("gpsimd", lambda e: e.dma_start(out=wb_bf, in_=wb_in),
             inc="wld", amt=16)
        cwld = p.do("gpsimd", lambda e: e.dma_start(out=wc_bf, in_=wc_in),
                    inc="wld", amt=16)
        for cg in range(8):
            ctr = p.do("sync", lambda e, cg=cg: e.dma_start(
                out=wstage[:, 0:4096], in_=win_bf[:, cg * 128:(cg + 1) * 128],
                transpose=True), wait=[("wld", cwld), ("st0", p.val("st0"))],
                inc="trl", amt=16)
            p.do("sync", lambda e, cg=cg: e.dma_start(
                out=winT_hbm[cg * 128:(cg + 1) * 128, :], in_=wstage[:, 0:4096]),
                wait=[("trl", ctr)], inc="st0", amt=16)
        for cg in range(16):
            ctr = p.do("sync", lambda e, cg=cg: e.dma_start(
                out=wstage[:, 0:2048], in_=wdt_bf[:, cg * 128:(cg + 1) * 128],
                transpose=True), wait=[("st0", p.val("st0"))], inc="trl", amt=16)
            p.do("sync", lambda e, cg=cg: e.dma_start(
                out=wdtT_hbm[cg * 128:(cg + 1) * 128, :], in_=wstage[:, 0:2048]),
                wait=[("trl", ctr)], inc="st0", amt=16)
        for cg in range(16):
            ctr = p.do("sync", lambda e, cg=cg: e.dma_start(
                out=wstage[:, 0:1024], in_=wo_bf[:, cg * 128:(cg + 1) * 128],
                transpose=True), wait=[("st0", p.val("st0"))], inc="trl", amt=16)
            p.do("sync", lambda e, cg=cg: e.dma_start(
                out=woT_hbm[cg * 128:(cg + 1) * 128, :], in_=wstage[:, 0:1024]),
                wait=[("trl", ctr)], inc="st0", amt=16)
        for ke in range(NET):
            p.do("sync", lambda e, ke=ke: e.dma_start(
                out=wbT[:, ke, :], in_=wb_bf[:, ke * 128:(ke + 1) * 128],
                transpose=True), inc="trl", amt=16)
            p.do("sync", lambda e, ke=ke: e.dma_start(
                out=wcT[:, ke, :], in_=wc_bf[:, ke * 128:(ke + 1) * 128],
                transpose=True), inc="trl", amt=16)
        wT_done = p.val("st0")
        wbc_done = p.val("trl")

        cal = p.do("sync", lambda e: e.dma_start(
            out=ALsb[:, :, :], in_=al_in.rearrange("(et ep) n -> ep et n", ep=128)),
            inc="wldh", amt=16)
        ca1 = p.do("scalar", lambda e: e.activation(
            Asb[:, :, :], ALsb[:, :, :], AF.Exp), wait=[("wldh", cal)], inc="s")
        cAneg = p.do("vector", lambda e: e.tensor_scalar(
            Asb[:, :, :], Asb[:, :, :], -1.0, None, ALU.mult),
            wait=[("s", ca1)], inc="v")
        cbd = p.do("sync", lambda e: e.dma_start(out=brow[:, :], in_=bdt_in),
                   inc="wldh", amt=16)
        prev_cp = cconst
        for f in range(NET):
            ctp = p.do("tensor", lambda e, f=f: e.transpose(
                pa[0:128, 0:1], brow[0:1, f * 128:(f + 1) * 128],
                identf[0:1, 0:1]),
                wait=[("wldh", cbd), ("g", ident_done), ("v", prev_cp)],
                inc="mmt")
            prev_cp = p.do("vector", lambda e, f=f, ctp=ctp: e.tensor_copy(
                bdtT[0:128, f:f + 1], pa[0:128, 0:1]), wait=[("mmt", ctp)],
                inc="v")
        cdd = p.do("sync", lambda e: e.dma_start(out=brow[:, :], in_=dd_in),
                   wait=[("mmt", p.val("mmt"))], inc="wldh", amt=16)
        for f in range(NET):
            ctp = p.do("tensor", lambda e, f=f: e.transpose(
                pa[0:128, 0:1], brow[0:1, f * 128:(f + 1) * 128],
                identf[0:1, 0:1]),
                wait=[("wldh", cdd), ("v", prev_cp)], inc="mmt")
            prev_cp = p.do("vector", lambda e, f=f, ctp=ctp: e.tensor_copy(
                DT[0:128, f:f + 1], pa[0:128, 0:1]), wait=[("mmt", ctp)], inc="v")
        cog = p.do("sync", lambda e: e.dma_start(out=ogsb[:, :], in_=og_in),
                   inc="wldh", amt=16)
        csg = p.do("scalar", lambda e: e.activation(
            ogsg[:, :], ogsb[:, :], AF.Sigmoid), wait=[("wldh", cog)], inc="s")
        csgb = p.do("scalar", lambda e: e.activation(
            ogsgb[:, :], ogsg[:, :], AF.Copy), wait=[("s", csg)], inc="s")
        ctp = p.do("tensor", lambda e: e.matmul(
            pa[0:128, 0:1], onesb[0:1, :], ogsgb[0:1, 0:1],
            start=True, stop=True),
            wait=[("s", csgb), ("v", prev_cp)], inc="mmt")
        prev_cp = p.do("vector", lambda e, ctp=ctp: e.tensor_copy(
            sgcol[:, 0:1], pa[0:128, 0:1]), wait=[("mmt", ctp)], inc="v")
        prep_v = p.val("v")
        cst0 = p.do("vector", lambda e: e.memset(state[:, :, :], 0.0), inc="v")

        state_cnt = {}
        ymm_done = 0
        for ch in range(NCH):
            bv = p.val("v"); bg = p.val("vG")
            p.wait_only("sync", [("vG", bg), ("st", p.val("st"))])
            p.wait_only("vector", [("vG", bg), ("st", p.val("st"))])
            p.wait_only("scalar", [("vG", bg), ("st", p.val("st"))])
            xn_counts = []
            for ttl in range(CT // 128):
                tt = ch * (CT // 128) + ttl
                w = [("v", xn_counts[-1] if xn_counts else bv)]
                cl = p.do("sync", lambda e, tt=tt: e.dma_start(
                    out=xt[:, :], in_=x_in[tt * 128:(tt + 1) * 128, :]),
                    wait=w, inc="ld", amt=16)
                cxn = _emit_ln(p, xt, xn, stats, scr, tt, cl, ccols[0], ccols[1],
                               cconst,
                               prev_v=(xn_counts[-1] if xn_counts else bv),
                               prev_s=p.val("s"), prev_store=p.val("xnst"))
                p.do("sync", lambda e, tt=tt: e.dma_start(
                    out=xn_hbm[tt * 128:(tt + 1) * 128, :], in_=xn[:, :]),
                    wait=[("v", cxn)], inc="xnst", amt=16)
                xn_counts.append(cxn)
            xnst_ch = p.val("xnst")
            for dt_ in range(8):
                p.do("sync", lambda e, dt_=dt_, ch=ch: e.dma_start(
                    out=xnT[:, dt_, :],
                    in_=xn_hbm[ch * CT:(ch + 1) * CT, dt_ * 128:(dt_ + 1) * 128],
                    transpose=True),
                    wait=[("xnst", xnst_ch), ("vG", bg),
                          ("mmh", p.val("mmh"))], inc="xnT", amt=16)
            xnT_ch = p.val("xnT")

            pquad = [pa, pb, pc, pd]
            for fg in range(8):
                cwin = p.do("gpsimd", lambda e, fg=fg: e.dma_start(
                    out=wingrp[:, :, :],
                    in_=winT_hbm[:, fg * 512:(fg + 1) * 512]
                    .rearrange("(kt kp) f -> kp kt f", kp=128)),
                    wait=[("st0", wT_done), ("mmh", p.val("mmh"))],
                    inc="wg1", amt=16)
                for fl in range(4):
                    f = fg * 4 + fl
                    bank = pquad[fl]
                    for kd in range(8):
                        cmh = p.do("tensor", lambda e, fl=fl, kd=kd, bank=bank:
                                   e.matmul(
                                       bank[0:128, 0:512],
                                       wingrp[:, kd, fl * 128:(fl + 1) * 128],
                                       xnT[:, kd, :],
                                       start=(kd == 0), stop=(kd == 7)),
                                   wait=([("wg1", cwin), ("xnT", xnT_ch),
                                          ("s", p.val("s")),
                                          ("mmb", p.val("mmb")),
                                          ("v", p.val("v"))] if kd == 0 else []),
                                   inc=("mmh" if kd == 7 else None))
                    if f < 16:
                        p.do("scalar", lambda e, f=f, bank=bank, cmh=cmh:
                             e.activation(xssmT[:, f, :], bank[0:128, 0:512],
                                          AF.Copy),
                             wait=[("mmh", cmh), ("s", p.val("s"))], inc="s")
                    else:
                        p.do("scalar", lambda e, f=f, bank=bank, cmh=cmh:
                             e.activation(zraw[:, f - 16, :], bank[0:128, 0:512],
                                          AF.Copy),
                             wait=[("mmh", cmh), ("s", p.val("s"))], inc="s")
            xz_done = p.val("s")

            for which in range(2):
                wt = wbT if which == 0 else wcT
                dstm = Bm if which == 0 else Cm
                for ke in range(NET):
                    cmb = p.do("tensor", lambda e, ke=ke, wt=wt:
                               e.matmul(
                                   pe_[0:16, 0:512], wt[:, ke, :],
                                   xssmT[:, ke, :],
                                   start=(ke == 0), stop=(ke == NET - 1)),
                               wait=([("s", xz_done), ("trl", wbc_done),
                                      ("v", p.val("v"))] if ke == 0 else []),
                               inc=("mmb" if ke == NET - 1 else None))
                p.do("vector", lambda e, dstm=dstm, cmb=cmb: e.tensor_copy(
                    dstm[0:16, :], pe_[0:16, 0:512]),
                    wait=[("mmb", cmb), ("v", p.val("v"))], inc="v")
            bc_done = p.val("v")

            for eg in range(4):
                cwdt = p.do("gpsimd", lambda e, eg=eg: e.dma_start(
                    out=wgrp[:, :, :],
                    in_=wdtT_hbm[:, eg * 512:(eg + 1) * 512]
                    .rearrange("(kt kp) f -> kp kt f", kp=128)),
                    wait=[("st0", wT_done), ("mmh", p.val("mmh")),
                          ("mmo", ymm_done)],
                    inc="wg2", amt=16)
                for el in range(4):
                    et = eg * 4 + el
                    bank = pquad[el]
                    for ke in range(NET):
                        cmh = p.do("tensor", lambda e, el=el, ke=ke, bank=bank:
                                   e.matmul(
                                       bank[0:128, 0:512],
                                       wgrp[:, ke, el * 128:(el + 1) * 128],
                                       xssmT[:, ke, :],
                                       start=(ke == 0), stop=(ke == NET - 1)),
                                   wait=([("wg2", cwdt), ("s", xz_done),
                                          ("v", p.val("v"))] if ke == 0 else []),
                                   inc=("mmh" if ke == NET - 1 else None))
                    cm1 = p.do("vector", lambda e, et=et, bank=bank, cmh=cmh:
                               e.tensor_scalar(
                                   scr[:, 0:512], bank[0:128, 0:512],
                                   bdtT[:, et:et + 1], 0.0, ALU.add, ALU.max),
                               wait=[("mmh", cmh), ("v", p.val("v"))], inc="v")
                    cs1 = p.do("scalar", lambda e, et=et, bank=bank, cmh=cmh:
                               e.activation(
                                   scr2[:, 0:512], bank[0:128, 0:512], AF.Abs,
                                   bias=bdtT[:, et:et + 1]),
                               wait=[("mmh", cmh), ("v", prep_v),
                                     ("s", p.val("s"))], inc="s")
                    cs2 = p.do("scalar", lambda e: e.activation(
                        scr2[:, 0:512], scr2[:, 0:512], AF.Exp, scale=-1.0),
                        wait=[("s", cs1)], inc="s")
                    cs3 = p.do("scalar", lambda e: e.activation(
                        scr2[:, 0:512], scr2[:, 0:512], AF.Ln, bias=1.0),
                        wait=[("s", cs2)], inc="s")
                    p.do("vector", lambda e, et=et: e.tensor_add(
                        dtbuf[:, et, :], scr[:, 0:512], scr2[:, 0:512]),
                        wait=[("s", cs3), ("v", cm1)], inc="v")
            dt_done = p.val("v")

            for n in range(NS):
                ca = p.do("gpsimd", lambda e, n=n: e.affine_select(
                    sel32[0:16, :], ones32[0:16, :], [[0, 128]], ALU.is_equal,
                    0.0, base=-n, channel_multiplier=1),
                    wait=[("v", c32), ("mmt", p.val("mmt"))], inc="g")
                ctb = p.do("tensor", lambda e, n=n: e.matmul(
                    pe_[0:128, 0:512], sel32[0:16, :], Bm[0:16, :],
                    start=True, stop=True),
                    wait=[("g", ca), ("v", max(bc_done, p.val("v"))),
                          ("s", p.val("s")), ("vG", bg)], inc="mmt")
                p.do("scalar", lambda e, n=n, ctb=ctb: e.activation(
                    bcBa[:, n, :], pe_[0:128, 0:512], AF.Copy),
                    wait=[("mmt", ctb), ("s", p.val("s"))], inc="s")
                ctc = p.do("tensor", lambda e, n=n: e.matmul(
                    pf_[0:128, 0:512], sel32[0:16, :], Cm[0:16, :],
                    start=True, stop=True),
                    wait=[("v", p.val("v")), ("s", p.val("s")),
                          ("vG", bg)], inc="mmt")
                p.do("scalar", lambda e, n=n, ctc=ctc: e.activation(
                    bcCa[:, n, :], pf_[0:128, 0:512], AF.Copy),
                    wait=[("mmt", ctc), ("s", p.val("s"))], inc="s")
            bc_built = p.val("s")

            for et in range(NET):
                cu = p.do("vector", lambda e, et=et: e.tensor_mul(
                    uet[:, :], dtbuf[:, et, :], xssmT[:, et, :]),
                    wait=[("v", max(dt_done, p.val("v"))), ("s", bc_built)],
                    inc="v")
                cy0 = p.do("vector", lambda e: e.memset(yet[:, :], 0.0),
                           wait=[("v", p.val("v"))], inc="v")
                for n in range(NS):
                    cab = p.do("scalar", lambda e, et=et, n=n: e.activation(
                        ab[:, :], dtbuf[:, et, :], AF.Exp,
                        scale=Asb[:, et, n:n + 1]),
                        wait=[("v", p.val("v")), ("s", p.val("s"))], inc="s")
                    cbx = p.do("vector", lambda e, et=et, n=n: e.tensor_mul(
                        bx[:, :], uet[:, :], bcBa[:, n, :]),
                        wait=[("v", p.val("v")), ("s", bc_built)], inc="v")
                    csc = p.do("vector", lambda e, et=et, n=n:
                               e.tensor_tensor_scan(
                                   hseq[:, :], ab[:, :], bx[:, :],
                                   state[:, et, n:n + 1], ALU.mult, ALU.add),
                               wait=[("s", cab), ("v", p.val("v"))], inc="v")
                    p.do("vector", lambda e, et=et, n=n: e.tensor_copy(
                        state[:, et, n:n + 1], hseq[:, CT - 1:CT]),
                        wait=[("v", p.val("v"))], inc="v")
                    p.do("vector", lambda e, n=n: e.tensor_mul(
                        hseq[:, :], hseq[:, :], bcCa[:, n, :]),
                        wait=[("v", p.val("v"))], inc="v")
                    p.do("vector", lambda e: e.tensor_add(
                        yet[:, :], yet[:, :], hseq[:, :]),
                        wait=[("v", p.val("v"))], inc="v")
                cyd = p.do("vector", lambda e, et=et: e.scalar_tensor_tensor(
                    yet[:, :], xssmT[:, et, :], DT[:, et:et + 1], yet[:, :],
                    ALU.mult, ALU.add), wait=[("v", p.val("v"))], inc="v")
                czs = p.do("scalar", lambda e, et=et: e.activation(
                    zsil[:, :], zraw[:, et, :], AF.Silu),
                    wait=[("s", p.val("s")), ("v", p.val("v"))], inc="s")
                cyz = p.do("vector", lambda e: e.tensor_mul(
                    yet[:, :], yet[:, :], zsil[:, :]),
                    wait=[("v", cyd), ("s", czs)], inc="v")
                p.do("vector", lambda e, et=et: e.tensor_scalar(
                    ygb[:, et, :], yet[:, :], sgcol[:, 0:1], None, ALU.mult),
                    wait=[("v", p.val("v")), ("mmo", ymm_done)], inc="v")
            scan_done = p.val("v")

            for dc in range(2):
                cwo = p.do("gpsimd", lambda e, dc=dc, ch=ch: e.dma_start(
                    out=wgrp[:, :, :],
                    in_=woT_hbm[:, dc * 512:(dc + 1) * 512]
                    .rearrange("(kt kp) f -> kp kt f", kp=128)),
                    wait=[("st0", wT_done), ("mmh", p.val("mmh")),
                          ("mmo", p.val("mmo"))],
                    inc="wg2", amt=16)
                for ttl in range(CT // 128):
                    tt = ch * (CT // 128) + ttl
                    bank = pe_ if dc == 0 else pf_
                    for ke in range(NET):
                        p.do("tensor", lambda e, ttl=ttl, dc=dc, ke=ke,
                             bank=bank: e.matmul(
                                 bank[0:128, 0:512],
                                 ygb[:, ke, ttl * 128:(ttl + 1) * 128],
                                 wgrp[:, ke, :],
                                 start=(ke == 0), stop=(ke == NET - 1)),
                             wait=([("v", scan_done), ("wg2", cwo),
                                    ("vG", p.val("vG"))]
                                   if ke == 0 else []),
                             inc=("mmo" if ke == NET - 1 else None))
                    cmo = p.val("mmo")
                    cl2 = p.do("sync", lambda e, tt=tt, dc=dc: e.dma_start(
                        out=xt[:, dc * 512:(dc + 1) * 512],
                        in_=x_in[tt * 128:(tt + 1) * 128,
                                 dc * 512:(dc + 1) * 512]),
                        wait=[("vG", p.val("vG")), ("v", xn_counts[-1])],
                        inc="ld2", amt=16)
                    cv1 = p.do("vector", lambda e, dc=dc, bank=bank, cmo=cmo,
                               cl2=cl2: e.tensor_add(
                                   scr[:, dc * 512:(dc + 1) * 512],
                                   xt[:, dc * 512:(dc + 1) * 512],
                                   bank[0:128, 0:512]),
                               wait=[("mmo", cmo), ("ld2", cl2),
                                     ("st", p.val("st")),
                                     ("v", p.val("v"))], inc="vG")
                    p.do("sync", lambda e, tt=tt, dc=dc: e.dma_start(
                        out=y_out[tt * 128:(tt + 1) * 128,
                                  dc * 512:(dc + 1) * 512],
                        in_=scr[:, dc * 512:(dc + 1) * 512]),
                        wait=[("vG", cv1)], inc="st", amt=16)
            ymm_done = p.val("mmo")

        p.wait_only("gpsimd", [("st", p.val("st"))])
        p.finish()
    return nc


def run_mamba2(x, inputs):
    nc = _get("mamba2", build_mamba2)
    w = dict(
        win=np.ascontiguousarray(inputs["L1_attn_a_W_in"]),
        al=np.ascontiguousarray(inputs["L1_attn_a_A_log"]),
        wb=np.ascontiguousarray(inputs["L1_attn_a_W_B"]),
        wc=np.ascontiguousarray(inputs["L1_attn_a_W_C"]),
        wdt=np.ascontiguousarray(inputs["L1_attn_a_W_dt"]),
        bdt=np.ascontiguousarray(inputs["L1_attn_a_b_dt"]).reshape(1, 2048),
        dd=np.ascontiguousarray(inputs["L1_attn_a_D"]).reshape(1, 2048),
        og=np.ascontiguousarray(inputs["L1_attn_a_output_gate"]).reshape(1, 1),
        wo=np.ascontiguousarray(inputs["L1_attn_a_W_out"]),
    )
    in_maps = [dict(x=np.ascontiguousarray(x[(c // 2) * 4096:(c // 2 + 1) * 4096]),
                    **w) for c in range(NCORES)]
    r = run_bass_kernel_spmd(nc, in_maps, core_ids=list(range(NCORES)))
    out = np.empty_like(x)
    for c in range(NCORES):
        b, h = c // 2, c % 2
        rows = slice(b * 4096 + h * 2048, b * 4096 + (h + 1) * 2048)
        out[rows] = r.results[c]["y"][h * 2048:(h + 1) * 2048]
    return out



def build_attnb1():
    nc = bass.Bass("TRN2", target_bir_lowering=False)
    TB, TQ, NH = 4096, 2048, 16
    NRT = TQ // 128

    x_in = nc.dram_tensor("x", [TB, D], F32, kind="ExternalInput").ap()
    xq_in = nc.dram_tensor("xq", [TQ, D], F32, kind="ExternalInput").ap()
    jl_in = nc.dram_tensor("jl", [128, TQ // 128], F32, kind="ExternalInput").ap()
    wq_in = nc.dram_tensor("wq", [D, D], F32, kind="ExternalInput").ap()
    wk_in = nc.dram_tensor("wk", [D, D], F32, kind="ExternalInput").ap()
    wv_in = nc.dram_tensor("wv", [D, D], F32, kind="ExternalInput").ap()
    wo_in = nc.dram_tensor("wo", [D, D], F32, kind="ExternalInput").ap()
    bo_in = nc.dram_tensor("bo", [1, D], F32, kind="ExternalInput").ap()
    y_out = nc.dram_tensor("y", [TQ, D], F32, kind="ExternalOutput").ap()
    xn_hbm = nc.dram_tensor("xn_hbm", [TB, D], BF16).ap()
    xqn_hbm = nc.dram_tensor("xqn_hbm", [TQ, D], BF16).ap()
    v_hbm = nc.dram_tensor("v_hbm", [TB, D], BF16).ap()
    av_hbm = nc.dram_tensor("av_hbm", [D, TQ], BF16).ap()
    wbfs = {}
    wTs = {}
    for nm in ("wq", "wk", "wv", "wo"):
        wbfs[nm] = nc.dram_tensor(nm + "_bf", [D, D], BF16).ap()
        wTs[nm] = nc.dram_tensor(nm + "T_hbm", [D, D], BF16).ap()

    POLY = [-0.037, 0.481, -2.3, 4.917, -1.791]

    p = Prog(nc)
    from contextlib import ExitStack
    with ExitStack() as ctx:
        sb = lambda n, sh, dt: ctx.enter_context(nc.sbuf_tensor(n, sh, dt))
        psf = lambda n, sh, dt=F32: ctx.enter_context(nc.psum_tensor(n, sh, dt))
        xt = sb("xt", [128, 1024], F32)
        xn = sb("xn", [128, 1024], BF16)
        scr = sb("scr", [128, 1024], F32)
        stats = sb("stats", [128, 8], F32)
        kT = sb("kT", [128, 8, TB], BF16)
        qT = sb("qT", [128, 8, TQ], BF16)
        ubig = sb("ubig", [128, 32768], BF16)
        vhb = sb("vhb", [128, 32, 64], BF16)
        avT = sb("avT", [128, 8, 128], BF16)
        avsb = sb("avsb", [64, 128], BF16)
        pTsb1 = sb("pTsb1", [128, 128], BF16)
        pTsb2 = sb("pTsb2", [128, 128], BF16)
        vstage = sb("vstage", [128, 512], BF16)
        mkbuf = sb("mkbuf", [128, 512], I32)
        jidx = sb("jidx", [128, 512], F32)
        jlsb = sb("jlsb", [128, NRT], F32)
        cols = sb("cols", [128, 16], F32)
        bosb = sb("bosb", [1, 1024], BF16)
        onesb = sb("onesb", [1, 128], BF16)
        identb = sb("identb", [128, 128], BF16)
        identf = sb("identf", [128, 128], F32)
        ones128 = sb("ones128", [128, 128], F32)
        consts = sb("consts", [128, 4], F32)
        wstage = sb("wstage", [128, 1024], BF16)
        pa = psf("pa", [128, 512]); pb = psf("pb", [128, 512])
        pt1 = psf("pt1", [128, 128], BF16); pt2 = psf("pt2", [128, 128], BF16)
        pav = psf("pav", [128, 128])
        pw1 = psf("pw1", [128, 512]); pw2 = psf("pw2", [128, 512])

        dots = ubig[:, 0:8192].bitcast(F32)
        ebuf = ubig[:, 8192:16384].bitcast(F32)
        lnpb = ubig[:, 16384:24576].bitcast(F32)
        p2b = ubig[:, 16384:24576]
        woTr = ubig[:, 24576:32768].rearrange("p (kt f) -> p kt f", kt=8)
        wkTr = ubig[:, 0:8192].rearrange("p (kt f) -> p kt f", kt=8)
        wvTr = ubig[:, 8192:16384].rearrange("p (kt f) -> p kt f", kt=8)
        wqTr = ubig[:, 16384:24576].rearrange("p (kt f) -> p kt f", kt=8)
        xnTs = ubig[:, 24576:28672].rearrange("p (kt f) -> p kt f", kt=8)

        _ident_f32(p, identf, ones128)
        ident_done = p.val("g")
        ccols = _emit_consts(p, consts, [1e-5, -0.5, 1e-9, -1e30])
        cid = p.do("vector", lambda e: e.tensor_copy(identb[:, :], identf[:, :]),
                   wait=[("g", ident_done)], inc="v")
        p.do("vector", lambda e: e.memset(onesb[:, :], 1.0), inc="v")
        cjx = p.do("gpsimd", lambda e: e.iota(
            jidx[:, :], [[1, 512]], base=0, channel_multiplier=0,
            allow_small_or_imprecise_dtypes=True), inc="g")
        cconst = p.val("v")

        for nm, src in (("wq", wq_in), ("wk", wk_in), ("wv", wv_in),
                        ("wo", wo_in)):
            p.do("gpsimd", lambda e, src=src, dst=wbfs[nm]: e.dma_start(
                out=dst, in_=src), inc="wld", amt=16)
        cwc = p.val("wld")
        p.do("gpsimd", lambda e: e.dma_start(out=bosb[:, :], in_=bo_in),
             inc="wld", amt=16)
        wld0 = p.val("wld")
        for nm in ("wq", "wk", "wv", "wo"):
            for cg in range(8):
                ctr = p.do("sync", lambda e, nm=nm, cg=cg: e.dma_start(
                    out=wstage[:, :], in_=wbfs[nm][:, cg * 128:(cg + 1) * 128],
                    transpose=True),
                    wait=[("wld", cwc), ("st0", p.val("st0"))],
                    inc="trl", amt=16)
                p.do("sync", lambda e, nm=nm, cg=cg: e.dma_start(
                    out=wTs[nm][cg * 128:(cg + 1) * 128, :], in_=wstage[:, :]),
                    wait=[("trl", ctr)], inc="st0", amt=16)
        wT_done = p.val("st0")
        cjl = p.do("sync", lambda e: e.dma_start(
            out=jlsb[:, :], in_=jl_in), inc="wldh", amt=16)

        prev = cconst
        for i, (src, dst, n_t) in enumerate(
                [(x_in, xn_hbm, TB // 128), (xq_in, xqn_hbm, TQ // 128)]):
            for tt in range(n_t):
                cl = p.do("sync", lambda e, src=src, tt=tt: e.dma_start(
                    out=xt[:, :], in_=src[tt * 128:(tt + 1) * 128, :]),
                    wait=[("v", prev)], inc="ld", amt=16)
                cxn = _emit_ln(p, xt, xn, stats, scr, tt, cl, ccols[0],
                               ccols[1], cconst, prev_v=prev,
                               prev_s=p.val("s"), prev_store=p.val("xnst"))
                p.do("sync", lambda e, dst=dst, tt=tt: e.dma_start(
                    out=dst[tt * 128:(tt + 1) * 128, :], in_=xn[:, :]),
                    wait=[("v", cxn)], inc="xnst", amt=16)
                prev = cxn
        xnst_all = p.val("xnst")

        ckl = p.do("gpsimd", lambda e: e.dma_start(
            out=wkTr[:, :, :],
            in_=wTs["wk"].rearrange("(kt kp) f -> kp kt f", kp=128)),
            wait=[("st0", wT_done)], inc="wg1", amt=16)
        cvl = p.do("gpsimd", lambda e: e.dma_start(
            out=wvTr[:, :, :],
            in_=wTs["wv"].rearrange("(kt kp) f -> kp kt f", kp=128)),
            inc="wg1", amt=16)
        cql = p.do("gpsimd", lambda e: e.dma_start(
            out=wqTr[:, :, :],
            in_=wTs["wq"].rearrange("(kt kp) f -> kp kt f", kp=128)),
            inc="wg1", amt=16)
        for tc in range(TB // 512):
            for kd in range(8):
                p.do("sync", lambda e, tc=tc, kd=kd: e.dma_start(
                    out=xnTs[:, kd, :],
                    in_=xn_hbm[tc * 512:(tc + 1) * 512,
                               kd * 128:(kd + 1) * 128],
                    transpose=True),
                    wait=[("xnst", xnst_all), ("mmh", p.val("mmh"))],
                    inc="xnT", amt=16)
            xnT_c = p.val("xnT")
            for ft in range(8):
                for kd in range(8):
                    cmh = p.do("tensor", lambda e, ft=ft, kd=kd: e.matmul(
                        pa[0:128, 0:512], wkTr[:, kd, ft * 128:(ft + 1) * 128],
                        xnTs[:, kd, :], start=(kd == 0), stop=(kd == 7)),
                        wait=([("wg1", cql), ("xnT", xnT_c),
                               ("s", p.val("s"))] if kd == 0 else []),
                        inc=("mmh" if kd == 7 else None))
                p.do("scalar", lambda e, ft=ft, tc=tc, cmh=cmh: e.activation(
                    kT[:, ft, tc * 512:(tc + 1) * 512], pa[0:128, 0:512],
                    AF.Copy), wait=[("mmh", cmh), ("s", p.val("s"))], inc="s")
            for fc in range(2):
                for sub in range(4):
                    for kd in range(8):
                        cmh = p.do("tensor", lambda e, fc=fc, sub=sub, kd=kd:
                                   e.matmul(
                                       pb[0:128, 0:512],
                                       xnTs[:, kd, sub * 128:(sub + 1) * 128],
                                       wvTr[:, kd, fc * 512:(fc + 1) * 512],
                                       start=(kd == 0), stop=(kd == 7)),
                                   wait=([("s", p.val("s"))] if kd == 0 else []),
                                   inc=("mmh" if kd == 7 else None))
                    cev = p.do("scalar", lambda e, cmh=cmh: e.activation(
                        vstage[:, :], pb[0:128, 0:512], AF.Copy),
                        wait=[("mmh", cmh), ("s", p.val("s")),
                              ("st1", p.val("st1"))], inc="s")
                    p.do("sync", lambda e, tc=tc, sub=sub, fc=fc, cev=cev:
                         e.dma_start(
                             out=v_hbm[tc * 512 + sub * 128:
                                       tc * 512 + (sub + 1) * 128,
                                       fc * 512:(fc + 1) * 512],
                             in_=vstage[:, :]),
                         wait=[("s", cev)], inc="st1", amt=16)
        for tc in range(TQ // 512):
            for kd in range(8):
                p.do("sync", lambda e, tc=tc, kd=kd: e.dma_start(
                    out=xnTs[:, kd, :],
                    in_=xqn_hbm[tc * 512:(tc + 1) * 512,
                                kd * 128:(kd + 1) * 128],
                    transpose=True),
                    wait=[("xnst", xnst_all), ("mmh", p.val("mmh"))],
                    inc="xnT", amt=16)
            xnT_c = p.val("xnT")
            for ft in range(8):
                for kd in range(8):
                    cmh = p.do("tensor", lambda e, ft=ft, kd=kd: e.matmul(
                        pa[0:128, 0:512], wqTr[:, kd, ft * 128:(ft + 1) * 128],
                        xnTs[:, kd, :], start=(kd == 0), stop=(kd == 7)),
                        wait=([("xnT", xnT_c), ("s", p.val("s"))]
                              if kd == 0 else []),
                        inc=("mmh" if kd == 7 else None))
                p.do("scalar", lambda e, ft=ft, tc=tc, cmh=cmh: e.activation(
                    qT[:, ft, tc * 512:(tc + 1) * 512], pa[0:128, 0:512],
                    AF.Copy), wait=[("mmh", cmh), ("s", p.val("s"))], inc="s")
        proj_done = p.val("s")
        proj_mm_done = p.val("mmh")

        cwoT = p.do("gpsimd", lambda e: e.dma_start(
            out=woTr[:, :, :],
            in_=wTs["wo"].rearrange("(kt kp) f -> kp kt f", kp=128)),
            wait=[("st0", wT_done), ("mmh", proj_mm_done), ("s", proj_done)],
            inc="wg1", amt=16)

        dpp = [pa, pb]
        ptp = [pt1, pt2]
        pTs = [pTsb1, pTsb2]
        for h in range(NH):
            cvh = p.do("gpsimd", lambda e, h=h: e.dma_start(
                out=vhb[:, :, :],
                in_=v_hbm[:, h * 64:(h + 1) * 64]
                .rearrange("(jt jp) d2 -> jp jt d2", jp=128)),
                wait=[("st1", p.val("st1")), ("mmA", p.val("mmA"))],
                inc="wg2", amt=16)
            for rt in range(NRT):
                JC = (rt + 2) // 2
                jstat = (rt + 1) * 256
                dots_ev = {}
                for jc in range(JC):
                    bank = dpp[jc % 2]
                    cmd = p.do("tensor", lambda e, h=h, rt=rt, jc=jc, bank=bank:
                               e.matmul(
                                   bank[0:128, 0:512],
                                   qT[(h % 2) * 64:(h % 2) * 64 + 64, h // 2,
                                      rt * 128:(rt + 1) * 128],
                                   kT[(h % 2) * 64:(h % 2) * 64 + 64, h // 2,
                                      jc * 512:(jc + 1) * 512],
                                   start=True, stop=True),
                               wait=(([("s", proj_done), ("wg1", cwoT)]
                                      if jc == 0 else [])
                                     + [("v", dots_ev.get(jc - 2,
                                                          p.val("v") if jc == 0
                                                          else 0))]),
                               inc="mmA")
                    dots_ev[jc] = p.do(
                        "vector", lambda e, jc=jc, bank=bank, cmd=cmd:
                        e.tensor_scalar(
                            dots[:, jc * 512:(jc + 1) * 512],
                            bank[0:128, 0:512], 0.125, None, ALU.mult),
                        wait=[("mmA", cmd), ("v", p.val("v"))], inc="v")
                csh = p.do("vector", lambda e, rt=rt, JC=JC: e.tensor_scalar(
                    cols[:, 10:11], jlsb[:, rt:rt + 1],
                    float((JC - 1) * 512), None, ALU.subtract),
                    wait=[("v", p.val("v")), ("wldh", cjl), ("g", cjx)],
                    inc="v")
                cmk = p.do("vector", lambda e: e.tensor_scalar(
                    mkbuf[:, :], jidx[:, :], cols[:, 10:11], None, ALU.is_ge),
                    wait=[("v", p.val("v"))], inc="v")
                csel = p.do("vector", lambda e, JC=JC: e.copy_predicated(
                    dots[:, (JC - 1) * 512:JC * 512], mkbuf[:, :],
                    consts[:, 3:4].to_broadcast([128, 512])),
                    wait=[("v", p.val("v"))], inc="v")
                W = JC * 512
                cmx = p.do("vector", lambda e, W=W: e.reduce_max(
                    cols[:, 0:1], dots[:, 0:W], axis=AX.X),
                    wait=[("v", p.val("v"))], inc="v")
                cnm = p.do("vector", lambda e: e.tensor_scalar(
                    cols[:, 1:2], cols[:, 0:1], -1.0, None, ALU.mult),
                    wait=[("v", p.val("v"))], inc="v")
                cex = p.do("scalar", lambda e, W=W: e.activation(
                    ebuf[:, 0:W], dots[:, 0:W], AF.Exp, bias=cols[:, 1:2],
                    accum_out=cols[:, 2:3]),
                    wait=[("v", cnm), ("s", p.val("s"))], inc="s")
                crc = p.do("vector", lambda e: e.reciprocal(
                    cols[:, 3:4], cols[:, 2:3]),
                    wait=[("s", cex), ("v", p.val("v"))], inc="v")
                cln = p.do("scalar", lambda e, W=W: e.activation(
                    lnpb[:, 0:W], ebuf[:, 0:W], AF.Ln, scale=cols[:, 3:4],
                    bias=ccols[2]), wait=[("v", crc), ("s", p.val("s"))],
                    inc="s")
                cent = p.do("vector", lambda e, W=W: e.scalar_tensor_tensor(
                    lnpb[:, 0:W], ebuf[:, 0:W], cols[:, 3:4], lnpb[:, 0:W],
                    ALU.mult, ALU.mult, accum_out=cols[:, 4:5]),
                    wait=[("s", cln), ("v", p.val("v"))], inc="v")
                cneg = p.do("vector", lambda e: e.tensor_scalar(
                    cols[:, 5:6], cols[:, 4:5], -1.0, None, ALU.mult),
                    wait=[("v", p.val("v"))], inc="v")
                cpl = p.do("vector", lambda e: e.tensor_scalar(
                    cols[:, 6:7], cols[:, 5:6], POLY[0], POLY[1],
                    ALU.mult, ALU.add), wait=[("v", p.val("v"))], inc="v")
                for ci in (POLY[2], POLY[3], POLY[4]):
                    cpl = p.do("vector", lambda e, ci=ci: e.tensor_scalar(
                        cols[:, 6:7], cols[:, 6:7], cols[:, 5:6], ci,
                        ALU.mult, ALU.add), wait=[("v", p.val("v"))], inc="v")
                cb1 = p.do("vector", lambda e: e.tensor_scalar(
                    cols[:, 7:8], cols[:, 6:7], 1.0, None, ALU.max),
                    wait=[("v", p.val("v"))], inc="v")
                cg1 = p.do("vector", lambda e: e.tensor_scalar(
                    cols[:, 8:9], cols[:, 5:6], 0.5, None, ALU.is_gt),
                    wait=[("v", p.val("v"))], inc="v")
                cb2 = p.do("vector", lambda e: e.tensor_scalar(
                    cols[:, 7:8], cols[:, 7:8], 1.0, None, ALU.subtract),
                    wait=[("v", p.val("v"))], inc="v")
                cb3 = p.do("vector", lambda e: e.tensor_mul(
                    cols[:, 7:8], cols[:, 7:8], cols[:, 8:9]),
                    wait=[("v", p.val("v"))], inc="v")
                cb4 = p.do("vector", lambda e: e.tensor_scalar(
                    cols[:, 7:8], cols[:, 7:8], 1.0, None, ALU.add),
                    wait=[("v", p.val("v"))], inc="v")
                cbb = p.do("vector", lambda e: e.tensor_mul(
                    cols[:, 9:10], cols[:, 7:8], cols[:, 1:2]),
                    wait=[("v", p.val("v"))], inc="v")
                cp2 = p.do("scalar", lambda e, W=W: e.activation(
                    p2b[:, 0:W], dots[:, 0:W], AF.Exp, scale=cols[:, 7:8],
                    bias=cols[:, 9:10], accum_out=cols[:, 11:12]),
                    wait=[("v", cbb), ("s", p.val("s"))], inc="s")
                crc2 = p.do("vector", lambda e: e.reciprocal(
                    cols[:, 12:13], cols[:, 11:12]),
                    wait=[("s", cp2), ("v", p.val("v"))], inc="v")
                cpn = p.do("vector", lambda e, W=W: e.tensor_scalar(
                    p2b[:, 0:W], p2b[:, 0:W], cols[:, 12:13], None, ALU.mult),
                    wait=[("v", crc2), ("s", cp2)], inc="v")
                njt = jstat // 128
                for jt in range(njt):
                    bank = ptp[jt % 2]
                    ctt = p.do("tensor", lambda e, jt=jt, bank=bank:
                               e.transpose(
                                   bank[0:128, 0:128],
                                   p2b[:, jt * 128:(jt + 1) * 128],
                                   identb[:, :]),
                               wait=[("v", cpn), ("sp", p.val("sp"))],
                               inc="mmA")
                    cpt = p.do("scalar", lambda e, ctt=ctt, bank=bank, jt=jt:
                               e.activation(pTs[jt % 2][:, :],
                                            bank[0:128, 0:128], AF.Copy),
                               wait=[("mmA", p.val("mmA")), ("s", p.val("s"))],
                               inc="sp")
                    p.do("tensor", lambda e, jt=jt, cpt=cpt: e.matmul(
                        pav[0:64, 0:128], vhb[:, jt, :], pTs[jt % 2][:, :],
                        start=(jt == 0), stop=(jt == njt - 1),
                        skip_group_check=True),
                        wait=[("sp", cpt), ("wg2", cvh)], inc="mmA")
                cavd = p.val("mmA")
                cav = p.do("vector", lambda e, cavd=cavd: e.tensor_copy(
                    avsb[0:64, :], pav[0:64, 0:128]),
                    wait=[("mmA", cavd), ("v", p.val("v")),
                          ("st2", p.val("st2"))], inc="v")
                p.do("sync", lambda e, h=h, rt=rt, cav=cav: e.dma_start(
                    out=av_hbm[h * 64:(h + 1) * 64,
                               rt * 128:(rt + 1) * 128], in_=avsb[0:64, :]),
                    wait=[("v", cav)], inc="st2", amt=16)
        attn_done = p.val("st2")

        for rt in range(NRT):
            cavl = p.do("sync", lambda e, rt=rt: e.dma_start(
                out=avT[:, :, :],
                in_=av_hbm[:, rt * 128:(rt + 1) * 128]
                .rearrange("(kt kp) r -> kp kt r", kp=128)),
                wait=[("st2", attn_done), ("mmo", p.val("mmo"))],
                inc="avl", amt=16)
            for dc in range(2):
                bank = pw1 if dc == 0 else pw2
                for kt in range(8):
                    p.do("tensor", lambda e, rt=rt, dc=dc, kt=kt, bank=bank:
                         e.matmul(
                             bank[0:128, 0:512], avT[:, kt, :],
                             woTr[:, kt, dc * 512:(dc + 1) * 512],
                             start=(kt == 0), stop=False),
                         wait=([("avl", cavl), ("wg1", cwoT),
                                ("vG", p.val("vG"))] if kt == 0 else []),
                         inc=None)
                cmo = p.do("tensor", lambda e, dc=dc, bank=bank: e.matmul(
                    bank[0:128, 0:512], onesb[0:1, :],
                    bosb[0:1, dc * 512:(dc + 1) * 512],
                    start=False, stop=True),
                    wait=[("wld", wld0), ("v", cconst)], inc="mmo")
            cl2 = p.do("sync", lambda e, rt=rt: e.dma_start(
                out=xt[:, :], in_=xq_in[rt * 128:(rt + 1) * 128, :]),
                wait=[("vG", p.val("vG"))], inc="ld2", amt=16)
            cv1 = p.do("vector", lambda e, cmo=cmo, cl2=cl2: e.tensor_add(
                scr[:, 0:512], xt[:, 0:512], pw1[0:128, 0:512]),
                wait=[("mmo", cmo), ("ld2", cl2), ("st", p.val("st")),
                      ("v", p.val("v"))], inc="vG")
            cv2 = p.do("vector", lambda e: e.tensor_add(
                scr[:, 512:1024], xt[:, 512:1024], pw2[0:128, 0:512]),
                wait=[("vG", cv1)], inc="vG")
            p.do("sync", lambda e, rt=rt: e.dma_start(
                out=y_out[rt * 128:(rt + 1) * 128, :], in_=scr[:, :]),
                wait=[("vG", cv2)], inc="st", amt=16)
        p.wait_only("gpsimd", [("st", p.val("st"))])
        p.finish()
    return nc


def _rows_parity(b, half):
    return np.arange(half, 4096, 2) + b * 4096


def run_attnb1(x, inputs):
    nc = _get("attnb1", build_attnb1)
    w = dict(
        wq=np.ascontiguousarray(inputs["L1_attn_b_Wq"]),
        wk=np.ascontiguousarray(inputs["L1_attn_b_Wk"]),
        wv=np.ascontiguousarray(inputs["L1_attn_b_Wv"]),
        wo=np.ascontiguousarray(inputs["L1_attn_b_Wo"]),
        bo=np.ascontiguousarray(inputs["L1_attn_b_bo"]).reshape(1, D),
    )
    in_maps = []
    for c in range(NCORES):
        b, half = c // 2, c % 2
        rows = _rows_parity(b, half)
        jl = np.ascontiguousarray(
            (np.arange(half, 4096, 2) + 1).astype(np.float32)
            .reshape(16, 128).T)
        in_maps.append(dict(
            x=np.ascontiguousarray(x[b * 4096:(b + 1) * 4096]),
            xq=np.ascontiguousarray(x[rows]),
            jl=jl, **w))
    r = run_bass_kernel_spmd(nc, in_maps, core_ids=list(range(NCORES)))
    out = np.empty_like(x)
    for c in range(NCORES):
        b, half = c // 2, c % 2
        out[_rows_parity(b, half)] = r.results[c]["y"]
    return out



def build_attnb0():
    nc = bass.Bass("TRN2", target_bir_lowering=False)
    TB, TQ, NK = 4096, 2048, 16
    NRT = TQ // 128
    M = 16
    JALL = TB + M

    x_in = nc.dram_tensor("x", [TB, D], F32, kind="ExternalInput").ap()
    xq_in = nc.dram_tensor("xq", [TQ, D], F32, kind="ExternalInput").ap()
    jl_in = nc.dram_tensor("jl", [128, NRT], F32, kind="ExternalInput").ap()
    wq_in = nc.dram_tensor("wq", [D, D], F32, kind="ExternalInput").ap()
    wkv_in = nc.dram_tensor("wkv", [D, 2 * D], F32, kind="ExternalInput").ap()
    wo_in = nc.dram_tensor("wo", [D, D], F32, kind="ExternalInput").ap()
    bo_in = nc.dram_tensor("bo", [1, D], F32, kind="ExternalInput").ap()
    pre_in = nc.dram_tensor("pre", [16, 16], F32, kind="ExternalInput").ap()
    memkT_in = nc.dram_tensor("memkT", [D, M], F32, kind="ExternalInput").ap()
    memvJ_in = nc.dram_tensor("memvJ", [M, D], F32, kind="ExternalInput").ap()
    y_out = nc.dram_tensor("y", [TQ, D], F32, kind="ExternalOutput").ap()
    xn_hbm = nc.dram_tensor("xn_hbm", [TB, D], BF16).ap()
    xqn_hbm = nc.dram_tensor("xqn_hbm", [TQ, D], BF16).ap()
    k_hbm = nc.dram_tensor("k_hbm", [D, JALL], BF16).ap()
    v_hbm = nc.dram_tensor("v_hbm", [JALL, D], BF16).ap()
    av_hbm = nc.dram_tensor("av_hbm", [D, TQ], BF16).ap()

    p = Prog(nc)
    from contextlib import ExitStack
    with ExitStack() as ctx:
        sb = lambda n, sh, dt: ctx.enter_context(nc.sbuf_tensor(n, sh, dt))
        psf = lambda n, sh, dt=F32: ctx.enter_context(nc.psum_tensor(n, sh, dt))
        xt = sb("xt", [128, 1024], F32)
        xn = sb("xn", [128, 1024], BF16)
        scr = sb("scr", [128, 1024], F32)
        stats = sb("stats", [128, 8], F32)
        Khat = sb("Khat", [128, 8, JALL], BF16)
        qT = sb("qT", [128, 8, TQ], BF16)
        ubig = sb("ubig", [128, 32768], BF16)
        vk = sb("vk", [128, 33, 64], BF16)
        sT = sb("sT", [128, 8, 16], F32)
        presb = sb("presb", [2, 8, 16], BF16)
        UT = sb("UT", [2, 128], BF16)
        it2 = sb("it2", [2, 128], F32)
        it2b = sb("it2b", [2, 128], F32)
        avT = sb("avT", [128, 8, 128], BF16)
        avsb = sb("avsb", [64, 128], BF16)
        pTsb1 = sb("pTsb1", [128, 128], BF16)
        pTsb2 = sb("pTsb2", [128, 128], BF16)
        kstage = sb("kstage", [128, 512], BF16)
        m8 = sb("m8", [128, 8], F32)
        jlsb = sb("jlsb", [128, NRT], F32)
        cols = sb("cols", [128, 16], F32)
        bosb = sb("bosb", [1, 1024], BF16)
        onesb = sb("onesb", [1, 128], BF16)
        identb = sb("identb", [128, 128], BF16)
        identf = sb("identf", [128, 128], F32)
        ones128 = sb("ones128", [128, 128], F32)
        consts = sb("consts", [128, 4], F32)
        pa = psf("pa", [128, 512]); pb = psf("pb", [128, 512])
        pt1 = psf("pt1", [128, 128], BF16); pt2 = psf("pt2", [128, 128], BF16)
        pav = psf("pav", [128, 128])
        pw1 = psf("pw1", [128, 512]); pw2 = psf("pw2", [128, 512])

        dots = ubig[:, 0:8224].bitcast(F32)
        wz = ubig[:, 8224:16448].bitcast(F32)
        wzi = ubig[:, 8224:16448].bitcast(I32)
        p2b = ubig[:, 16448:20560]
        jidx = ubig[:, 20560:28784].bitcast(F32)
        xnTs = ubig[:, 0:4096].rearrange("p (kt f) -> p kt f", kt=8)
        wkvK = ubig[:, 4096:12288].rearrange("p (kt f) -> p kt f", kt=8)
        wkvV = ubig[:, 12288:20480].rearrange("p (kt f) -> p kt f", kt=8)
        wqR = ubig[:, 20480:28672].rearrange("p (kt f) -> p kt f", kt=8)
        woR = ubig[:, 0:8192].rearrange("p (kt f) -> p kt f", kt=8)

        _ident_f32(p, identf, ones128)
        ident_done = p.val("g")
        ccols = _emit_consts(p, consts, [1e-5, -0.5, 0.125, -1e30])
        cid = p.do("vector", lambda e: e.tensor_copy(identb[:, :], identf[:, :]),
                   wait=[("g", ident_done)], inc="v")
        p.do("vector", lambda e: e.memset(onesb[:, :], 1.0), inc="v")
        cconst = p.val("v")

        ci1 = p.do("gpsimd", lambda e: e.iota(
            it2[:, :], [[1, 128]], base=0, channel_multiplier=-64,
            allow_small_or_imprecise_dtypes=True), inc="g")
        ci2 = p.do("gpsimd", lambda e: e.iota(
            it2b[:, :], [[-1, 128]], base=63, channel_multiplier=64,
            allow_small_or_imprecise_dtypes=True), inc="g")
        cu1 = p.do("vector", lambda e: e.tensor_scalar(
            it2[:, :], it2[:, :], 0.0, None, ALU.is_ge),
            wait=[("g", p.val("g")), ("v", cconst)], inc="v")
        cu2 = p.do("vector", lambda e: e.tensor_scalar(
            it2b[:, :], it2b[:, :], 0.0, None, ALU.is_ge),
            wait=[("v", p.val("v"))], inc="v")
        cut = p.do("vector", lambda e: e.tensor_mul(
            UT[:, :], it2[:, :], it2b[:, :]), wait=[("v", p.val("v"))], inc="v")

        p.do("gpsimd", lambda e: e.dma_start(out=bosb[:, :], in_=bo_in),
             inc="wld", amt=16)
        for kt in range(8):
            p.do("gpsimd", lambda e, kt=kt: e.dma_start(
                out=presb[:, kt, :], in_=pre_in[kt * 2:(kt + 1) * 2, :]),
                inc="wld", amt=16)
        p.do("gpsimd", lambda e: e.dma_start(
            out=k_hbm[:, 0:M], in_=memkT_in), inc="wld", amt=16)
        p.do("gpsimd", lambda e: e.dma_start(
            out=v_hbm[0:M, :], in_=memvJ_in), inc="wld", amt=16)
        wld0 = p.val("wld")
        cjl = p.do("sync", lambda e: e.dma_start(out=jlsb[:, :], in_=jl_in),
                   inc="wldh", amt=16)
        for kt in range(8):
            ct = p.do("tensor", lambda e, kt=kt: e.matmul(
                pav[0:128, 0:16], UT[:, :], presb[:, kt, :],
                start=True, stop=True),
                wait=[("v", cut), ("wld", wld0), ("v", p.val("v"))], inc="mmt")
            p.do("vector", lambda e, kt=kt, ct=ct: e.tensor_scalar(
                sT[:, kt, :], pav[0:128, 0:16], 0.125, None, ALU.mult),
                wait=[("mmt", ct), ("v", p.val("v"))], inc="v")
        sT_done = p.val("v")

        prev = p.val("v")
        for src, dst, n_t in [(x_in, xn_hbm, TB // 128),
                              (xq_in, xqn_hbm, TQ // 128)]:
            for tt in range(n_t):
                cl = p.do("sync", lambda e, src=src, tt=tt: e.dma_start(
                    out=xt[:, :], in_=src[tt * 128:(tt + 1) * 128, :]),
                    wait=[("v", prev)], inc="ld", amt=16)
                cxn = _emit_ln(p, xt, xn, stats, scr, tt, cl, ccols[0],
                               ccols[1], cconst, prev_v=prev,
                               prev_s=p.val("s"), prev_store=p.val("xnst"))
                p.do("sync", lambda e, dst=dst, tt=tt: e.dma_start(
                    out=dst[tt * 128:(tt + 1) * 128, :], in_=xn[:, :]),
                    wait=[("v", cxn)], inc="xnst", amt=16)
                prev = cxn
        xnst_all = p.val("xnst")

        ckv1 = p.do("gpsimd", lambda e: e.dma_start(
            out=wkvK[:, :, :],
            in_=wkv_in[:, 0:1024].rearrange("(kt kp) f -> kp kt f", kp=128)),
            inc="wg1", amt=16)
        ckv2 = p.do("gpsimd", lambda e: e.dma_start(
            out=wkvV[:, :, :],
            in_=wkv_in[:, 1024:2048].rearrange("(kt kp) f -> kp kt f", kp=128)),
            inc="wg1", amt=16)
        cq1 = p.do("gpsimd", lambda e: e.dma_start(
            out=wqR[:, :, :],
            in_=wq_in.rearrange("(kt kp) f -> kp kt f", kp=128)),
            inc="wg1", amt=16)
        for tc in range(TB // 512):
            for kd in range(8):
                p.do("sync", lambda e, tc=tc, kd=kd: e.dma_start(
                    out=xnTs[:, kd, :],
                    in_=xn_hbm[tc * 512:(tc + 1) * 512,
                               kd * 128:(kd + 1) * 128],
                    transpose=True),
                    wait=[("xnst", xnst_all), ("mmh", p.val("mmh"))],
                    inc="xnT", amt=16)
            xnT_c = p.val("xnT")
            for ft in range(8):
                for kd in range(8):
                    cmh = p.do("tensor", lambda e, ft=ft, kd=kd: e.matmul(
                        pa[0:128, 0:512],
                        wkvK[:, kd, ft * 128:(ft + 1) * 128],
                        xnTs[:, kd, :], start=(kd == 0), stop=(kd == 7)),
                        wait=([("wg1", cq1), ("xnT", xnT_c),
                               ("s", p.val("s")),
                               ("st1", p.val("st1"))] if kd == 0 else []),
                        inc=("mmh" if kd == 7 else None))
                cev = p.do("scalar", lambda e, cmh=cmh: e.activation(
                    kstage[:, :], pa[0:128, 0:512], AF.Copy),
                    wait=[("mmh", cmh), ("s", p.val("s")),
                          ("st1", p.val("st1"))], inc="s")
                p.do("sync", lambda e, ft=ft, tc=tc, cev=cev: e.dma_start(
                    out=k_hbm[ft * 128:(ft + 1) * 128,
                              M + tc * 512:M + (tc + 1) * 512],
                    in_=kstage[:, :]), wait=[("s", cev)], inc="st1", amt=16)
            for fc in range(2):
                for sub in range(4):
                    for kd in range(8):
                        cmh = p.do("tensor", lambda e, fc=fc, sub=sub, kd=kd:
                                   e.matmul(
                                       pb[0:128, 0:512],
                                       xnTs[:, kd, sub * 128:(sub + 1) * 128],
                                       wkvV[:, kd, fc * 512:(fc + 1) * 512],
                                       start=(kd == 0), stop=(kd == 7)),
                                   wait=([("s", p.val("s"))] if kd == 0 else []),
                                   inc=("mmh" if kd == 7 else None))
                    cev = p.do("scalar", lambda e, cmh=cmh: e.activation(
                        kstage[:, :], pb[0:128, 0:512], AF.Copy),
                        wait=[("mmh", cmh), ("s", p.val("s")),
                              ("st1", p.val("st1"))], inc="s")
                    p.do("sync", lambda e, tc=tc, sub=sub, fc=fc, cev=cev:
                         e.dma_start(
                             out=v_hbm[M + tc * 512 + sub * 128:
                                       M + tc * 512 + (sub + 1) * 128,
                                       fc * 512:(fc + 1) * 512],
                             in_=kstage[:, :]),
                         wait=[("s", cev)], inc="st1", amt=16)
        for tc in range(TQ // 512):
            for kd in range(8):
                p.do("sync", lambda e, tc=tc, kd=kd: e.dma_start(
                    out=xnTs[:, kd, :],
                    in_=xqn_hbm[tc * 512:(tc + 1) * 512,
                                kd * 128:(kd + 1) * 128],
                    transpose=True),
                    wait=[("xnst", xnst_all), ("mmh", p.val("mmh"))],
                    inc="xnT", amt=16)
            xnT_c = p.val("xnT")
            for ft in range(8):
                for kd in range(8):
                    cmh = p.do("tensor", lambda e, ft=ft, kd=kd: e.matmul(
                        pa[0:128, 0:512], wqR[:, kd, ft * 128:(ft + 1) * 128],
                        xnTs[:, kd, :], start=(kd == 0), stop=(kd == 7)),
                        wait=([("xnT", xnT_c), ("s", p.val("s"))]
                              if kd == 0 else []),
                        inc=("mmh" if kd == 7 else None))
                p.do("scalar", lambda e, ft=ft, tc=tc, cmh=cmh: e.activation(
                    qT[:, ft, tc * 512:(tc + 1) * 512], pa[0:128, 0:512],
                    AF.Copy), wait=[("mmh", cmh), ("s", p.val("s"))], inc="s")
        proj_done = p.val("s")
        proj_mm_done = p.val("mmh")
        proj_st = p.val("st1")
        cji = p.do("gpsimd", lambda e: e.iota(
            jidx[:, :], [[1, JALL]], base=0, channel_multiplier=0,
            allow_small_or_imprecise_dtypes=True),
            wait=[("mmh", proj_mm_done), ("s", proj_done)], inc="g")

        dpp = [pa, pb]
        ptp = [pt1, pt2]
        pTs = [pTsb1, pTsb2]
        for k in range(NK):
            ckh = p.do("gpsimd", lambda e, k=k: e.dma_start(
                out=Khat[:, :, :],
                in_=k_hbm.rearrange("(kt kp) j -> kp kt j", kp=128)),
                wait=[("st1", proj_st), ("mmA", p.val("mmA")),
                      ("v", p.val("v"))], inc="wg2", amt=16)
            csc = 0
            for kt in range(8):
                csc = p.do("vector", lambda e, k=k, kt=kt: e.tensor_scalar(
                    Khat[:, kt, :], Khat[:, kt, :], sT[:, kt, k:k + 1], None,
                    ALU.mult),
                    wait=[("wg2", ckh), ("v", max(sT_done, p.val("v")))],
                    inc="v")
            cvk1 = p.do("gpsimd", lambda e, k=k: e.dma_start(
                out=vk[:, 0:32, :],
                in_=v_hbm[0:4096, k * 64:(k + 1) * 64]
                .rearrange("(jt jp) d2 -> jp jt d2", jp=128)),
                wait=[("st1", proj_st), ("mmA", p.val("mmA"))],
                inc="wg2", amt=16)
            cvk2 = p.do("gpsimd", lambda e, k=k: e.dma_start(
                out=vk[0:16, 32, :], in_=v_hbm[4096:4112, k * 64:(k + 1) * 64]),
                inc="wg2", amt=16)
            for rt in range(NRT):
                JTOT = (rt + 1) * 256 + M
                JCB = (JTOT + 511) // 512
                dots_ev = {}
                for jc in range(JCB):
                    wjc = min(512, JTOT - jc * 512)
                    bank = dpp[jc % 2]
                    cmd = p.do("tensor", lambda e, k=k, rt=rt, jc=jc, wjc=wjc,
                               bank=bank, kt=0: e.matmul(
                                   bank[0:128, 0:wjc],
                                   qT[:, 0, rt * 128:(rt + 1) * 128],
                                   Khat[:, 0, jc * 512:jc * 512 + wjc],
                                   start=True, stop=False),
                               wait=(([("s", proj_done), ("v", csc)]
                                      if jc == 0 else [])
                                     + [("v", dots_ev.get(jc - 2, 0))]),
                               inc="mmA")
                    for kt in range(1, 8):
                        cmd = p.do("tensor", lambda e, jc=jc, wjc=wjc, kt=kt,
                                   rt=rt, bank=bank: e.matmul(
                                       bank[0:128, 0:wjc],
                                       qT[:, kt, rt * 128:(rt + 1) * 128],
                                       Khat[:, kt, jc * 512:jc * 512 + wjc],
                                       start=False, stop=(kt == 7)),
                                   inc="mmA")
                    dots_ev[jc] = p.do(
                        "vector", lambda e, jc=jc, wjc=wjc, bank=bank, cmd=cmd:
                        e.tensor_copy(dots[:, jc * 512:jc * 512 + wjc],
                                      bank[0:128, 0:wjc]),
                        wait=[("mmA", cmd), ("v", p.val("v"))], inc="v")
                mstart = max(0, (JCB - 2) * 512)
                cwm = p.do("vector", lambda e, rt=rt, mstart=mstart, JTOT=JTOT:
                           e.tensor_scalar(
                               wzi[:, mstart:JTOT], jidx[:, mstart:JTOT],
                               jlsb[:, rt:rt + 1], None, ALU.is_ge),
                           wait=[("v", p.val("v")), ("wldh", cjl),
                                 ("g", cji)], inc="v")
                cms = p.do("vector", lambda e, mstart=mstart, JTOT=JTOT:
                           e.copy_predicated(
                               dots[:, mstart:JTOT], wzi[:, mstart:JTOT],
                               consts[:, 3:4].to_broadcast(
                                   [128, JTOT - mstart])),
                           wait=[("v", p.val("v"))], inc="v")
                cmx = p.do("vector", lambda e, JTOT=JTOT: e.max(
                    out=m8[:, :], in_=dots[:, 0:JTOT]),
                    wait=[("v", p.val("v"))], inc="v")
                csv = p.do("vector", lambda e: e.tensor_scalar(
                    cols[:, 1:2], m8[:, 0:1], -1.0, None, ALU.mult),
                    wait=[("v", p.val("v"))], inc="v")
                cmr = p.do("vector", lambda e, JTOT=JTOT: e.match_replace(
                    out=wz[:, 0:JTOT], in_to_replace=m8[:, :],
                    in_values=dots[:, 0:JTOT], imm_value=-2e30),
                    wait=[("v", p.val("v"))], inc="v")
                for rnd in range(7):
                    cmx = p.do("vector", lambda e, JTOT=JTOT: e.max(
                        out=m8[:, :], in_=wz[:, 0:JTOT]),
                        wait=[("v", p.val("v"))], inc="v")
                    cmr = p.do("vector", lambda e, JTOT=JTOT: e.match_replace(
                        out=wz[:, 0:JTOT], in_to_replace=m8[:, :],
                        in_values=wz[:, 0:JTOT], imm_value=-2e30),
                        wait=[("v", p.val("v"))], inc="v")
                ct64 = p.do("vector", lambda e: e.tensor_copy(
                    cols[:, 2:3], m8[:, 7:8]), wait=[("v", p.val("v"))],
                    inc="v")
                ckm = p.do("vector", lambda e, JTOT=JTOT: e.tensor_scalar(
                    wzi[:, 0:JTOT], dots[:, 0:JTOT], cols[:, 2:3], None,
                    ALU.is_lt), wait=[("v", p.val("v"))], inc="v")
                ckp = p.do("vector", lambda e, JTOT=JTOT: e.copy_predicated(
                    dots[:, 0:JTOT], wzi[:, 0:JTOT],
                    consts[:, 3:4].to_broadcast([128, JTOT])),
                    wait=[("v", p.val("v"))], inc="v")
                cp2 = p.do("scalar", lambda e, JTOT=JTOT: e.activation(
                    p2b[:, 0:JTOT], dots[:, 0:JTOT], AF.Exp,
                    bias=cols[:, 1:2], accum_out=cols[:, 4:5]),
                    wait=[("v", ckp), ("s", p.val("s"))], inc="s")
                crc = p.do("vector", lambda e: e.reciprocal(
                    cols[:, 5:6], cols[:, 4:5]),
                    wait=[("s", cp2), ("v", p.val("v"))], inc="v")
                cpn = p.do("vector", lambda e, JTOT=JTOT: e.tensor_scalar(
                    p2b[:, 0:JTOT], p2b[:, 0:JTOT], cols[:, 5:6], None,
                    ALU.mult), wait=[("v", crc), ("s", cp2)], inc="v")
                njt = (JTOT + 127) // 128
                for jt in range(njt):
                    kp = min(128, JTOT - jt * 128)
                    bank = ptp[jt % 2]
                    ctt = p.do("tensor", lambda e, jt=jt, kp=kp, bank=bank:
                               e.transpose(
                                   bank[0:kp, 0:128],
                                   p2b[:, jt * 128:jt * 128 + kp],
                                   identb[:, :]),
                               wait=[("v", cpn), ("sp", p.val("sp"))],
                               inc="mmA")
                    cpt = p.do("scalar", lambda e, jt=jt, kp=kp, bank=bank:
                               e.activation(pTs[jt % 2][0:kp, :],
                                            bank[0:kp, 0:128], AF.Copy),
                               wait=[("mmA", p.val("mmA")), ("s", p.val("s"))],
                               inc="sp")
                    p.do("tensor", lambda e, jt=jt, kp=kp, cpt=cpt, njt=njt:
                         e.matmul(
                             pav[0:64, 0:128], vk[0:kp, jt, :],
                             pTs[jt % 2][0:kp, :],
                             start=(jt == 0), stop=(jt == njt - 1),
                             skip_group_check=True),
                         wait=[("sp", cpt), ("wg2", cvk2)], inc="mmA")
                cavd = p.val("mmA")
                cav = p.do("vector", lambda e, cavd=cavd: e.tensor_copy(
                    avsb[0:64, :], pav[0:64, 0:128]),
                    wait=[("mmA", cavd), ("v", p.val("v")),
                          ("st2", p.val("st2"))], inc="v")
                p.do("sync", lambda e, k=k, rt=rt, cav=cav: e.dma_start(
                    out=av_hbm[k * 64:(k + 1) * 64,
                               rt * 128:(rt + 1) * 128], in_=avsb[0:64, :]),
                    wait=[("v", cav)], inc="st2", amt=16)
        attn_done = p.val("st2")
        attn_mm_done = p.val("mmA")

        cwoR = p.do("gpsimd", lambda e: e.dma_start(
            out=woR[:, :, :],
            in_=wo_in.rearrange("(kt kp) f -> kp kt f", kp=128)),
            wait=[("mmA", attn_mm_done), ("s", p.val("s")),
                  ("v", p.val("v"))], inc="wg1", amt=16)
        for rt in range(NRT):
            cavl = p.do("sync", lambda e, rt=rt: e.dma_start(
                out=avT[:, :, :],
                in_=av_hbm[:, rt * 128:(rt + 1) * 128]
                .rearrange("(kt kp) r -> kp kt r", kp=128)),
                wait=[("st2", attn_done), ("mmo", p.val("mmo"))],
                inc="avl", amt=16)
            for dc in range(2):
                bank = pw1 if dc == 0 else pw2
                for kt in range(8):
                    p.do("tensor", lambda e, rt=rt, dc=dc, kt=kt, bank=bank:
                         e.matmul(
                             bank[0:128, 0:512], avT[:, kt, :],
                             woR[:, kt, dc * 512:(dc + 1) * 512],
                             start=(kt == 0), stop=False),
                         wait=([("avl", cavl), ("wg1", cwoR),
                                ("vG", p.val("vG"))] if kt == 0 else []),
                         inc=None)
                cmo = p.do("tensor", lambda e, dc=dc, bank=bank: e.matmul(
                    bank[0:128, 0:512], onesb[0:1, :],
                    bosb[0:1, dc * 512:(dc + 1) * 512],
                    start=False, stop=True),
                    wait=[("wld", wld0), ("v", cconst)], inc="mmo")
            cl2 = p.do("sync", lambda e, rt=rt: e.dma_start(
                out=xt[:, :], in_=xq_in[rt * 128:(rt + 1) * 128, :]),
                wait=[("vG", p.val("vG"))], inc="ld2", amt=16)
            cv1 = p.do("vector", lambda e, cmo=cmo, cl2=cl2: e.tensor_add(
                scr[:, 0:512], xt[:, 0:512], pw1[0:128, 0:512]),
                wait=[("mmo", cmo), ("ld2", cl2), ("st", p.val("st")),
                      ("v", p.val("v"))], inc="vG")
            cv2 = p.do("vector", lambda e: e.tensor_add(
                scr[:, 512:1024], xt[:, 512:1024], pw2[0:128, 0:512]),
                wait=[("vG", cv1)], inc="vG")
            p.do("sync", lambda e, rt=rt: e.dma_start(
                out=y_out[rt * 128:(rt + 1) * 128, :], in_=scr[:, :]),
                wait=[("vG", cv2)], inc="st", amt=16)
        p.wait_only("gpsimd", [("st", p.val("st"))])
        p.finish()
    return nc


def run_attnb0(x, inputs):
    nc = _get("attnb0", build_attnb0)
    memk = np.ascontiguousarray(
        inputs["L0_attn_b_mem_k"].transpose(0, 2, 1).reshape(1024, 16))
    memv = np.ascontiguousarray(
        inputs["L0_attn_b_mem_v"].transpose(1, 0, 2).reshape(16, 1024))
    w = dict(
        wq=np.ascontiguousarray(inputs["L0_attn_b_Wq"]),
        wkv=np.ascontiguousarray(inputs["L0_attn_b_Wkv"]),
        wo=np.ascontiguousarray(inputs["L0_attn_b_Wo"]),
        bo=np.ascontiguousarray(inputs["L0_attn_b_bo"]).reshape(1, D),
        pre=np.ascontiguousarray(inputs["L0_attn_b_pre_proj"]),
        memkT=memk, memvJ=memv,
    )
    in_maps = []
    for c in range(NCORES):
        b, half = c // 2, c % 2
        rows = _rows_parity(b, half)
        jl = np.ascontiguousarray(
            (np.arange(half, 4096, 2) + 17).astype(np.float32)
            .reshape(16, 128).T)
        in_maps.append(dict(
            x=np.ascontiguousarray(x[b * 4096:(b + 1) * 4096]),
            xq=np.ascontiguousarray(x[rows]), jl=jl, **w))
    r = run_bass_kernel_spmd(nc, in_maps, core_ids=list(range(NCORES)))
    out = np.empty_like(x)
    for c in range(NCORES):
        b, half = c // 2, c % 2
        out[_rows_parity(b, half)] = r.results[c]["y"]
    return out



def _barrier(p, dma_sems=()):
    for s_name in dma_sems:
        p.wait_only("sync", [(s_name, p.val(s_name))])
        p.wait_only("gpsimd", [(s_name, p.val(s_name))])
    s = p.sem("bar")
    p.cnt["bar"] += 5
    tgt = p.cnt["bar"]
    for e in Prog.ENGINES:
        def run(engine, s=s, tgt=tgt):
            engine.sem_inc(s, 1)
            engine.wait_ge(s, tgt)
        getattr(p._block, e)(run)


_PAIRS = [[0, 1], [2, 3], [4, 5], [6, 7]]


def _emit_ar_add(p, nc, C, myout, arout, xcur, tag):
    cst = p.val("st3")
    car = p.do("gpsimd", lambda e: e.collective_compute(
        "AllReduce", ALU.add, replica_groups=_PAIRS,
        ins=[myout.opt()], outs=[arout.opt()]),
        wait=[("st3", cst)], inc="cc")
    xt, scr = C["xt"], C["scr"]
    prevv = p.val("vG")
    for tt in range(32):
        cl1 = p.do("sync", lambda e, tt=tt: e.dma_start(
            out=xt[:, :], in_=xcur[tt * 128:(tt + 1) * 128, :]),
            wait=[("cc", p.val("cc")), ("vG", p.val("vG"))],
            inc="ld2", amt=16)
        cl2 = p.do("sync", lambda e, tt=tt: e.dma_start(
            out=scr[:, :], in_=arout[tt * 128:(tt + 1) * 128, :]),
            wait=[("st4", p.val("st4"))], inc="ld3", amt=16)
        cv = p.do("vector", lambda e, cl1=cl1, cl2=cl2: e.tensor_add(
            scr[:, :], xt[:, :], scr[:, :]),
            wait=[("ld2", cl1), ("ld3", cl2), ("st4", p.val("st4")),
                  ("v", p.val("v"))], inc="vG")
        p.do("sync", lambda e, tt=tt: e.dma_start(
            out=xcur[tt * 128:(tt + 1) * 128, :], in_=scr[:, :]),
            wait=[("vG", cv)], inc="st4", amt=16)


def _emit_ln_all(p, nc, C, xcur, xn_hbm, n_rows):
    xt, xn, scr, stats = C["xt"], C["xn"], C["scr"], C["stats"]
    prev = p.val("v")
    for tt in range(n_rows // 128):
        cl = p.do("sync", lambda e, tt=tt: e.dma_start(
            out=xt[:, :], in_=xcur[tt * 128:(tt + 1) * 128, :]),
            wait=[("v", prev), ("st4", p.val("st4"))], inc="ld", amt=16)
        cxn = _emit_ln(p, xt, xn, stats, scr, tt, cl, C["eps"], C["nhalf"],
                       C["cconst"], prev_v=prev, prev_s=p.val("s"),
                       prev_store=p.val("xnst"))
        p.do("sync", lambda e, tt=tt: e.dma_start(
            out=xn_hbm[tt * 128:(tt + 1) * 128, :], in_=xn[:, :]),
            wait=[("v", cxn), ("xnT", p.val("xnT"))], inc="xnst", amt=16)
        prev = cxn
    return p.val("xnst")


def _emit_mamba1(p, nc, ctx, C, W, xcur, xn_hbm):
    T, NCH, CT, NSTATE = 4096, 8, 512, 16
    sb = lambda n, sh, dt: ctx.enter_context(nc.sbuf_tensor(n, sh, dt))
    xt, xn, scr, scr2, stats = C["xt"], C["xn"], C["scr"], C["scr2"], C["stats"]
    identf, ones128, consts = C["identf"], C["ones128"], C["consts"]
    ones32, sel32, onesb = C["ones32"], C["sel32"], C["onesb"]
    pa, pb, pc, pd = C["pa"], C["pb"], C["pc"], C["pd"]
    cconst = C["cconst"]
    ident_done = C["ident_done"]

    xnT = sb("m1_xnT", [128, 8, CT], BF16)
    delta = sb("m1_delta", [128, 8, CT], F32)
    ubuf = sb("m1_ubuf", [128, 8, CT], F32)
    ybuf = sb("m1_ybuf", [128, 8, CT], F32)
    yb = sb("m1_yb", [128, 8, CT], BF16)
    hseq = sb("m1_hseq", [128, CT], F32)
    ab8 = sb("m1_ab8", [128, 8, CT], F32)
    bx = sb("m1_bx", [128, 2, CT], F32)
    bcB = sb("m1_bcB", [128, CT], F32)
    bcC = sb("m1_bcC", [128, CT], F32)
    BC = sb("m1_BC", [32, CT], F32)
    state = sb("m1_state", [128, 8, NSTATE], F32)
    wxpsb = sb("m1_wxps", [128, 8, 1056], BF16)
    wosb = sb("m1_wos", [128, 8, 1024], BF16)
    bosb = sb("m1_bos", [1, 1024], BF16)
    ALsb = sb("m1_AL", [128, 8, NSTATE], F32)
    Asb = sb("m1_A", [128, 8, NSTATE], F32)
    bxpT = sb("m1_bxpT", [128, 9], F32)
    bxp_r = sb("m1_bxpr", [1, 1056], F32)
    dp_r = sb("m1_dpr", [1, 1024], F32)
    DpT = sb("m1_DpT", [128, 8], F32)

    p.do("gpsimd", lambda e: e.dma_start(
        out=wxpsb[:, :, :], in_=W["wxp"].rearrange("(kt kp) f -> kp kt f", kp=128)),
        inc="wld", amt=16)
    p.do("gpsimd", lambda e: e.dma_start(
        out=wosb[:, :, :], in_=W["wo"].rearrange("(kt kp) f -> kp kt f", kp=128)),
        inc="wld", amt=16)
    p.do("gpsimd", lambda e: e.dma_start(out=bosb[:, :], in_=W["bo"]),
         inc="wld", amt=16)
    wld0 = p.val("wld")
    p.do("sync", lambda e: e.dma_start(
        out=ALsb[:, :, :], in_=W["al"].rearrange("(dt dp2) n -> dp2 dt n", dp2=128)),
        inc="wldh", amt=16)
    p.do("sync", lambda e: e.dma_start(out=bxp_r[:, :], in_=W["bxp"]),
         inc="wldh", amt=16)
    cwh = p.do("sync", lambda e: e.dma_start(out=dp_r[:, :], in_=W["dp"]),
               inc="wldh", amt=16)
    ca1 = p.do("scalar", lambda e: e.activation(
        Asb[:, :, :], ALsb[:, :, :], AF.Exp),
        wait=[("wldh", cwh), ("s", p.val("s"))], inc="s")
    cAneg = p.do("vector", lambda e: e.tensor_scalar(
        Asb[:, :, :], Asb[:, :, :], -1.0, None, ALU.mult),
        wait=[("s", ca1), ("v", p.val("v"))], inc="v")
    ctp = p.chain("tensor", [
        (lambda e, f=f, w=(128 if f < 8 else 32): e.transpose(
            pa[0:w, f:f + 1], bxp_r[0:1, f * 128:f * 128 + w],
            identf[0:1, 0:1]))
        for f in range(9)] + [
        (lambda e, f=f: e.transpose(
            pb[0:128, f:f + 1], dp_r[0:1, f * 128:(f + 1) * 128],
            identf[0:1, 0:1]))
        for f in range(8)],
        wait=[("wldh", cwh), ("g", ident_done), ("v", p.val("v")),
              ("mmt", p.val("mmt"))], inc="mmt")
    prev_cp = p.chain("vector", [
        (lambda e: e.tensor_copy(bxpT[0:128, 0:8], pa[0:128, 0:8])),
        (lambda e: e.tensor_copy(bxpT[0:32, 8:9], pa[0:32, 8:9])),
        (lambda e: e.tensor_copy(DpT[0:128, 0:8], pb[0:128, 0:8])),
    ], wait=[("mmt", ctp), ("v", p.val("v"))], inc="v")
    prep_v = p.val("v")
    cst0 = p.do("vector", lambda e: e.memset(state[:, :, :], 0.0), inc="v")

    state_cnt = {}
    ymm_done = p.val("mmo")
    for ch in range(NCH):
        bv = p.val("v")
        p.wait_only("sync", [("vG", p.val("vG")), ("st", p.val("st"))])
        p.wait_only("vector", [("vG", p.val("vG")), ("st", p.val("st"))])
        p.wait_only("scalar", [("vG", p.val("vG")), ("st", p.val("st"))])
        xn_counts = []
        for ttl in range(CT // 128):
            tt = ch * (CT // 128) + ttl
            w = [("v", xn_counts[-1] if xn_counts else bv)]
            cl = p.do("sync", lambda e, tt=tt: e.dma_start(
                out=xt[:, :], in_=xcur[tt * 128:(tt + 1) * 128, :]),
                wait=w, inc="ld", amt=16)
            cxn = _emit_ln(p, xt, xn, stats, scr, tt, cl, C["eps"], C["nhalf"],
                           cconst, prev_v=(xn_counts[-1] if xn_counts else bv),
                           prev_s=p.val("s"), prev_store=p.val("xnst"))
            p.do("sync", lambda e, tt=tt: e.dma_start(
                out=xn_hbm[tt * 128:(tt + 1) * 128, :], in_=xn[:, :]),
                wait=[("v", cxn)], inc="xnst", amt=16)
            xn_counts.append(cxn)
        xnst_ch = p.val("xnst")
        for dt_ in range(8):
            p.do("sync", lambda e, dt_=dt_, ch=ch: e.dma_start(
                out=xnT[:, dt_, :],
                in_=xn_hbm[ch * CT:(ch + 1) * CT, dt_ * 128:(dt_ + 1) * 128],
                transpose=True),
                wait=[("xnst", xnst_ch), ("vG", p.val("vG")),
                      ("mmh", p.val("mmh"))], inc="xnT", amt=16)
        xnT_ch = p.val("xnT")

        ph_pair = [pa, pb]
        for f in range(9):
            w = 128 if f < 8 else 32
            for tc in range(CT // 512):
                bank = ph_pair[f % 2]
                cmh = p.chain("tensor", [
                    (lambda e, f=f, tc=tc, kt=kt, w=w, bank=bank: e.matmul(
                        bank[0:w, 0:512],
                        wxpsb[:, kt, f * 128:f * 128 + w],
                        xnT[:, kt, tc * 512:(tc + 1) * 512],
                        start=(kt == 0), stop=(kt == 7)))
                    for kt in range(8)],
                    wait=[("xnT", xnT_ch), ("wld", wld0),
                          ("v", p.val("v"))], inc="mmh")
                if f < 8:
                    cm1 = p.do("vector", lambda e, f=f, tc=tc, bank=bank,
                               cmh=cmh: e.tensor_scalar(
                                   scr[:, tc * 512:(tc + 1) * 512],
                                   bank[0:128, 0:512], bxpT[:, f:f + 1], 0.0,
                                   ALU.add, ALU.max),
                               wait=[("mmh", cmh), ("v", p.val("v"))], inc="v")
                    cs3 = p.chain("scalar", [
                        (lambda e, f=f, tc=tc, bank=bank: e.activation(
                            scr2[:, tc * 512:(tc + 1) * 512],
                            bank[0:128, 0:512], AF.Abs,
                            bias=bxpT[:, f:f + 1])),
                        (lambda e, tc=tc: e.activation(
                            scr2[:, tc * 512:(tc + 1) * 512],
                            scr2[:, tc * 512:(tc + 1) * 512], AF.Exp,
                            scale=-1.0)),
                        (lambda e, tc=tc: e.activation(
                            scr2[:, tc * 512:(tc + 1) * 512],
                            scr2[:, tc * 512:(tc + 1) * 512], AF.Ln,
                            bias=1.0)),
                    ], wait=[("mmh", cmh), ("v", prep_v),
                             ("s", p.val("s"))], inc="s")
                    p.do("vector", lambda e, f=f, tc=tc: e.tensor_add(
                        delta[:, f, tc * 512:(tc + 1) * 512],
                        scr[:, tc * 512:(tc + 1) * 512],
                        scr2[:, tc * 512:(tc + 1) * 512]),
                        wait=[("s", cs3), ("v", cm1)], inc="v")
                else:
                    p.do("vector", lambda e, tc=tc, bank=bank, cmh=cmh:
                         e.tensor_scalar(
                             BC[0:32, tc * 512:(tc + 1) * 512],
                             bank[0:32, 0:512], bxpT[0:32, 8:9], None,
                             ALU.add),
                         wait=[("mmh", cmh), ("v", p.val("v")),
                               ("mmt", p.val("mmt"))], inc="v")
        delta_done = p.val("v")
        bc_done = p.val("v")

        for dt_ in range(8):
            p.do("vector", lambda e, dt_=dt_: e.tensor_mul(
                ubuf[:, dt_, :], delta[:, dt_, :], xnT[:, dt_, :]),
                wait=[("v", p.val("v"))], inc="v")
        for dt_ in range(8):
            p.do("vector", lambda e, dt_=dt_: e.memset(ybuf[:, dt_, :], 0.0),
                 wait=[("v", p.val("v"))], inc="v")

        for n in range(16):
            ca = p.do("gpsimd", lambda e, n=n: e.affine_select(
                sel32[:, :], ones32[:, :], [[0, 128]], ALU.is_equal, 0.0,
                base=-n, channel_multiplier=1),
                wait=[("v", C["c32"]), ("mmt", p.val("mmt"))], inc="g")
            ctb = p.do("tensor", lambda e: e.matmul(
                pc[0:128, 0:512], sel32[:, :], BC[0:32, :],
                start=True, stop=True),
                wait=[("g", ca), ("v", p.val("v")),
                      ("vG", p.val("vG"))], inc="mmt")
            p.do("vector", lambda e, ctb=ctb: e.tensor_copy(
                bcB[:, :], pc[0:128, 0:512]),
                wait=[("mmt", ctb), ("v", p.val("v"))], inc="v")
            ca2_ = p.do("gpsimd", lambda e, n=n: e.affine_select(
                sel32[:, :], ones32[:, :], [[0, 128]], ALU.is_equal, 0.0,
                base=-(16 + n), channel_multiplier=1),
                wait=[("mmt", p.val("mmt"))], inc="g")
            ctc = p.do("tensor", lambda e: e.matmul(
                pd[0:128, 0:512], sel32[:, :], BC[0:32, :],
                start=True, stop=True),
                wait=[("g", ca2_), ("v", p.val("v")), ("vG", p.val("vG"))],
                inc="mmt")
            p.do("vector", lambda e, ctc=ctc: e.tensor_copy(
                bcC[:, :], pd[0:128, 0:512]),
                wait=[("mmt", ctc), ("v", p.val("v"))], inc="v")
            cab_v = p.chain("vector", [
                (lambda e, dt_=dt_, n=n: e.tensor_scalar(
                    ab8[:, dt_, :], delta[:, dt_, :],
                    Asb[:, dt_, n:n + 1], None, ALU.mult))
                for dt_ in range(8)],
                wait=[("v", max(delta_done, cAneg, p.val("v")))], inc="v")
            cab = p.do("scalar", lambda e: e.activation(
                ab8[:, :, :], ab8[:, :, :], AF.Exp),
                wait=[("v", cab_v), ("s", p.val("s"))], inc="s")
            for dt_ in range(8):
                p.chain("vector", [
                    (lambda e, dt_=dt_: e.tensor_mul(
                        bx[:, 0, :], ubuf[:, dt_, :], bcB[:, :])),
                    (lambda e, dt_=dt_, n=n: e.tensor_tensor_scan(
                        hseq[:, :], ab8[:, dt_, :], bx[:, 0, :],
                        state[:, dt_, n:n + 1], ALU.mult, ALU.add)),
                    (lambda e, dt_=dt_, n=n: e.tensor_copy(
                        state[:, dt_, n:n + 1], hseq[:, CT - 1:CT])),
                    (lambda e: e.tensor_mul(
                        hseq[:, :], hseq[:, :], bcC[:, :])),
                    (lambda e, dt_=dt_: e.tensor_add(
                        ybuf[:, dt_, :], ybuf[:, dt_, :], hseq[:, :])),
                ], wait=[("s", cab), ("v", p.val("v"))], inc="v",
                    linked=True)

        for dt_ in range(8):
            p.chain("vector", [
                (lambda e, dt_=dt_: e.scalar_tensor_tensor(
                    ybuf[:, dt_, :], xnT[:, dt_, :], DpT[:, dt_:dt_ + 1],
                    ybuf[:, dt_, :], ALU.mult, ALU.add)),
                (lambda e, dt_=dt_: e.tensor_copy(
                    yb[:, dt_, :], ybuf[:, dt_, :])),
            ], wait=[("v", p.val("v")), ("mmo", ymm_done)], inc="v")
        yb_done = p.val("v")

        for ttl in range(CT // 128):
            tt = ch * (CT // 128) + ttl
            fns = []
            for dc in range(2):
                bank = pc if dc == 0 else pd
                for dt_ in range(8):
                    fns.append(lambda e, ttl=ttl, dc=dc, dt_=dt_,
                               bank=bank: e.matmul(
                                   bank[0:128, 0:512],
                                   yb[:, dt_, ttl * 128:(ttl + 1) * 128],
                                   wosb[:, dt_, dc * 512:(dc + 1) * 512],
                                   start=(dt_ == 0), stop=False))
                fns.append(lambda e, dc=dc, bank=bank: e.matmul(
                    bank[0:128, 0:512], onesb[0:1, :],
                    bosb[0:1, dc * 512:(dc + 1) * 512],
                    start=False, stop=True))
            cmo = p.chain("tensor", fns,
                          wait=[("v", yb_done), ("wld", wld0),
                                ("vG", p.val("vG"))], inc="mmo")
            cl2 = p.do("sync", lambda e, tt=tt: e.dma_start(
                out=xt[:, :], in_=xcur[tt * 128:(tt + 1) * 128, :]),
                wait=[("vG", p.val("vG")), ("v", xn_counts[-1])],
                inc="ld2", amt=16)
            cv1 = p.do("vector", lambda e, cmo=cmo, cl2=cl2: e.tensor_add(
                scr[:, 0:512], xt[:, 0:512], pc[0:128, 0:512]),
                wait=[("mmo", cmo), ("ld2", cl2), ("st", p.val("st")),
                      ("v", p.val("v"))], inc="vG")
            cv2 = p.do("vector", lambda e: e.tensor_add(
                scr[:, 512:1024], xt[:, 512:1024], pd[0:128, 0:512]),
                wait=[("vG", cv1)], inc="vG")
            p.do("sync", lambda e, tt=tt: e.dma_start(
                out=xcur[tt * 128:(tt + 1) * 128, :], in_=scr[:, :]),
                wait=[("vG", cv2)], inc="st", amt=16)
        ymm_done = p.val("mmo")


def _emit_ffn0(p, nc, ctx, C, W, xcur, xn_hbm, myout, tag="f0"):
    T, E, H = 2048, 4, 4096
    NTT, NS, GS = 16, 32, 8
    NG = NS // GS
    sb = lambda n, sh, dt: ctx.enter_context(nc.sbuf_tensor(n, sh, dt))
    xt, scr, stats = C["xt"], C["scr"], C["stats"]
    identf, consts, onesb = C["identf"], C["consts"], C["onesb"]
    pa, pb = C["pa"], C["pb"]
    pO = [C["pc"], C["pd"], C["pe_"], C["pf_"]]

    xnT = sb(tag + "_xnT", [128, 8, T], BF16)
    hts = sb(tag + "_hts", [128, GS, T], BF16)
    acc = sb(tag + "_acc", [128, NTT, 1024], F32)
    w2g = sb(tag + "_w2gs", [128, GS, 1024], BF16)
    w1s = sb(tag + "_w1st", [128, 2, 8, 128], BF16)
    wgsb = sb(tag + "_wgs", [128, 8, 8], BF16)
    b1nat = sb(tag + "_b1ns", [E, H], F32)
    b1T = sb(tag + "_b1T", [128, NS, E], F32)
    b2sb = sb(tag + "_b2s", [E, 1024], BF16)
    mw = sb(tag + "_mw", [128, NTT, 8], F32)
    mwTb = sb(tag + "_mwT", [8, T], BF16)
    gatef = sb(tag + "_gf", [128, 8], F32)
    m8g = sb(tag + "_m8", [128, 8], F32)
    gcol = sb(tag + "_gc", [128, 8], F32)
    eg = sb(tag + "_eg", [128, 8], F32)

    p.do("gpsimd", lambda e: e.dma_start(
        out=wgsb[:, :, :], in_=W["wg"].rearrange("(kt kp) f -> kp kt f", kp=128)),
        inc="wld", amt=16)
    p.do("gpsimd", lambda e: e.dma_start(out=b2sb[:, :], in_=W["b2"]),
         inc="wld", amt=16)
    wld0 = p.val("wld")
    cb1 = p.do("sync", lambda e: e.dma_start(out=b1nat[:, :], in_=W["b1"]),
               inc="wldh", amt=16)
    for s_ in range(NS):
        bank = pa if s_ % 2 == 0 else pb
        ct = p.do("tensor", lambda e, s_=s_, bank=bank: e.transpose(
            bank[0:128, 0:E], b1nat[0:E, s_ * 128:(s_ + 1) * 128],
            identf[0:E, 0:E]),
            wait=[("wldh", cb1), ("g", C["ident_done"]), ("v", p.val("v")),
                  ("mmt", p.val("mmt"))], inc="mmt")
        p.do("vector", lambda e, s_=s_, ct=ct, bank=bank: e.tensor_copy(
            b1T[:, s_, :], bank[0:128, 0:E]), wait=[("mmt", ct)], inc="v")
    b1T_done_v = p.val("v")

    for hoff in (0, 2048):
        for dt_ in range(8):
            p.do("sync", lambda e, dt_=dt_, hoff=hoff: e.dma_start(
                out=xnT[:, dt_, :],
                in_=xn_hbm[hoff:hoff + T, dt_ * 128:(dt_ + 1) * 128],
                transpose=True),
                wait=[("xnst", C["xnst_cur"]), ("mmh", p.val("mmh")),
                      ("mmt", p.val("mmt")), ("mmo", p.val("mmo"))],
                inc="xnT", amt=16)
        xnT_all = p.val("xnT")

        for tt in range(NTT):
            ct = p.chain("tensor", [
                (lambda e, kt=kt, tt=tt: e.matmul(
                    pa[0:128, 0:8], xnT[:, kt, tt * 128:(tt + 1) * 128],
                    wgsb[:, kt, :], start=(kt == 0), stop=(kt == 7)))
                for kt in range(8)],
                wait=[("xnT", xnT_all), ("wld", wld0), ("v", p.val("v"))],
                inc="mmt")
            cn = p.chain("vector", [
                (lambda e: e.tensor_copy(gatef[:, :], pa[0:128, 0:8])),
                (lambda e: e.max(out=m8g[:, :], in_=gatef[:, :])),
                (lambda e: e.tensor_scalar(
                    gcol[:, 0:1], m8g[:, 0:1], -1.0, None, ALU.mult)),
            ], wait=[("mmt", ct), ("v", p.val("v"))], inc="v", linked=True)
            ce = p.do("scalar", lambda e: e.activation(
                eg[:, :], gatef[:, :], AF.Exp, bias=gcol[:, 0:1]),
                wait=[("v", cn), ("s", p.val("s"))], inc="s")
            p.chain("vector", [
                (lambda e: e.scalar_tensor_tensor(
                    eg[:, :], gatef[:, :], m8g[:, 1:2], eg[:, :],
                    ALU.is_ge, ALU.mult, accum_out=gcol[:, 1:2])),
                (lambda e: e.tensor_scalar(
                    gcol[:, 2:3], gcol[:, 1:2], 8.0, None, ALU.mult)),
                (lambda e: e.reciprocal(gcol[:, 3:4], gcol[:, 2:3])),
                (lambda e, tt=tt: e.tensor_scalar(
                    mw[:, tt, :], eg[:, :], gcol[:, 3:4], None, ALU.mult)),
            ], wait=[("s", ce), ("v", p.val("v"))], inc="v", linked=True)
        mw_done = p.val("v")

        prev_ev = p.val("v")
        for tt in range(NTT):
            ct = p.do("tensor", lambda e, tt=tt: e.transpose(
                pb[0:8, 0:128], mw[:, tt, :], identf[:, :]),
                wait=[("v", max(mw_done, prev_ev))], inc="mmt")
            prev_ev = p.do("vector", lambda e, tt=tt: e.tensor_copy(
                mwTb[0:8, tt * 128:(tt + 1) * 128], pb[0:8, 0:128]),
                wait=[("mmt", ct), ("v", p.val("v"))], inc="v")
        mwT_done = prev_ev
        for tt in range(NTT):
            ct = p.chain("tensor", [
                (lambda e, tt=tt, dc=dc: e.matmul(
                    pO[dc][0:128, 0:512],
                    mwTb[0:E, tt * 128:(tt + 1) * 128],
                    b2sb[0:E, dc * 512:(dc + 1) * 512],
                    start=True, stop=True))
                for dc in range(2)],
                wait=[("v", mwT_done), ("wld", wld0),
                      ("vacc", p.val("vacc"))], inc="mmt")
            p.chain("vector", [
                (lambda e, tt=tt, dc=dc: e.tensor_copy(
                    acc[:, tt, dc * 512:(dc + 1) * 512],
                    pO[dc][0:128, 0:512]))
                for dc in range(2)],
                wait=[("mmt", ct), ("v", p.val("v"))], inc="vacc", amt=2)
        acc_init = p.val("vacc")

        ph_evac = {}
        ph_iter = 0
        hts_free_at = p.val("vacc")
        w2_free_at = p.val("mmo")
        for e_i in range(E):
            for g in range(NG):
                cw2 = p.do("gpsimd", lambda en, e_i=e_i, g=g: en.dma_start(
                    out=w2g[:, :, :],
                    in_=W["w2"][e_i, g * (GS * 128):(g + 1) * (GS * 128), :]
                    .rearrange("(st hp) d -> hp st d", hp=128)),
                    wait=[("mmo", w2_free_at)], inc="w2ld", amt=16)
                for s_l in range(GS):
                    s_ = g * GS + s_l
                    cw1 = p.do("gpsimd", lambda en, e_i=e_i, s_=s_: en.dma_start(
                        out=w1s[:, s_ % 2, :, :],
                        in_=W["w1"][e_i, :, s_ * 128:(s_ + 1) * 128]
                        .rearrange("(kt kp) f -> kp kt f", kp=128)),
                        wait=[("mmh", max(0, p.val("mmh") - 4))],
                        inc="w1ld", amt=16)
                    for blk in range(4):
                        bank = pa if ph_iter % 2 == 0 else pb
                        wv = [("sh", ph_evac.get(ph_iter - 2, 0))]
                        cmh = p.chain("tensor", [
                            (lambda en, kt=kt, s_=s_, blk=blk, bank=bank:
                             en.matmul(
                                 bank[0:128, 0:512],
                                 w1s[:, s_ % 2, kt, :],
                                 xnT[:, kt, blk * 512:(blk + 1) * 512],
                                 start=(kt == 0), stop=(kt == 7)))
                            for kt in range(8)],
                            wait=[("w1ld", cw1), ("xnT", xnT_all),
                                  ("vacc", hts_free_at)] + wv, inc="mmh")
                        ce = p.do("vector", lambda en, s_=s_, s_l=s_l,
                                  blk=blk, e_i=e_i, bank=bank: en.tensor_scalar(
                                      hts[:, s_l, blk * 512:(blk + 1) * 512],
                                      bank[0:128, 0:512],
                                      b1T[:, s_, e_i:e_i + 1], 0.0,
                                      ALU.add, ALU.max),
                                  wait=[("mmh", cmh), ("v", p.val("v"))],
                                  inc="sh")
                        ph_evac[ph_iter] = ce
                        ph_iter += 1
                f1_done = p.val("sh")
                for tb in range(8):
                    fns = []
                    for s_l in range(GS):
                        for sub in range(2):
                            for dc in range(2):
                                fns.append(
                                    lambda en, s_l=s_l, tb=tb, sub=sub,
                                    dc=dc, bi=sub * 2 + dc: en.matmul(
                                        pO[bi][0:128, 0:512],
                                        hts[:, s_l,
                                            tb * 256 + sub * 128:
                                            tb * 256 + sub * 128 + 128],
                                        w2g[:, s_l,
                                            dc * 512:(dc + 1) * 512],
                                        start=(s_l == 0),
                                        stop=(s_l == GS - 1)))
                    cmo = p.chain("tensor", fns,
                                  wait=[("sh", f1_done), ("w2ld", cw2),
                                        ("vacc", p.val("vacc"))],
                                  inc="mmo")
                    p.chain("vector", [
                        (lambda en, bi=sub * 2 + dc, tt_i=tb * 2 + sub,
                         dc=dc, e_i=e_i: en.scalar_tensor_tensor(
                             acc[:, tt_i, dc * 512:(dc + 1) * 512],
                             pO[bi][0:128, 0:512],
                             mw[:, tt_i, e_i:e_i + 1],
                             acc[:, tt_i, dc * 512:(dc + 1) * 512],
                             ALU.mult, ALU.add))
                        for sub in range(2) for dc in range(2)],
                        wait=[("mmo", cmo), ("v", mw_done),
                              ("vacc", p.val("vacc"))], inc="vacc", amt=4)
                hts_free_at = p.val("vacc")
                w2_free_at = p.val("mmo")
        acc_done = p.val("vacc")

        for tt in range(NTT):
            p.do("sync", lambda e, tt=tt, hoff=hoff: e.dma_start(
                out=myout[hoff + tt * 128:hoff + (tt + 1) * 128, :],
                in_=acc[:, tt, :]),
                wait=[("vacc", acc_done)], inc="st3", amt=16)
        p.wait_only("vector", [("st3", p.val("st3"))])


def build_fused(upto=99, skip=()):
    nc = bass.Bass("TRN2", target_bir_lowering=False)
    T = 4096

    x_in = nc.dram_tensor("x", [T, D], F32, kind="ExternalInput").ap()
    y_out = nc.dram_tensor("y", [T, D], F32, kind="ExternalOutput").ap()
    WIN = {}
    def di(name, shape):
        WIN[name] = nc.dram_tensor(name, shape, F32, kind="ExternalInput").ap()
        return WIN[name]
    di("m1_al", [D, 16]); di("m1_dp", [1, D]); di("m1_wxp", [D, 1056])
    di("m1_bxp", [1, 1056]); di("m1_wo", [D, D]); di("m1_bo", [1, D])
    di("f0_wg", [D, 8]); di("f0_w1", [4, D, 4096]); di("f0_b1", [4, 4096])
    di("f0_w2", [4, 4096, D]); di("f0_b2", [4, D])
    di("a0_wq", [D, D]); di("a0_wkvk", [D, D]); di("a0_wkvv", [D, 512])
    di("a0_wo", [512, D]); di("a0_bo", [1, D]); di("a0_pre", [16, 8])
    di("a0_memkT", [D, 16]); di("a0_memvJ", [16, 512])
    di("m2_win", [4096, D]); di("m2_al", [2048, 16]); di("m2_wb", [16, 2048])
    di("m2_wc", [16, 2048]); di("m2_wdt", [2048, 2048])
    di("m2_bdt", [1, 2048]); di("m2_dd", [1, 2048]); di("m2_og", [1, 1])
    di("m2_wo", [D, 2048])
    di("f1_rw", [D, 64]); di("f1_rb", [1, 64])
    di("f1_w1", [32, D, 256]); di("f1_b1", [32, 256])
    di("f1_w2", [32, 256, D]); di("f1_b2", [32, D])
    di("f1_sw1", [1, D, 2048]); di("f1_sb1", [1, 2048])
    di("f1_sw2", [1, 2048, D]); di("f1_sb2", [1, D])
    di("a1_wq", [512, D]); di("a1_wk", [512, D]); di("a1_wv", [512, D])
    di("a1_wo", [D, 512]); di("a1_bo", [1, D])
    xcur = nc.dram_tensor("xcur", [T, D], F32).ap()
    xn_hbm = nc.dram_tensor("xn_hbm", [T, D], BF16).ap()
    myout = nc.dram_tensor("myout", [T, D], F32).ap()
    arout = nc.dram_tensor("arout", [T, D], F32).ap()
    DR = dict(
        k_hbm=nc.dram_tensor("k_hbm", [D, 4112], BF16).ap(),
        v_hbm=nc.dram_tensor("v_hbm", [4112, 512], BF16).ap(),
        av_hbm=nc.dram_tensor("av_hbm", [512, T], BF16).ap(),
        q_hbm=nc.dram_tensor("q_hbm", [D, T], BF16).ap(),
    )
    DR3 = dict(
        v_hbm1=nc.dram_tensor("a1_v_hbm", [T, 512], BF16).ap(),
        wq_bf=nc.dram_tensor("a1_wq_bf", [512, D], BF16).ap(),
        wk_bf=nc.dram_tensor("a1_wk_bf", [512, D], BF16).ap(),
        wv_bf=nc.dram_tensor("a1_wv_bf", [512, D], BF16).ap(),
        wo_bf=nc.dram_tensor("a1_wo_bf", [D, 512], BF16).ap(),
        wqT=nc.dram_tensor("a1_wqT", [D, 512], BF16).ap(),
        wkT=nc.dram_tensor("a1_wkT", [D, 512], BF16).ap(),
        wvT=nc.dram_tensor("a1_wvT", [D, 512], BF16).ap(),
        woT=nc.dram_tensor("a1_woT", [512, D], BF16).ap(),
    )
    DR2 = dict(
        win_bf=nc.dram_tensor("m2_win_bf", [4096, D], BF16).ap(),
        wdt_bf=nc.dram_tensor("m2_wdt_bf", [2048, 2048], BF16).ap(),
        wo_bf=nc.dram_tensor("m2_wo_bf", [D, 2048], BF16).ap(),
        wb_bf=nc.dram_tensor("m2_wb_bf", [16, 2048], BF16).ap(),
        wc_bf=nc.dram_tensor("m2_wc_bf", [16, 2048], BF16).ap(),
        winT_hbm=nc.dram_tensor("m2_winT_hbm", [D, 4096], BF16).ap(),
        wdtT_hbm=nc.dram_tensor("m2_wdtT_hbm", [2048, 2048], BF16).ap(),
        woT_hbm=nc.dram_tensor("m2_woT_hbm", [2048, D], BF16).ap(),
    )

    p = Prog(nc)
    global _SEQ_MARKS
    _SEQ_MARKS = [("init", 0)]
    mark = lambda nm: _SEQ_MARKS.append((nm, nc.n_instructions()))
    from contextlib import ExitStack
    with ExitStack() as ctx0:
        sb = lambda n, sh, dt: ctx0.enter_context(nc.sbuf_tensor(n, sh, dt))
        ps = lambda n, sh, dt=F32: ctx0.enter_context(nc.psum_tensor(n, sh, dt))
        C = {}
        C["xt"] = sb("xt", [128, 1024], F32)
        C["xn"] = sb("xn", [128, 1024], BF16)
        C["scr"] = sb("scr", [128, 1024], F32)
        C["scr2"] = sb("scr2", [128, 1024], F32)
        C["stats"] = sb("stats", [128, 8], F32)
        C["cols"] = sb("cols", [128, 16], F32)
        C["consts"] = sb("consts", [128, 6], F32)
        C["identf"] = sb("identf", [128, 128], F32)
        C["identb"] = sb("identb", [128, 128], BF16)
        C["ones128"] = sb("ones128", [128, 128], F32)
        C["onesb"] = sb("onesb", [1, 128], BF16)
        C["ones32"] = sb("ones32", [32, 128], F32)
        C["sel32"] = sb("sel32", [32, 128], F32)
        C["m8"] = sb("m8c", [128, 8], F32)
        C["pa"] = ps("pa", [128, 512]); C["pb"] = ps("pb", [128, 512])
        C["pc"] = ps("pc", [128, 512]); C["pd"] = ps("pd", [128, 512])
        C["pe_"] = ps("pe_", [128, 512]); C["pf_"] = ps("pf_", [128, 512])
        C["pt1"] = ps("pt1", [128, 128], BF16)
        C["pt2"] = ps("pt2", [128, 128], BF16)

        _ident_f32(p, C["identf"], C["ones128"])
        C["ident_done"] = p.val("g")
        ccols = _emit_consts(p, C["consts"],
                             [1e-5, -0.5, 1e-9, -1e30, 0.125])
        (C["eps"], C["nhalf"], C["epsln"], C["neg"],
         C["c0125"]) = ccols
        cid = p.do("vector", lambda e: e.tensor_copy(
            C["identb"][:, :], C["identf"][:, :]),
            wait=[("g", C["ident_done"])], inc="v")
        C["cid"] = cid
        p.do("vector", lambda e: e.memset(C["onesb"][:, :], 1.0), inc="v")
        C["c32"] = p.do("vector", lambda e: e.memset(C["ones32"][:, :], 1.0),
                        inc="v")
        p.do("vector", lambda e: e.memset(C["cols"][:, :], 0.0), inc="v")
        C["cconst"] = p.val("v")

        for tt in range(T // 128):
            cl = p.do("sync", lambda e, tt=tt: e.dma_start(
                out=C["xt"][:, :], in_=x_in[tt * 128:(tt + 1) * 128, :]),
                wait=[("ld", p.val("ld"))], inc="ld", amt=16)
            cc = p.do("vector", lambda e: e.tensor_copy(
                C["scr"][:, :], C["xt"][:, :]),
                wait=[("ld", cl), ("v", p.val("v")), ("st4", p.val("st4"))],
                inc="v")
            p.do("sync", lambda e, tt=tt: e.dma_start(
                out=xcur[tt * 128:(tt + 1) * 128, :], in_=C["scr"][:, :]),
                wait=[("v", cc)], inc="st4", amt=16)
        _barrier(p, dma_sems=("st4",))

        seq = 0
        if seq <= upto and seq not in skip:
            with ExitStack() as ctx:
                M1W = dict(al=WIN["m1_al"], dp=WIN["m1_dp"], wxp=WIN["m1_wxp"],
                           bxp=WIN["m1_bxp"], wo=WIN["m1_wo"], bo=WIN["m1_bo"])
                _emit_mamba1(p, nc, ctx, C, M1W, xcur, xn_hbm)
                _barrier(p, dma_sems=("st", "xnst"))
        mark("m1")
        seq += 1
        if seq <= upto and seq not in skip:
            C["xnst_cur"] = _emit_ln_all(p, nc, C, xcur, xn_hbm, T)
            with ExitStack() as ctx:
                F0W = dict(wg=WIN["f0_wg"], w1=WIN["f0_w1"], b1=WIN["f0_b1"],
                           w2=WIN["f0_w2"], b2=WIN["f0_b2"])
                _emit_ffn0(p, nc, ctx, C, F0W, xcur, xn_hbm, myout)
                _barrier(p, dma_sems=("st3",))
            _emit_ar_add(p, nc, C, myout, arout, xcur, "f0a")
            _barrier(p, dma_sems=("st4",))
        mark("f0a")
        seq += 1
        if seq <= upto and seq not in skip:
            C["xnst_cur"] = _emit_ln_all(p, nc, C, xcur, xn_hbm, T)
            with ExitStack() as ctx:
                A0W = dict(wq=WIN["a0_wq"], wkvk=WIN["a0_wkvk"],
                           wkvv=WIN["a0_wkvv"], wo=WIN["a0_wo"],
                           bo=WIN["a0_bo"], pre=WIN["a0_pre"],
                           memkT=WIN["a0_memkT"], memvJ=WIN["a0_memvJ"])
                _emit_attnb0(p, nc, ctx, C, A0W, xn_hbm, myout, DR)
                _barrier(p, dma_sems=("st3", "st2", "st1", "wg1", "wg2",
                                      "avl", "wld"))
            _emit_ar_add(p, nc, C, myout, arout, xcur, "a0")
            _barrier(p, dma_sems=("st4",))
        mark("a0")
        seq += 1
        if seq <= upto and seq not in skip:
            C["xnst_cur"] = _emit_ln_all(p, nc, C, xcur, xn_hbm, T)
            with ExitStack() as ctx:
                F0W = dict(wg=WIN["f0_wg"], w1=WIN["f0_w1"], b1=WIN["f0_b1"],
                           w2=WIN["f0_w2"], b2=WIN["f0_b2"])
                _emit_ffn0(p, nc, ctx, C, F0W, xcur, xn_hbm, myout, tag="f0b")
                _barrier(p, dma_sems=("st3",))
            _emit_ar_add(p, nc, C, myout, arout, xcur, "f0b")
            _barrier(p, dma_sems=("st4",))
        mark("f0b")
        seq += 1
        if seq <= upto and seq not in skip:
            with ExitStack() as ctx:
                M2W = dict(win=WIN["m2_win"], al=WIN["m2_al"],
                           wb=WIN["m2_wb"], wc=WIN["m2_wc"],
                           wdt=WIN["m2_wdt"], bdt=WIN["m2_bdt"],
                           dd=WIN["m2_dd"], og=WIN["m2_og"],
                           wo=WIN["m2_wo"])
                _emit_mamba2(p, nc, ctx, C, M2W, xcur, xn_hbm, DR2)
                _barrier(p, dma_sems=("st4", "xnst", "wld", "wldh", "st0",
                                      "trl", "wg1", "wg2", "ld", "ld2"))
        mark("m2")
        seq += 1
        F1W = dict(rw=WIN["f1_rw"], rb=WIN["f1_rb"], w1=WIN["f1_w1"],
                   b1=WIN["f1_b1"], w2=WIN["f1_w2"], b2=WIN["f1_b2"],
                   sw1=WIN["f1_sw1"], sb1=WIN["f1_sb1"], sw2=WIN["f1_sw2"],
                   sb2=WIN["f1_sb2"])
        if seq <= upto and seq not in skip:
            C["xnst_cur"] = _emit_ln_all(p, nc, C, xcur, xn_hbm, T)
            with ExitStack() as ctx:
                _emit_ffn1(p, nc, ctx, C, F1W, xn_hbm, myout, tag="f1a")
                _barrier(p, dma_sems=("st3", "wld", "wldh", "wg1", "wg2"))
            _emit_ar_add(p, nc, C, myout, arout, xcur, "f1a")
            _barrier(p, dma_sems=("st4",))
        mark("f1a")
        seq += 1
        if seq <= upto and seq not in skip:
            C["xnst_cur"] = _emit_ln_all(p, nc, C, xcur, xn_hbm, T)
            with ExitStack() as ctx:
                A1W = dict(wq=WIN["a1_wq"], wk=WIN["a1_wk"], wv=WIN["a1_wv"],
                           wo=WIN["a1_wo"], bo=WIN["a1_bo"])
                _emit_attnb1(p, nc, ctx, C, A1W, xn_hbm, myout, DR, DR3)
                _barrier(p, dma_sems=("st3", "st2", "st1", "st0", "trl",
                                      "wg1", "wg2", "avl", "wld"))
            _emit_ar_add(p, nc, C, myout, arout, xcur, "a1")
            _barrier(p, dma_sems=("st4",))
        mark("a1")
        seq += 1
        if seq <= upto and seq not in skip:
            C["xnst_cur"] = _emit_ln_all(p, nc, C, xcur, xn_hbm, T)
            with ExitStack() as ctx:
                _emit_ffn1(p, nc, ctx, C, F1W, xn_hbm, myout, tag="f1b")
                _barrier(p, dma_sems=("st3", "wld", "wldh", "wg1", "wg2"))
            _emit_ar_add(p, nc, C, myout, arout, xcur, "f1b")
            _barrier(p, dma_sems=("st4",))
        mark("f1b")
        seq += 1

        for tt in range(T // 128):
            cl = p.do("sync", lambda e, tt=tt: e.dma_start(
                out=C["xt"][:, :], in_=xcur[tt * 128:(tt + 1) * 128, :]),
                wait=[("ld2", p.val("ld2")), ("vG", p.val("vG"))],
                inc="ld2", amt=16)
            cc = p.do("vector", lambda e: e.tensor_copy(
                C["scr"][:, :], C["xt"][:, :]),
                wait=[("ld2", cl), ("v", p.val("v")), ("st", p.val("st")),
                      ("vG", p.val("vG"))], inc="vG")
            p.do("sync", lambda e, tt=tt: e.dma_start(
                out=y_out[tt * 128:(tt + 1) * 128, :], in_=C["scr"][:, :]),
                wait=[("vG", cc)], inc="st", amt=16)
        mark("fin")
        p.wait_only("gpsimd", [("st", p.val("st"))])
        p.wait_only("sync", [("st", p.val("st"))])
        p.finish()
    return nc


def _fused_in_maps(x, inputs, upto=99):
    maps = []
    for c in range(NCORES):
        b, half = c // 2, c % 2
        m = dict(x=np.ascontiguousarray(x[b * 4096:(b + 1) * 4096]))
        m["m1_al"] = np.ascontiguousarray(inputs["L0_attn_a_A_log"])
        m["m1_dp"] = np.ascontiguousarray(inputs["L0_attn_a_Dp"]).reshape(1, D)
        m["m1_wxp"] = np.ascontiguousarray(inputs["L0_attn_a_W_xproj"])
        m["m1_bxp"] = np.ascontiguousarray(
            inputs["L0_attn_a_b_xproj"]).reshape(1, 1056)
        m["m1_wo"] = np.ascontiguousarray(inputs["L0_attn_a_W_out"])
        m["m1_bo"] = np.ascontiguousarray(inputs["L0_attn_a_b_out"]).reshape(1, D)
        eperm = np.r_[np.arange(half * 4, half * 4 + 4),
                      np.arange((1 - half) * 4, (1 - half) * 4 + 4)]
        m["f0_wg"] = np.ascontiguousarray(inputs["L0_ffn_W_gate"][:, eperm])
        mine = eperm[:4]
        m["f0_w1"] = np.ascontiguousarray(inputs["L0_ffn_W1"][mine])
        m["f0_b1"] = np.ascontiguousarray(inputs["L0_ffn_b1"][mine])
        m["f0_w2"] = np.ascontiguousarray(inputs["L0_ffn_W2"][mine])
        m["f0_b2"] = np.ascontiguousarray(inputs["L0_ffn_b2"][mine])
        myk = np.arange(half * 8, half * 8 + 8)
        vcols = (myk[:, None] * 64 + np.arange(64)[None, :]).reshape(-1)
        m["a0_wq"] = np.ascontiguousarray(inputs["L0_attn_b_Wq"])
        m["a0_wkvk"] = np.ascontiguousarray(inputs["L0_attn_b_Wkv"][:, 0:D])
        m["a0_wkvv"] = np.ascontiguousarray(
            inputs["L0_attn_b_Wkv"][:, D:][:, vcols])
        m["a0_wo"] = np.ascontiguousarray(inputs["L0_attn_b_Wo"][vcols, :])
        bo = inputs["L0_attn_b_bo"].reshape(1, D)
        m["a0_bo"] = np.ascontiguousarray(bo if half == 0
                                          else np.zeros_like(bo))
        m["a0_pre"] = np.ascontiguousarray(inputs["L0_attn_b_pre_proj"][:, myk])
        memk = np.ascontiguousarray(
            inputs["L0_attn_b_mem_k"].transpose(0, 2, 1).reshape(D, 16))
        memv = np.ascontiguousarray(
            inputs["L0_attn_b_mem_v"].transpose(1, 0, 2).reshape(16, D))
        m["a0_memkT"] = memk
        m["a0_memvJ"] = np.ascontiguousarray(memv[:, vcols])
        m["m2_win"] = np.ascontiguousarray(inputs["L1_attn_a_W_in"])
        m["m2_al"] = np.ascontiguousarray(inputs["L1_attn_a_A_log"])
        m["m2_wb"] = np.ascontiguousarray(inputs["L1_attn_a_W_B"])
        m["m2_wc"] = np.ascontiguousarray(inputs["L1_attn_a_W_C"])
        m["m2_wdt"] = np.ascontiguousarray(inputs["L1_attn_a_W_dt"])
        m["m2_bdt"] = np.ascontiguousarray(
            inputs["L1_attn_a_b_dt"]).reshape(1, 2048)
        m["m2_dd"] = np.ascontiguousarray(
            inputs["L1_attn_a_D"]).reshape(1, 2048)
        m["m2_og"] = np.ascontiguousarray(
            inputs["L1_attn_a_output_gate"]).reshape(1, 1)
        m["m2_wo"] = np.ascontiguousarray(inputs["L1_attn_a_W_out"])
        e1perm = np.r_[np.arange(half * 32, half * 32 + 32),
                       np.arange((1 - half) * 32, (1 - half) * 32 + 32)]
        mine1 = e1perm[:32]
        m["f1_rw"] = np.ascontiguousarray(
            inputs["L1_ffn_router_w"][:, e1perm])
        m["f1_rb"] = np.ascontiguousarray(
            inputs["L1_ffn_router_b"].reshape(1, 64)[:, e1perm])
        m["f1_w1"] = np.ascontiguousarray(inputs["L1_ffn_w1"][mine1])
        m["f1_b1"] = np.ascontiguousarray(inputs["L1_ffn_b1"][mine1])
        m["f1_w2"] = np.ascontiguousarray(inputs["L1_ffn_w2"][mine1])
        m["f1_b2"] = np.ascontiguousarray(inputs["L1_ffn_b2"][mine1])
        m["f1_sw1"] = np.ascontiguousarray(inputs["L1_ffn_sw1"][half:half + 1])
        m["f1_sb1"] = np.ascontiguousarray(
            inputs["L1_ffn_sb1"][half:half + 1])
        m["f1_sw2"] = np.ascontiguousarray(inputs["L1_ffn_sw2"][half:half + 1])
        m["f1_sb2"] = np.ascontiguousarray(
            inputs["L1_ffn_sb2"][half:half + 1])
        hr = slice(half * 512, (half + 1) * 512)
        m["a1_wq"] = np.ascontiguousarray(inputs["L1_attn_b_Wq"][hr, :])
        m["a1_wk"] = np.ascontiguousarray(inputs["L1_attn_b_Wk"][hr, :])
        m["a1_wv"] = np.ascontiguousarray(inputs["L1_attn_b_Wv"][hr, :])
        m["a1_wo"] = np.ascontiguousarray(inputs["L1_attn_b_Wo"][:, hr])
        bo1 = inputs["L1_attn_b_bo"].reshape(1, D)
        m["a1_bo"] = np.ascontiguousarray(bo1 if half == 0
                                          else np.zeros_like(bo1))
        maps.append(m)
    return maps


def run_fused(x, inputs, upto=99):
    nc = _get(f"fused{upto}", lambda: build_fused(upto))
    r = run_bass_kernel_spmd(nc, _fused_in_maps(x, inputs, upto),
                             core_ids=list(range(NCORES)))
    out = np.empty_like(x)
    for b in range(4):
        out[b * 4096:(b + 1) * 4096] = r.results[2 * b]["y"]
    return out


def _emit_attnb0(p, nc, ctx, C, W, xn_hbm, myout, DR):
    TB, NK, M = 4096, 8, 16
    JALL = TB + M
    sb = lambda n, sh, dt: ctx.enter_context(nc.sbuf_tensor(n, sh, dt))
    xt, scr, stats, cols, m8 = C["xt"], C["scr"], C["stats"], C["cols"], C["m8"]
    identb, consts, onesb = C["identb"], C["consts"], C["onesb"]
    pa, pb, pt1, pt2 = C["pa"], C["pb"], C["pt1"], C["pt2"]
    pav = C["pe_"]
    k_hbm, v_hbm, av_hbm, q_hbm = (DR["k_hbm"], DR["v_hbm"], DR["av_hbm"],
                                   DR["q_hbm"])

    Khat = sb("a0_Khat", [128, 8, JALL], BF16)
    qTh = sb("a0_qTh", [128, 8, 2048], BF16)
    ubig = sb("a0_ubig", [128, 32896], BF16)
    jidx = sb("a0_jidx", [128, JALL], F32)
    jlall = sb("a0_jlall", [128, 32], F32)
    vk = sb("a0_vk", [128, 33, 64], BF16)
    sT = sb("a0_sT", [128, 8, NK], F32)
    presb = sb("a0_presb", [2, 8, NK], BF16)
    UT = sb("a0_UT", [2, 128], BF16)
    it2 = sb("a0_it2", [2, 128], F32)
    it2b = sb("a0_it2b", [2, 128], F32)
    avT = sb("a0_avT", [128, 4, 128], BF16)
    avsb = sb("a0_avsb", [64, 128], BF16)
    pTsb1 = sb("a0_pT1", [128, 128], BF16)
    pTsb2 = sb("a0_pT2", [128, 128], BF16)
    kstage = sb("a0_kst", [128, 512], BF16)
    bosb = sb("a0_bos", [1, 1024], BF16)

    dots2 = [ubig[:, 0:8224].bitcast(F32),
             ubig[:, 8224:16448].bitcast(F32)]
    wz = ubig[:, 16448:24672].bitcast(F32)
    wzi = ubig[:, 16448:24672].bitcast(I32)
    p2b2 = [ubig[:, 24672:28784], ubig[:, 28784:32896]]
    xnTs = ubig[:, 0:4096].rearrange("p (kt f) -> p kt f", kt=8)
    wkvK = ubig[:, 4096:12288].rearrange("p (kt f) -> p kt f", kt=8)
    wkvV = ubig[:, 12288:16384].rearrange("p (kt f) -> p kt f", kt=8)
    wqR = ubig[:, 16384:24576].rearrange("p (kt f) -> p kt f", kt=8)
    woR = ubig[:, 0:4096].rearrange("p (kt f) -> p kt f", kt=4)

    p.do("gpsimd", lambda e: e.iota(
        it2[:, :], [[1, 128]], base=0, channel_multiplier=-64,
        allow_small_or_imprecise_dtypes=True), inc="g")
    p.do("gpsimd", lambda e: e.iota(
        it2b[:, :], [[-1, 128]], base=63, channel_multiplier=64,
        allow_small_or_imprecise_dtypes=True), inc="g")
    cji = p.do("gpsimd", lambda e: e.iota(
        jidx[:, :], [[1, JALL]], base=0, channel_multiplier=0,
        allow_small_or_imprecise_dtypes=True), inc="g")
    cjl = p.do("gpsimd", lambda e: e.iota(
        jlall[:, :], [[128, 32]], base=M + 1, channel_multiplier=1,
        allow_small_or_imprecise_dtypes=True), inc="g")
    cu1 = p.do("vector", lambda e: e.tensor_scalar(
        it2[:, :], it2[:, :], 0.0, None, ALU.is_ge),
        wait=[("g", p.val("g")), ("v", p.val("v"))], inc="v")
    cu2 = p.do("vector", lambda e: e.tensor_scalar(
        it2b[:, :], it2b[:, :], 0.0, None, ALU.is_ge),
        wait=[("v", p.val("v"))], inc="v")
    cut = p.do("vector", lambda e: e.tensor_mul(
        UT[:, :], it2[:, :], it2b[:, :]), wait=[("v", p.val("v"))], inc="v")

    p.do("gpsimd", lambda e: e.dma_start(out=bosb[:, :], in_=W["bo"]),
         inc="wld", amt=16)
    for kt in range(8):
        p.do("gpsimd", lambda e, kt=kt: e.dma_start(
            out=presb[:, kt, :], in_=W["pre"][kt * 2:(kt + 1) * 2, :]),
            inc="wld", amt=16)
    p.do("gpsimd", lambda e: e.dma_start(
        out=k_hbm[:, 0:M], in_=W["memkT"]), inc="wld", amt=16)
    p.do("gpsimd", lambda e: e.dma_start(
        out=v_hbm[0:M, :], in_=W["memvJ"]), inc="wld", amt=16)
    wld0 = p.val("wld")
    for kt in range(8):
        ct = p.do("tensor", lambda e, kt=kt: e.matmul(
            pav[0:128, 0:NK], UT[:, :], presb[:, kt, :],
            start=True, stop=True),
            wait=[("v", cut), ("wld", wld0), ("v", p.val("v")),
                  ("mmo", p.val("mmo"))], inc="mmo")
        p.do("vector", lambda e, kt=kt, ct=ct: e.tensor_scalar(
            sT[:, kt, :], pav[0:128, 0:NK], 0.125, None, ALU.mult),
            wait=[("mmo", ct), ("v", p.val("v"))], inc="v")
    sT_done = p.val("v")

    ckv1 = p.do("gpsimd", lambda e: e.dma_start(
        out=wkvK[:, :, :],
        in_=W["wkvk"].rearrange("(kt kp) f -> kp kt f", kp=128)),
        inc="wg1", amt=16)
    ckv2 = p.do("gpsimd", lambda e: e.dma_start(
        out=wkvV[:, :, :],
        in_=W["wkvv"].rearrange("(kt kp) f -> kp kt f", kp=128)),
        inc="wg1", amt=16)
    cq1 = p.do("gpsimd", lambda e: e.dma_start(
        out=wqR[:, :, :],
        in_=W["wq"].rearrange("(kt kp) f -> kp kt f", kp=128)),
        inc="wg1", amt=16)
    for tc in range(TB // 512):
        for kd in range(8):
            p.do("sync", lambda e, tc=tc, kd=kd: e.dma_start(
                out=xnTs[:, kd, :],
                in_=xn_hbm[tc * 512:(tc + 1) * 512, kd * 128:(kd + 1) * 128],
                transpose=True),
                wait=[("xnst", C["xnst_cur"]), ("mmh", p.val("mmh"))],
                inc="xnT", amt=16)
        xnT_c = p.val("xnT")
        for ft in range(8):
            cmh = p.chain("tensor", [
                (lambda e, ft=ft, kd=kd: e.matmul(
                    pa[0:128, 0:512],
                    wkvK[:, kd, ft * 128:(ft + 1) * 128],
                    xnTs[:, kd, :], start=(kd == 0), stop=(kd == 7)))
                for kd in range(8)],
                wait=[("wg1", cq1), ("xnT", xnT_c), ("v", p.val("v")),
                      ("st1", p.val("st1"))], inc="mmh")
            cev = p.do("vector", lambda e: e.tensor_copy(
                kstage[:, :], pa[0:128, 0:512]),
                wait=[("mmh", cmh), ("v", p.val("v")),
                      ("st1", p.val("st1"))], inc="v")
            p.do("sync", lambda e, ft=ft, tc=tc, cev=cev: e.dma_start(
                out=k_hbm[ft * 128:(ft + 1) * 128,
                          M + tc * 512:M + (tc + 1) * 512],
                in_=kstage[:, :]), wait=[("v", cev)], inc="st1", amt=16)
        for sub in range(4):
            cmh = p.chain("tensor", [
                (lambda e, sub=sub, kd=kd: e.matmul(
                    pb[0:128, 0:512],
                    xnTs[:, kd, sub * 128:(sub + 1) * 128],
                    wkvV[:, kd, :], start=(kd == 0), stop=(kd == 7)))
                for kd in range(8)],
                wait=[("v", p.val("v"))], inc="mmh")
            cev = p.do("vector", lambda e: e.tensor_copy(
                kstage[:, :], pb[0:128, 0:512]),
                wait=[("mmh", cmh), ("v", p.val("v")),
                      ("st1", p.val("st1"))], inc="v")
            p.do("sync", lambda e, tc=tc, sub=sub, cev=cev: e.dma_start(
                out=v_hbm[M + tc * 512 + sub * 128:
                          M + tc * 512 + (sub + 1) * 128, :],
                in_=kstage[:, :]), wait=[("v", cev)], inc="st1", amt=16)
        for ft in range(8):
            cmh = p.chain("tensor", [
                (lambda e, ft=ft, kd=kd: e.matmul(
                    pa[0:128, 0:512], wqR[:, kd, ft * 128:(ft + 1) * 128],
                    xnTs[:, kd, :], start=(kd == 0), stop=(kd == 7)))
                for kd in range(8)],
                wait=[("v", p.val("v"))], inc="mmh")
            cev = p.do("vector", lambda e: e.tensor_copy(
                kstage[:, :], pa[0:128, 0:512]),
                wait=[("mmh", cmh), ("v", p.val("v")),
                      ("st1", p.val("st1"))], inc="v")
            p.do("sync", lambda e, ft=ft, tc=tc, cev=cev: e.dma_start(
                out=q_hbm[ft * 128:(ft + 1) * 128,
                          tc * 512:(tc + 1) * 512],
                in_=kstage[:, :]), wait=[("v", cev)], inc="st1", amt=16)
    proj_done = p.val("v")
    proj_st = p.val("st1")

    dpp = [pa, pb, C["pc"], C["pd"]]
    ptp = [pt1, pt2]
    pTs = [pTsb1, pTsb2]
    bank_free = [0] * 4
    ckp_hist = {}
    st = {}

    def a0_S0(rt, rtl, csc, cqh):
        JTOT = rt * 128 + 128 + M
        JCB = (JTOT + 511) // 512
        dots = dots2[rt % 2]
        for jc in range(JCB):
            wjc = min(512, JTOT - jc * 512)
            bank = dpp[jc % 4]
            cmd = p.chain("tensor", [
                (lambda e, rtl=rtl, jc=jc, wjc=wjc, bank=bank, kt=kt:
                 e.matmul(
                     bank[0:128, 0:wjc],
                     qTh[:, kt, rtl * 128:(rtl + 1) * 128],
                     Khat[:, kt, jc * 512:jc * 512 + wjc],
                     start=(kt == 0), stop=(kt == 7)))
                for kt in range(8)],
                wait=(([("v", csc), ("wg2", cqh)] if jc == 0 else [])
                      + [("v", bank_free[jc % 4])]),
                inc="mmA")
            bank_free[jc % 4] = p.do(
                "vector", lambda e, jc=jc, wjc=wjc, bank=bank, dots=dots:
                e.tensor_copy(dots[:, jc * 512:jc * 512 + wjc],
                              bank[0:128, 0:wjc]),
                wait=[("mmA", cmd), ("v", p.val("v")),
                      ("v", ckp_hist.get(rt % 2, 0)),
                      ("s", st.get(("exp", rt % 2), 0))], inc="v")
        st[("ev", rt)] = bank_free[(JCB - 1) % 4]
        st[("J", rt)] = (JTOT, JCB)

    def a0_S1(rt):
        JTOT, JCB = st[("J", rt)]
        dots = dots2[rt % 2]
        mstart = max(0, (JCB - 2) * 512)
        tk = [
            (lambda e, rt=rt, mstart=mstart, JTOT=JTOT, dots=dots:
             e.tensor_scalar(
                 wzi[:, mstart:JTOT], jidx[:, mstart:JTOT],
                 jlall[:, rt:rt + 1], None, ALU.is_ge)),
            (lambda e, mstart=mstart, JTOT=JTOT, dots=dots:
             e.copy_predicated(
                 dots[:, mstart:JTOT], wzi[:, mstart:JTOT],
                 consts[:, 3:4].to_broadcast([128, JTOT - mstart]))),
            (lambda e, JTOT=JTOT, dots=dots: e.max(
                out=m8[:, :], in_=dots[:, 0:JTOT])),
            (lambda e: e.tensor_scalar(
                cols[:, 1:2], m8[:, 0:1], -1.0, None, ALU.mult)),
            (lambda e, JTOT=JTOT, dots=dots: e.match_replace(
                out=wz[:, 0:JTOT], in_to_replace=m8[:, :],
                in_values=dots[:, 0:JTOT], imm_value=-2e30)),
        ]
        for rnd in range(7):
            tk.append(lambda e, JTOT=JTOT: e.max(
                out=m8[:, :], in_=wz[:, 0:JTOT]))
            tk.append(lambda e, JTOT=JTOT: e.match_replace(
                out=wz[:, 0:JTOT], in_to_replace=m8[:, :],
                in_values=wz[:, 0:JTOT], imm_value=-2e30))
        tk += [
            (lambda e: e.tensor_copy(cols[:, 2:3], m8[:, 7:8])),
            (lambda e, JTOT=JTOT, dots=dots: e.tensor_scalar(
                wzi[:, 0:JTOT], dots[:, 0:JTOT], cols[:, 2:3], None,
                ALU.is_lt)),
            (lambda e, JTOT=JTOT, dots=dots: e.copy_predicated(
                dots[:, 0:JTOT], wzi[:, 0:JTOT],
                consts[:, 3:4].to_broadcast([128, JTOT]))),
        ]
        ckp = p.chain("vector", tk,
                      wait=[("v", st[("ev", rt)]),
                            ("g", max(cji, cjl))],
                      inc="v", linked=True)
        ckp_hist[rt % 2] = ckp
        p2b = p2b2[rt % 2]
        cp2 = p.do("scalar", lambda e, JTOT=JTOT, dots=dots, p2b=p2b:
                   e.activation(
                       p2b[:, 0:JTOT], dots[:, 0:JTOT], AF.Exp,
                       bias=cols[:, 1:2], accum_out=cols[:, 4:5]),
                   wait=[("v", ckp), ("s", p.val("s")),
                         ("mmA", st.get(("avm", rt % 2), 0))], inc="s")
        st[("exp", rt % 2)] = cp2
        cpn = p.chain("vector", [
            (lambda e: e.reciprocal(cols[:, 5:6], cols[:, 4:5])),
            (lambda e, JTOT=JTOT, p2b=p2b: e.tensor_scalar(
                p2b[:, 0:JTOT], p2b[:, 0:JTOT], cols[:, 5:6], None,
                ALU.mult)),
        ], wait=[("s", cp2), ("v", p.val("v"))], inc="v",
            linked=True)
        st[("cpn", rt)] = cpn

    def a0_S2(rt, k, cvk2):
        JTOT, JCB = st[("J", rt)]
        cpn = st.pop(("cpn", rt))
        st.pop(("ev", rt)); st.pop(("J", rt))
        p2b = p2b2[rt % 2]
        njt = (JTOT + 127) // 128
        kps = [min(128, JTOT - jt * 128) for jt in range(njt)]
        ctt_c = p.chain("tensor", [
            (lambda e, p2b=p2b: e.transpose(
                ptp[0][0:kps[0], 0:128], p2b[:, 0:kps[0]],
                identb[:, :]))],
            wait=[("v", cpn), ("sp", p.val("sp"))], inc="mmA")
        for jt in range(njt):
            kp = kps[jt]
            cpt = p.do("vector", lambda e, jt=jt, kp=kp:
                       e.tensor_copy(pTs[jt % 2][0:kp, :],
                                     ptp[jt % 2][0:kp, 0:128]),
                       wait=[("mmA", ctt_c), ("v", p.val("v"))],
                       inc="sp")
            fns = [(lambda e, jt=jt, kp=kp, njt=njt: e.matmul(
                pav[0:64, 0:128], vk[0:kp, jt, :],
                pTs[jt % 2][0:kp, :],
                start=(jt == 0), stop=(jt == njt - 1),
                skip_group_check=True))]
            if jt + 1 < njt:
                fns.append(lambda e, jt=jt, p2b=p2b: e.transpose(
                    ptp[(jt + 1) % 2][0:kps[jt + 1], 0:128],
                    p2b[:, (jt + 1) * 128:(jt + 1) * 128 + kps[jt + 1]],
                    identb[:, :]))
            ctt_c = p.chain("tensor", fns,
                            wait=[("sp", cpt), ("wg2", cvk2)],
                            inc="mmA")
        st[("avm", rt % 2)] = ctt_c
        cav = p.do("vector", lambda e, cavd=ctt_c: e.tensor_copy(
            avsb[0:64, :], pav[0:64, 0:128]),
            wait=[("mmA", ctt_c), ("v", p.val("v")),
                  ("st2", p.val("st2"))], inc="v")
        p.do("sync", lambda e, k=k, rt=rt, cav=cav: e.dma_start(
            out=av_hbm[k * 64:(k + 1) * 64,
                       rt * 128:(rt + 1) * 128], in_=avsb[0:64, :]),
            wait=[("v", cav)], inc="st2", amt=16)

    for rh in range(2):
        cqh = p.do("gpsimd", lambda e, rh=rh: e.dma_start(
            out=qTh[:, :, :],
            in_=q_hbm[:, rh * 2048:(rh + 1) * 2048]
            .rearrange("(kt kp) t -> kp kt t", kp=128)),
            wait=[("st1", proj_st), ("mmA", p.val("mmA"))],
            inc="wg2", amt=16)
        for k in range(NK):
            ckh = p.do("gpsimd", lambda e: e.dma_start(
                out=Khat[:, :, :],
                in_=k_hbm.rearrange("(kt kp) j -> kp kt j", kp=128)),
                wait=[("st1", proj_st), ("mmA", p.val("mmA")),
                      ("v", p.val("v"))], inc="wg2", amt=16)
            csc = 0
            for kt in range(8):
                csc = p.do("vector", lambda e, k=k, kt=kt: e.tensor_scalar(
                    Khat[:, kt, :], Khat[:, kt, :], sT[:, kt, k:k + 1], None,
                    ALU.mult),
                    wait=[("wg2", ckh), ("v", max(sT_done, p.val("v")))],
                    inc="v")
            cvk1 = p.do("gpsimd", lambda e, k=k: e.dma_start(
                out=vk[:, 0:32, :],
                in_=v_hbm[0:4096, k * 64:(k + 1) * 64]
                .rearrange("(jt jp) d2 -> jp jt d2", jp=128)),
                wait=[("st1", proj_st), ("mmA", p.val("mmA"))],
                inc="wg2", amt=16)
            cvk2 = p.do("gpsimd", lambda e, k=k: e.dma_start(
                out=vk[0:16, 32, :],
                in_=v_hbm[4096:4112, k * 64:(k + 1) * 64]),
                inc="wg2", amt=16)
            for rtl in range(16):
                rt = rh * 16 + rtl
                a0_S0(rt, rtl, csc, cqh)
                if rtl >= 1:
                    a0_S1(rt - 1)
                if rtl >= 2:
                    a0_S2(rt - 2, k, cvk2)
            a0_S1(rh * 16 + 15)
            a0_S2(rh * 16 + 14, k, cvk2)
            a0_S2(rh * 16 + 15, k, cvk2)
    attn_done = p.val("st2")
    attn_mm_done = p.val("mmA")

    cwoR = p.do("gpsimd", lambda e: e.dma_start(
        out=woR[:, :, :],
        in_=W["wo"].rearrange("(kt kp) f -> kp kt f", kp=128)),
        wait=[("mmA", attn_mm_done), ("s", p.val("s")),
              ("v", p.val("v"))], inc="wg1", amt=16)
    for rt in range(32):
        cavl = p.do("sync", lambda e, rt=rt: e.dma_start(
            out=avT[:, :, :],
            in_=av_hbm[:, rt * 128:(rt + 1) * 128]
            .rearrange("(kt kp) r -> kp kt r", kp=128)),
            wait=[("st2", attn_done), ("mmo", p.val("mmo"))],
            inc="avl", amt=16)
        fns = []
        for dc in range(2):
            bank = C["pc"] if dc == 0 else C["pd"]
            for kt in range(4):
                fns.append(lambda e, rt=rt, dc=dc, kt=kt, bank=bank:
                           e.matmul(
                               bank[0:128, 0:512], avT[:, kt, :],
                               woR[:, kt, dc * 512:(dc + 1) * 512],
                               start=(kt == 0), stop=False))
            fns.append(lambda e, dc=dc, bank=bank: e.matmul(
                bank[0:128, 0:512], onesb[0:1, :],
                bosb[0:1, dc * 512:(dc + 1) * 512],
                start=False, stop=True))
        cmo = p.chain("tensor", fns,
                      wait=[("avl", cavl), ("wg1", cwoR), ("wld", wld0),
                            ("vG", p.val("vG")), ("v", C["cconst"])],
                      inc="mmo")
        cv2 = p.chain("vector", [
            (lambda e: e.tensor_copy(scr[:, 0:512], C["pc"][0:128, 0:512])),
            (lambda e: e.tensor_copy(scr[:, 512:1024],
                                     C["pd"][0:128, 0:512])),
        ], wait=[("mmo", cmo), ("st3", p.val("st3")), ("v", p.val("v"))],
            inc="vG")
        p.do("sync", lambda e, rt=rt: e.dma_start(
            out=myout[rt * 128:(rt + 1) * 128, :], in_=scr[:, :]),
            wait=[("vG", cv2)], inc="st3", amt=16)


def _emit_mamba2(p, nc, ctx, C, W, xcur, xn_hbm, DR2):
    T, NCH, CT, NS, NET = 4096, 8, 512, 16, 16
    sb = lambda n, sh, dt=BF16: ctx.enter_context(nc.sbuf_tensor(n, sh, dt))
    xt, xn, scr, scr2, stats = C["xt"], C["xn"], C["scr"], C["scr2"], C["stats"]
    identf, consts, onesb = C["identf"], C["consts"], C["onesb"]
    ones32, sel32 = C["ones32"], C["sel32"]
    pa, pb, pc, pd = C["pa"], C["pb"], C["pc"], C["pd"]
    pe_, pf_ = C["pe_"], C["pf_"]
    pquad = [pa, pb, pc, pd]
    cconst, ident_done, c32 = C["cconst"], C["ident_done"], C["c32"]
    win_bf, wdt_bf, wo_bf, wb_bf, wc_bf = (
        DR2["win_bf"], DR2["wdt_bf"], DR2["wo_bf"], DR2["wb_bf"],
        DR2["wc_bf"])
    winT_hbm, wdtT_hbm, woT_hbm = (DR2["winT_hbm"], DR2["wdtT_hbm"],
                                   DR2["woT_hbm"])

    xnT = sb("m2_xnT", [128, 8, CT])
    xssmT = sb("m2_xssmT", [128, NET, CT])
    zraw = sb("m2_zraw", [128, NET, CT])
    dtbuf = sb("m2_dtbuf", [128, NET, CT], F32)
    bcBa = sb("m2_bcBa", [128, NS, CT])
    bcCa = sb("m2_bcCa", [128, NS, CT])
    ygb = sb("m2_ygb", [128, NET, CT])
    uet = sb("m2_uet", [128, CT], F32)
    yet = sb("m2_yet", [128, CT], F32)
    zsil = sb("m2_zsil", [128, CT], F32)
    hseq = sb("m2_hseq", [128, CT], F32)
    ab8 = sb("m2_ab8", [128, 4, CT], F32)
    bx = sb("m2_bx", [128, CT], F32)
    Bm = sb("m2_Bm", [16, CT], F32)
    Cm = sb("m2_Cm", [16, CT], F32)
    state = sb("m2_state", [128, NET, NS], F32)
    wstage = sb("m2_wstage", [128, 4096])
    wgrp = sb("m2_wgrp", [128, NET, 512])
    wingrp = sb("m2_wingrp", [128, 8, 512])
    wbT = sb("m2_wbT", [128, NET, NS])
    wcT = sb("m2_wcT", [128, NET, NS])
    ALsb = sb("m2_ALsb", [128, NET, NS], F32)
    Asb = sb("m2_Asb", [128, NET, NS], F32)
    brow = sb("m2_brow", [1, 2048], F32)
    bdtT = sb("m2_bdtT", [128, NET], F32)
    DT = sb("m2_DT", [128, NET], F32)
    sgcol = sb("m2_sgcol", [128, 1], F32)
    ogsb = sb("m2_ogsb", [1, 1], F32)
    ogsg = sb("m2_ogsg", [1, 1], F32)
    ogsgb = sb("m2_ogsgb", [1, 1])

    p.do("gpsimd", lambda e: e.dma_start(out=win_bf, in_=W["win"]),
         inc="wld", amt=16)
    p.do("gpsimd", lambda e: e.dma_start(out=wdt_bf, in_=W["wdt"]),
         inc="wld", amt=16)
    p.do("gpsimd", lambda e: e.dma_start(out=wo_bf, in_=W["wo"]),
         inc="wld", amt=16)
    p.do("gpsimd", lambda e: e.dma_start(out=wb_bf, in_=W["wb"]),
         inc="wld", amt=16)
    cwld = p.do("gpsimd", lambda e: e.dma_start(out=wc_bf, in_=W["wc"]),
                inc="wld", amt=16)
    for cg in range(8):
        ctr = p.do("sync", lambda e, cg=cg: e.dma_start(
            out=wstage[:, 0:4096], in_=win_bf[:, cg * 128:(cg + 1) * 128],
            transpose=True), wait=[("wld", cwld), ("st0", p.val("st0"))],
            inc="trl", amt=16)
        p.do("sync", lambda e, cg=cg: e.dma_start(
            out=winT_hbm[cg * 128:(cg + 1) * 128, :], in_=wstage[:, 0:4096]),
            wait=[("trl", ctr)], inc="st0", amt=16)
    for cg in range(16):
        ctr = p.do("sync", lambda e, cg=cg: e.dma_start(
            out=wstage[:, 0:2048], in_=wdt_bf[:, cg * 128:(cg + 1) * 128],
            transpose=True), wait=[("st0", p.val("st0"))], inc="trl", amt=16)
        p.do("sync", lambda e, cg=cg: e.dma_start(
            out=wdtT_hbm[cg * 128:(cg + 1) * 128, :], in_=wstage[:, 0:2048]),
            wait=[("trl", ctr)], inc="st0", amt=16)
    for cg in range(16):
        ctr = p.do("sync", lambda e, cg=cg: e.dma_start(
            out=wstage[:, 0:1024], in_=wo_bf[:, cg * 128:(cg + 1) * 128],
            transpose=True), wait=[("st0", p.val("st0"))], inc="trl", amt=16)
        p.do("sync", lambda e, cg=cg: e.dma_start(
            out=woT_hbm[cg * 128:(cg + 1) * 128, :], in_=wstage[:, 0:1024]),
            wait=[("trl", ctr)], inc="st0", amt=16)
    for ke in range(NET):
        p.do("sync", lambda e, ke=ke: e.dma_start(
            out=wbT[:, ke, :], in_=wb_bf[:, ke * 128:(ke + 1) * 128],
            transpose=True), inc="trl", amt=16)
        p.do("sync", lambda e, ke=ke: e.dma_start(
            out=wcT[:, ke, :], in_=wc_bf[:, ke * 128:(ke + 1) * 128],
            transpose=True), inc="trl", amt=16)
    wT_done = p.val("st0")
    wbc_done = p.val("trl")

    cal = p.do("sync", lambda e: e.dma_start(
        out=ALsb[:, :, :],
        in_=W["al"].rearrange("(et ep) n -> ep et n", ep=128)),
        inc="wldh", amt=16)
    ca1 = p.do("scalar", lambda e: e.activation(
        Asb[:, :, :], ALsb[:, :, :], AF.Exp),
        wait=[("wldh", cal), ("s", p.val("s"))], inc="s")
    cAneg = p.do("vector", lambda e: e.tensor_scalar(
        Asb[:, :, :], Asb[:, :, :], -1.0, None, ALU.mult),
        wait=[("s", ca1), ("v", p.val("v"))], inc="v")
    cbd = p.do("sync", lambda e: e.dma_start(out=brow[:, :], in_=W["bdt"]),
               inc="wldh", amt=16)
    prev_cp = p.val("v")
    for f in range(NET):
        ctp = p.do("tensor", lambda e, f=f: e.transpose(
            pa[0:128, 0:1], brow[0:1, f * 128:(f + 1) * 128],
            identf[0:1, 0:1]),
            wait=[("wldh", cbd), ("g", ident_done), ("v", prev_cp),
                  ("mmt", p.val("mmt"))], inc="mmt")
        prev_cp = p.do("vector", lambda e, f=f, ctp=ctp: e.tensor_copy(
            bdtT[0:128, f:f + 1], pa[0:128, 0:1]), wait=[("mmt", ctp)],
            inc="v")
    cdd = p.do("sync", lambda e: e.dma_start(out=brow[:, :], in_=W["dd"]),
               wait=[("mmt", p.val("mmt"))], inc="wldh", amt=16)
    for f in range(NET):
        ctp = p.do("tensor", lambda e, f=f: e.transpose(
            pa[0:128, 0:1], brow[0:1, f * 128:(f + 1) * 128],
            identf[0:1, 0:1]),
            wait=[("wldh", cdd), ("v", prev_cp)], inc="mmt")
        prev_cp = p.do("vector", lambda e, f=f, ctp=ctp: e.tensor_copy(
            DT[0:128, f:f + 1], pa[0:128, 0:1]), wait=[("mmt", ctp)], inc="v")
    cog = p.do("sync", lambda e: e.dma_start(out=ogsb[:, :], in_=W["og"]),
               inc="wldh", amt=16)
    csg = p.do("scalar", lambda e: e.activation(
        ogsg[:, :], ogsb[:, :], AF.Sigmoid),
        wait=[("wldh", cog), ("s", p.val("s"))], inc="s")
    csgb = p.do("scalar", lambda e: e.activation(
        ogsgb[:, :], ogsg[:, :], AF.Copy), wait=[("s", csg)], inc="s")
    ctp = p.do("tensor", lambda e: e.matmul(
        pa[0:128, 0:1], onesb[0:1, :], ogsgb[0:1, 0:1],
        start=True, stop=True),
        wait=[("s", csgb), ("v", prev_cp), ("mmt", p.val("mmt"))], inc="mmt")
    prev_cp = p.do("vector", lambda e, ctp=ctp: e.tensor_copy(
        sgcol[:, 0:1], pa[0:128, 0:1]), wait=[("mmt", ctp)], inc="v")
    prep_v = p.val("v")
    cst0 = p.do("vector", lambda e: e.memset(state[:, :, :], 0.0), inc="v")

    ymm_done = p.val("mmo")
    for ch in range(NCH):
        bv = p.val("v"); bg = p.val("vG")
        p.wait_only("sync", [("vG", bg), ("st4", p.val("st4"))])
        p.wait_only("vector", [("vG", bg), ("st4", p.val("st4"))])
        p.wait_only("scalar", [("vG", bg), ("st4", p.val("st4"))])
        xn_counts = []
        for ttl in range(CT // 128):
            tt = ch * (CT // 128) + ttl
            w = [("v", xn_counts[-1] if xn_counts else bv)]
            cl = p.do("sync", lambda e, tt=tt: e.dma_start(
                out=xt[:, :], in_=xcur[tt * 128:(tt + 1) * 128, :]),
                wait=w, inc="ld", amt=16)
            cxn = _emit_ln(p, xt, xn, stats, scr, tt, cl, C["eps"],
                           C["nhalf"], cconst,
                           prev_v=(xn_counts[-1] if xn_counts else bv),
                           prev_s=p.val("s"), prev_store=p.val("xnst"))
            p.do("sync", lambda e, tt=tt: e.dma_start(
                out=xn_hbm[tt * 128:(tt + 1) * 128, :], in_=xn[:, :]),
                wait=[("v", cxn), ("xnT", p.val("xnT"))], inc="xnst", amt=16)
            xn_counts.append(cxn)
        xnst_ch = p.val("xnst")
        for dt_ in range(8):
            p.do("sync", lambda e, dt_=dt_, ch=ch: e.dma_start(
                out=xnT[:, dt_, :],
                in_=xn_hbm[ch * CT:(ch + 1) * CT, dt_ * 128:(dt_ + 1) * 128],
                transpose=True),
                wait=[("xnst", xnst_ch), ("vG", bg),
                      ("mmh", p.val("mmh"))], inc="xnT", amt=16)
        xnT_ch = p.val("xnT")

        for fg in range(8):
            cwin = p.do("gpsimd", lambda e, fg=fg: e.dma_start(
                out=wingrp[:, :, :],
                in_=winT_hbm[:, fg * 512:(fg + 1) * 512]
                .rearrange("(kt kp) f -> kp kt f", kp=128)),
                wait=[("st0", wT_done), ("mmh", p.val("mmh"))],
                inc="wg1", amt=16)
            for fl in range(4):
                f = fg * 4 + fl
                bank = pquad[fl]
                cmh = p.chain("tensor", [
                    (lambda e, fl=fl, kd=kd, bank=bank: e.matmul(
                        bank[0:128, 0:512],
                        wingrp[:, kd, fl * 128:(fl + 1) * 128],
                        xnT[:, kd, :],
                        start=(kd == 0), stop=(kd == 7)))
                    for kd in range(8)],
                    wait=[("wg1", cwin), ("xnT", xnT_ch),
                          ("mmb", p.val("mmb")),
                          ("v", p.val("v"))], inc="mmh")
                dst = (lambda e, f=f, bank=bank: e.tensor_copy(
                    xssmT[:, f, :], bank[0:128, 0:512])) if f < 16 else                     (lambda e, f=f, bank=bank: e.tensor_copy(
                        zraw[:, f - 16, :], bank[0:128, 0:512]))
                p.do("vector", dst,
                     wait=[("mmh", cmh), ("v", p.val("v"))], inc="v")
        xz_done = p.val("v")

        for which in range(2):
            wt = wbT if which == 0 else wcT
            dstm = Bm if which == 0 else Cm
            cmb = p.chain("tensor", [
                (lambda e, ke=ke, wt=wt: e.matmul(
                    pe_[0:16, 0:512], wt[:, ke, :],
                    xssmT[:, ke, :],
                    start=(ke == 0), stop=(ke == NET - 1)))
                for ke in range(NET)],
                wait=[("v", max(xz_done, p.val("v"))), ("trl", wbc_done)],
                inc="mmb")
            p.do("vector", lambda e, dstm=dstm, cmb=cmb: e.tensor_copy(
                dstm[0:16, :], pe_[0:16, 0:512]),
                wait=[("mmb", cmb), ("v", p.val("v"))], inc="v")
        bc_done = p.val("v")

        for eg in range(4):
            cwdt = p.do("gpsimd", lambda e, eg=eg: e.dma_start(
                out=wgrp[:, :, :],
                in_=wdtT_hbm[:, eg * 512:(eg + 1) * 512]
                .rearrange("(kt kp) f -> kp kt f", kp=128)),
                wait=[("st0", wT_done), ("mmh", p.val("mmh")),
                      ("mmo", ymm_done)],
                inc="wg2", amt=16)
            for el in range(4):
                et = eg * 4 + el
                bank = pquad[el]
                cmh = p.chain("tensor", [
                    (lambda e, el=el, ke=ke, bank=bank: e.matmul(
                        bank[0:128, 0:512],
                        wgrp[:, ke, el * 128:(el + 1) * 128],
                        xssmT[:, ke, :],
                        start=(ke == 0), stop=(ke == NET - 1)))
                    for ke in range(NET)],
                    wait=[("wg2", cwdt), ("v", max(xz_done, p.val("v")))],
                    inc="mmh")
                cm1 = p.do("vector", lambda e, et=et, bank=bank, cmh=cmh:
                           e.tensor_scalar(
                               scr[:, 0:512], bank[0:128, 0:512],
                               bdtT[:, et:et + 1], 0.0, ALU.add, ALU.max),
                           wait=[("mmh", cmh), ("v", p.val("v"))], inc="v")
                cs3 = p.chain("scalar", [
                    (lambda e, et=et, bank=bank: e.activation(
                        scr2[:, 0:512], bank[0:128, 0:512], AF.Abs,
                        bias=bdtT[:, et:et + 1])),
                    (lambda e: e.activation(
                        scr2[:, 0:512], scr2[:, 0:512], AF.Exp, scale=-1.0)),
                    (lambda e: e.activation(
                        scr2[:, 0:512], scr2[:, 0:512], AF.Ln, bias=1.0)),
                ], wait=[("mmh", cmh), ("v", prep_v), ("s", p.val("s"))],
                    inc="s")
                p.do("vector", lambda e, et=et: e.tensor_add(
                    dtbuf[:, et, :], scr[:, 0:512], scr2[:, 0:512]),
                    wait=[("s", cs3), ("v", cm1)], inc="v")
        dt_done = p.val("v")

        for n in range(NS):
            ca = p.do("gpsimd", lambda e, n=n: e.affine_select(
                sel32[0:16, :], ones32[0:16, :], [[0, 128]], ALU.is_equal,
                0.0, base=-n, channel_multiplier=1),
                wait=[("v", c32), ("mmt", p.val("mmt"))], inc="g")
            ctb = p.do("tensor", lambda e, n=n: e.matmul(
                pe_[0:128, 0:512], sel32[0:16, :], Bm[0:16, :],
                start=True, stop=True),
                wait=[("g", ca), ("v", max(bc_done, p.val("v"))),
                      ("vG", bg)], inc="mmt")
            p.do("vector", lambda e, n=n, ctb=ctb: e.tensor_copy(
                bcBa[:, n, :], pe_[0:128, 0:512]),
                wait=[("mmt", ctb), ("v", p.val("v"))], inc="v")
            ctc = p.do("tensor", lambda e, n=n: e.matmul(
                pf_[0:128, 0:512], sel32[0:16, :], Cm[0:16, :],
                start=True, stop=True),
                wait=[("v", p.val("v")), ("vG", bg)], inc="mmt")
            p.do("vector", lambda e, n=n, ctc=ctc: e.tensor_copy(
                bcCa[:, n, :], pf_[0:128, 0:512]),
                wait=[("mmt", ctc), ("v", p.val("v"))], inc="v")
        bc_built = p.val("v")

        for et in range(NET):
            cu = p.chain("vector", [
                (lambda e, et=et: e.tensor_mul(
                    uet[:, :], dtbuf[:, et, :], xssmT[:, et, :])),
                (lambda e: e.memset(yet[:, :], 0.0)),
            ], wait=[("v", max(dt_done, bc_built, p.val("v")))], inc="v")
            czs = p.do("scalar", lambda e, et=et: e.activation(
                zsil[:, :], zraw[:, et, :], AF.Silu),
                wait=[("s", p.val("s")), ("v", p.val("v"))], inc="s")
            for ng in range(4):
                cab_v = p.chain("vector", [
                    (lambda e, et=et, n=ng * 4 + nl, nl=nl: e.tensor_scalar(
                        ab8[:, nl, :], dtbuf[:, et, :],
                        Asb[:, et, n:n + 1], None, ALU.mult))
                    for nl in range(4)],
                    wait=[("v", p.val("v"))], inc="v")
                cab = p.do("scalar", lambda e: e.activation(
                    ab8[:, :, :], ab8[:, :, :], AF.Exp),
                    wait=[("v", cab_v), ("s", p.val("s"))], inc="s")
                for nl in range(4):
                    n = ng * 4 + nl
                    p.chain("vector", [
                        (lambda e, n=n: e.tensor_mul(
                            bx[:, :], uet[:, :], bcBa[:, n, :])),
                        (lambda e, et=et, n=n, nl=nl: e.tensor_tensor_scan(
                            hseq[:, :], ab8[:, nl, :], bx[:, :],
                            state[:, et, n:n + 1], ALU.mult, ALU.add)),
                        (lambda e, et=et, n=n: e.tensor_copy(
                            state[:, et, n:n + 1], hseq[:, CT - 1:CT])),
                        (lambda e, n=n: e.tensor_mul(
                            hseq[:, :], hseq[:, :], bcCa[:, n, :])),
                        (lambda e: e.tensor_add(
                            yet[:, :], yet[:, :], hseq[:, :])),
                    ], wait=[("s", cab), ("v", p.val("v"))], inc="v",
                        linked=True)
            p.chain("vector", [
                (lambda e, et=et: e.scalar_tensor_tensor(
                    yet[:, :], xssmT[:, et, :], DT[:, et:et + 1], yet[:, :],
                    ALU.mult, ALU.add)),
                (lambda e: e.tensor_mul(yet[:, :], yet[:, :], zsil[:, :])),
                (lambda e, et=et: e.tensor_scalar(
                    ygb[:, et, :], yet[:, :], sgcol[:, 0:1], None,
                    ALU.mult)),
            ], wait=[("v", p.val("v")), ("s", czs), ("mmo", ymm_done)],
                inc="v", linked=True)
        scan_done = p.val("v")

        for dc in range(2):
            cwo = p.do("gpsimd", lambda e, dc=dc, ch=ch: e.dma_start(
                out=wgrp[:, :, :],
                in_=woT_hbm[:, dc * 512:(dc + 1) * 512]
                .rearrange("(kt kp) f -> kp kt f", kp=128)),
                wait=[("st0", wT_done), ("mmh", p.val("mmh")),
                      ("mmo", p.val("mmo"))],
                inc="wg2", amt=16)
            for ttl in range(CT // 128):
                tt = ch * (CT // 128) + ttl
                bank = pe_ if dc == 0 else pf_
                cmo = p.chain("tensor", [
                    (lambda e, ttl=ttl, dc=dc, ke=ke, bank=bank: e.matmul(
                        bank[0:128, 0:512],
                        ygb[:, ke, ttl * 128:(ttl + 1) * 128],
                        wgrp[:, ke, :],
                        start=(ke == 0), stop=(ke == NET - 1)))
                    for ke in range(NET)],
                    wait=[("v", scan_done), ("wg2", cwo),
                          ("vG", p.val("vG"))], inc="mmo")
                cl2 = p.do("sync", lambda e, tt=tt, dc=dc: e.dma_start(
                    out=xt[:, dc * 512:(dc + 1) * 512],
                    in_=xcur[tt * 128:(tt + 1) * 128,
                             dc * 512:(dc + 1) * 512]),
                    wait=[("vG", p.val("vG")), ("v", xn_counts[-1])],
                    inc="ld2", amt=16)
                cv1 = p.do("vector", lambda e, dc=dc, bank=bank, cmo=cmo,
                           cl2=cl2: e.tensor_add(
                               scr[:, dc * 512:(dc + 1) * 512],
                               xt[:, dc * 512:(dc + 1) * 512],
                               bank[0:128, 0:512]),
                           wait=[("mmo", cmo), ("ld2", cl2),
                                 ("st4", p.val("st4")),
                                 ("v", p.val("v"))], inc="vG")
                p.do("sync", lambda e, tt=tt, dc=dc: e.dma_start(
                    out=xcur[tt * 128:(tt + 1) * 128,
                             dc * 512:(dc + 1) * 512],
                    in_=scr[:, dc * 512:(dc + 1) * 512]),
                    wait=[("vG", cv1)], inc="st4", amt=16)
        ymm_done = p.val("mmo")


def _emit_ffn1(p, nc, ctx, C, W, xn_hbm, myout, tag):
    T, NTT = 2048, 16
    EM = 32
    sb = lambda n, sh, dt=BF16: ctx.enter_context(nc.sbuf_tensor(n, sh, dt))
    xt, scr, stats = C["xt"], C["scr"], C["stats"]
    identf, consts, onesb = C["identf"], C["consts"], C["onesb"]
    pa, pb, pc, pd, pe_, pf_ = (C["pa"], C["pb"], C["pc"], C["pd"],
                                C["pe_"], C["pf_"])
    ph = [pa, pb]
    pO = [pc, pd, pe_, pf_]

    xnT = sb(tag + "_xnT", [128, 8, T])
    acc = sb(tag + "_acc", [128, NTT, 1024], F32)
    u = sb(tag + "_u", [128, 26624])
    rwsb = sb(tag + "_rw", [128, 8, 64])
    rbsb = sb(tag + "_rb", [1, 64])
    b1rT = sb(tag + "_b1rT", [128, 2, EM], F32)
    sb1T = sb(tag + "_sb1T", [128, 16, 1], F32)
    logits = sb(tag + "_lg", [128, 64], F32)
    m8r = sb(tag + "_m8r", [128, 8], F32)
    rcol = sb(tag + "_rc", [128, 8], F32)
    er = sb(tag + "_er", [128, 64], F32)
    cwn = sb(tag + "_cwn", [128, 64], F32)
    cwTb = sb(tag + "_cwT", [64, T])
    ones64 = sb(tag + "_o64", [64, 128])
    sel64 = sb(tag + "_s64", [64, 128])

    sb1a_t = u[0:1, 0:4096].bitcast(F32)
    b1nat = u[0:EM, 4096:4608].bitcast(F32)
    b2sb = u[0:EM, 4608:5632]
    sb2a = u[0:1, 5632:6656]
    w1g = u[:, 0:4096].rearrange("p (e kt f) -> p e kt f", e=2, kt=8)
    w2g = u[:, 4096:8192].rearrange("p (e ht f) -> p e ht f", e=2, ht=2)
    cwb = u[:, 8192:12288].rearrange("p (e f) -> p e f", e=2)
    hseg = u[:, 12288:20480].rearrange("p (e ht f) -> p e ht f", e=2, ht=2)
    s_hts = u[:, 0:16384].rearrange("p (s f) -> p s f", s=8)
    s_w2g = u[:, 16384:24576].rearrange("p (s f) -> p s f", s=8)
    s_w1s = u[:, 24576:26624].rearrange("p (b kt f) -> p b kt f", b=2, kt=8)

    cones = p.do("vector", lambda e: e.memset(ones64[:, :], 1.0),
                 wait=[("v", p.val("v"))], inc="v")

    p.do("gpsimd", lambda e: e.dma_start(
        out=rwsb[:, :, :],
        in_=W["rw"].rearrange("(kt kp) f -> kp kt f", kp=128)),
        inc="wld", amt=16)
    p.do("gpsimd", lambda e: e.dma_start(out=rbsb[:, :], in_=W["rb"]),
         inc="wld", amt=16)
    wld0 = p.val("wld")
    cb1 = p.do("sync", lambda e: e.dma_start(out=b1nat[:, :], in_=W["b1"]),
               inc="wldh", amt=16)
    cb1s = p.do("sync", lambda e: e.dma_start(
        out=sb1a_t[:, :], in_=W["sb1"]), inc="wldh", amt=16)
    for ht in range(2):
        bank = pe_ if ht == 0 else pf_
        ct = p.do("tensor", lambda e, ht=ht, bank=bank: e.transpose(
            bank[0:128, 0:EM], b1nat[0:EM, ht * 128:(ht + 1) * 128],
            identf[0:EM, 0:EM]),
            wait=[("wldh", cb1), ("g", C["ident_done"]), ("v", p.val("v")),
                  ("mmt", p.val("mmt"))], inc="mmt")
        p.do("vector", lambda e, ht=ht, ct=ct, bank=bank: e.tensor_copy(
            b1rT[:, ht, :], bank[0:128, 0:EM]), wait=[("mmt", ct)], inc="v")
    for st_i in range(16):
        bank = pe_ if (st_i % 2 == 0) else pf_
        ct = p.do("tensor", lambda e, st_i=st_i, bank=bank:
                  e.transpose(bank[0:128, 0:1],
                              sb1a_t[0:1, st_i * 128:(st_i + 1) * 128],
                              identf[0:1, 0:1]),
                  wait=[("wldh", cb1s), ("v", p.val("v"))], inc="mmt")
        p.do("vector", lambda e, st_i=st_i, ct=ct, bank=bank:
             e.tensor_copy(sb1T[:, st_i, 0:1], bank[0:128, 0:1]),
             wait=[("mmt", ct)], inc="v")
    prep_v = p.val("v")

    ph_evac = {}
    ph_evac2 = {}
    ph_iter = 0
    u_free_at = p.val("mmo")
    for hoff in (0, T):
        for dt_ in range(8):
            p.do("sync", lambda e, dt_=dt_, hoff=hoff: e.dma_start(
                out=xnT[:, dt_, :],
                in_=xn_hbm[hoff:hoff + T, dt_ * 128:(dt_ + 1) * 128],
                transpose=True),
                wait=[("xnst", C["xnst_cur"]), ("mmh", p.val("mmh")),
                      ("mmt", p.val("mmt")), ("mmo", p.val("mmo"))],
                inc="xnT", amt=16)
        xnT_all = p.val("xnT")

        cb2 = p.do("gpsimd", lambda e: e.dma_start(
            out=b2sb[:, :], in_=W["b2"]),
            wait=[("mmo", u_free_at), ("sh", p.val("sh")),
                  ("vh", p.val("vh"))], inc="wld", amt=16)
        cb2s = p.do("gpsimd", lambda e: e.dma_start(
            out=sb2a[:, :], in_=W["sb2"]), inc="wld", amt=16)

        prev_evac = 0
        last_tr = 0
        for tt in range(NTT):
            ct = p.chain("tensor", [
                (lambda e: e.matmul(
                    pe_[0:128, 0:64], onesb[0:1, :], rbsb[0:1, :],
                    start=True, stop=False))] + [
                (lambda e, kt=kt, tt=tt: e.matmul(
                    pe_[0:128, 0:64], xnT[:, kt, tt * 128:(tt + 1) * 128],
                    rwsb[:, kt, :], start=False, stop=(kt == 7)))
                for kt in range(8)],
                wait=[("xnT", xnT_all), ("wld", wld0), ("v", p.val("v")),
                      ("vacc", p.val("vacc"))],
                inc="mmt")
            cn = p.chain("vector", [
                (lambda e: e.tensor_copy(logits[:, :], pe_[0:128, 0:64])),
                (lambda e: e.max(out=m8r[:, :], in_=logits[:, :])),
                (lambda e: e.tensor_scalar(
                    rcol[:, 0:1], m8r[:, 0:1], -1.0, None, ALU.mult)),
            ], wait=[("mmt", ct), ("v", p.val("v"))], inc="v", linked=True)
            ce = p.do("scalar", lambda e: e.activation(
                er[:, :], logits[:, :], AF.Exp, bias=rcol[:, 0:1]),
                wait=[("v", cn), ("s", p.val("s"))], inc="s")
            ccw = p.chain("vector", [
                (lambda e: e.scalar_tensor_tensor(
                    er[:, :], logits[:, :], m8r[:, 7:8], er[:, :],
                    ALU.is_ge, ALU.mult, accum_out=rcol[:, 1:2])),
                (lambda e: e.reciprocal(rcol[:, 2:3], rcol[:, 1:2])),
                (lambda e, tt=tt: e.tensor_scalar(
                    cwn[:, :], er[:, :], rcol[:, 2:3], None, ALU.mult)),
            ], wait=[("s", ce), ("v", p.val("v")), ("mmt", last_tr)],
                inc="v", linked=True)
            last_tr = p.do("tensor", lambda e, tt=tt: e.transpose(
                pf_[0:64, 0:128], cwn[:, :], identf[:, :]),
                wait=[("v", ccw), ("v", prev_evac)], inc="mmt")
            prev_evac = p.do("vector", lambda e, tt=tt: e.tensor_copy(
                cwTb[0:64, tt * 128:(tt + 1) * 128], pf_[0:64, 0:128]),
                wait=[("mmt", last_tr), ("v", p.val("v"))], inc="v")
        cwT_done = prev_evac

        for tt in range(NTT):
            for dc in range(2):
                ct = p.chain("tensor", [
                    (lambda e, tt=tt, dc=dc: e.matmul(
                        pO[dc][0:128, 0:512],
                        cwTb[0:EM, tt * 128:(tt + 1) * 128],
                        b2sb[0:EM, dc * 512:(dc + 1) * 512],
                        start=True, stop=False)),
                    (lambda e, dc=dc: e.matmul(
                        pO[dc][0:128, 0:512], onesb[0:1, :],
                        sb2a[0:1, dc * 512:(dc + 1) * 512],
                        start=False, stop=True))],
                    wait=[("v", cwT_done), ("wld", p.val("wld")),
                          ("vacc", p.val("vacc")), ("v", cones)], inc="mmt")
                p.do("vector", lambda e, tt=tt, dc=dc, ct=ct: e.tensor_copy(
                    acc[:, tt, dc * 512:(dc + 1) * 512],
                    pO[dc][0:128, 0:512]),
                    wait=[("mmt", ct), ("v", p.val("v"))], inc="vacc")
        acc_init = p.val("vacc")

        for grp in range(16):
            ldw = []
            for i in range(2):
                e_g = grp * 2 + i
                c1 = p.do("gpsimd", lambda en, e_g=e_g, i=i: en.dma_start(
                    out=w1g[:, i, :, :],
                    in_=W["w1"][e_g].rearrange("(kt kp) f -> kp kt f",
                                               kp=128)),
                    wait=[("mmo", u_free_at), ("vacc", acc_init)],
                    inc="wg1", amt=16)
                c2 = p.do("gpsimd", lambda en, e_g=e_g, i=i: en.dma_start(
                    out=w2g[:, i, :, :],
                    in_=W["w2"][e_g].rearrange("(ht hp) d -> hp ht d",
                                               hp=128)),
                    inc="wg2", amt=16)
                ldw.append((c1, c2))
            cwb_done = 0
            for i in range(2):
                e_g = grp * 2 + i
                ca = p.do("gpsimd", lambda en, e_g=e_g: en.affine_select(
                    sel64[:, :], ones64[:, :], [[0, 128]], ALU.is_equal, 0.0,
                    base=-e_g, channel_multiplier=1),
                    wait=[("v", cones), ("mmt", p.val("mmt")),
                          ("g", p.val("g"))], inc="g")
                for chk in range(4):
                    ct = p.do("tensor", lambda e, i=i, chk=chk: e.matmul(
                        pe_[0:128, 0:512], sel64[:, :],
                        cwTb[0:64, chk * 512:(chk + 1) * 512],
                        start=True, stop=True),
                        wait=[("g", ca), ("v", max(cwT_done, p.val("v"))),
                              ("sh", p.val("sh")),
                              ("vacc", p.val("vacc"))], inc="mmt")
                    cwb_done = p.do("vector", lambda e, i=i, chk=chk, ct=ct:
                                    e.tensor_copy(
                                        cwb[:, i, chk * 512:(chk + 1) * 512],
                                        pe_[0:128, 0:512]),
                                    wait=[("mmt", ct), ("v", p.val("v"))],
                                    inc="sh")
            for i in range(2):
                for blk in range(4):
                    for ht in range(2):
                        bank = ph[ph_iter % 2]
                        wv = [("vh", ph_evac.get(ph_iter - 2, 0)),
                              ("sh", ph_evac2.get(ph_iter - 2, 0))]
                        cmh = p.chain("tensor", [
                            (lambda en, i=i, blk=blk, ht=ht, kt=kt,
                             bank=bank: en.matmul(
                                 bank[0:128, 0:512],
                                 w1g[:, i, kt, ht * 128:(ht + 1) * 128],
                                 xnT[:, kt, blk * 512:(blk + 1) * 512],
                                 start=(kt == 0), stop=(kt == 7)))
                            for kt in range(8)],
                            wait=[("wg1", ldw[i][0]),
                                  ("xnT", xnT_all)] + wv, inc="mmh")
                        cse = p.do("scalar", lambda en, i=i, blk=blk, ht=ht,
                                   grp=grp, bank=bank, cmh=cmh: en.activation(
                                       hseg[:, i, ht,
                                            blk * 512:(blk + 1) * 512],
                                       bank[0:128, 0:512], AF.Silu,
                                       bias=b1rT[:, ht, grp * 2 + i:
                                                 grp * 2 + i + 1]),
                                   wait=[("mmh", cmh), ("v", prep_v),
                                         ("s", p.val("s"))], inc="sh")
                        cvh = p.do("vector", lambda en, i=i, blk=blk, ht=ht,
                                   cse=cse: en.tensor_tensor(
                                       hseg[:, i, ht,
                                            blk * 512:(blk + 1) * 512],
                                       hseg[:, i, ht,
                                            blk * 512:(blk + 1) * 512],
                                       cwb[:, i, blk * 512:(blk + 1) * 512],
                                       ALU.mult),
                                   wait=[("sh", max(cse, cwb_done)),
                                         ("vh", p.val("vh"))], inc="vh")
                        ph_evac[ph_iter] = cvh
                        ph_iter += 1
            w1_done = p.val("vh")
            for tb in range(8):
                fns = []
                for i in range(2):
                    for ht in range(2):
                        for sub in range(2):
                            for dc in range(2):
                                fns.append(
                                    lambda en, i=i, ht=ht, tb=tb, sub=sub,
                                    dc=dc, bi=sub * 2 + dc: en.matmul(
                                        pO[bi][0:128, 0:512],
                                        hseg[:, i, ht,
                                             tb * 256 + sub * 128:
                                             tb * 256 + sub * 128 + 128],
                                        w2g[:, i, ht,
                                            dc * 512:(dc + 1) * 512],
                                        start=(i == 0 and ht == 0),
                                        stop=(i == 1 and ht == 1)))
                cmo = p.chain("tensor", fns,
                              wait=[("vh", w1_done), ("wg2", ldw[1][1]),
                                    ("vacc", p.val("vacc")),
                                    ("sh", p.val("sh"))], inc="mmo")
                p.chain("vector", [
                    (lambda en, bi=sub * 2 + dc, tt_i=tb * 2 + sub, dc=dc:
                     en.tensor_tensor(
                         acc[:, tt_i, dc * 512:(dc + 1) * 512],
                         acc[:, tt_i, dc * 512:(dc + 1) * 512],
                         pO[bi][0:128, 0:512], ALU.add))
                    for sub in range(2) for dc in range(2)],
                    wait=[("mmo", cmo), ("vacc", p.val("vacc"))],
                    inc="vacc", amt=4)
            u_free_at = p.val("mmo")
        routed_done_v = p.val("vacc")
        routed_done_mmo = p.val("mmo")

        for g in range(2):
            cw2 = p.do("gpsimd", lambda en, g=g: en.dma_start(
                out=s_w2g[:, :, :],
                in_=W["sw2"][0, g * 1024:(g + 1) * 1024, :]
                .rearrange("(st hp) d -> hp st d", hp=128)),
                wait=[("mmo", max(routed_done_mmo, p.val("mmo"))),
                      ("vh", p.val("vh")), ("sh", p.val("sh"))],
                inc="wg2", amt=16)
            for s_l in range(8):
                s_g = g * 8 + s_l
                cw1 = p.do("gpsimd", lambda en, s_g=s_g, s_l=s_l:
                           en.dma_start(
                               out=s_w1s[:, s_l % 2, :, :],
                               in_=W["sw1"][0, :, s_g * 128:(s_g + 1) * 128]
                               .rearrange("(kt kp) f -> kp kt f", kp=128)),
                           wait=[("mmh", max(0, p.val("mmh") - 4)),
                                 ("mmo", routed_done_mmo)],
                           inc="wg1", amt=16)
                for blk in range(4):
                    bank = ph[ph_iter % 2]
                    wv = [("sh", ph_evac2.get(ph_iter - 2, 0)),
                          ("vh", ph_evac.get(ph_iter - 2, 0))]
                    cmh = p.chain("tensor", [
                        (lambda en, s_l=s_l, blk=blk, kt=kt, bank=bank:
                         en.matmul(
                             bank[0:128, 0:512],
                             s_w1s[:, s_l % 2, kt, :],
                             xnT[:, kt, blk * 512:(blk + 1) * 512],
                             start=(kt == 0), stop=(kt == 7)))
                        for kt in range(8)],
                        wait=[("wg1", cw1), ("vacc", routed_done_v)] + wv,
                        inc="mmh")
                    cse = p.do("scalar", lambda en, s_g=s_g, s_l=s_l,
                               blk=blk, bank=bank, cmh=cmh: en.activation(
                                   s_hts[:, s_l, blk * 512:(blk + 1) * 512],
                                   bank[0:128, 0:512], AF.Silu,
                                   bias=sb1T[:, s_g, 0:1]),
                               wait=[("mmh", cmh), ("v", prep_v),
                                     ("s", p.val("s"))], inc="sh")
                    ph_evac2[ph_iter] = cse
                    ph_iter += 1
            f1_done = p.val("sh")
            for tb in range(8):
                fns = []
                for s_l in range(8):
                    for sub in range(2):
                        for dc in range(2):
                            fns.append(
                                lambda en, s_l=s_l, tb=tb, sub=sub, dc=dc,
                                bi=sub * 2 + dc: en.matmul(
                                    pO[bi][0:128, 0:512],
                                    s_hts[:, s_l,
                                          tb * 256 + sub * 128:
                                          tb * 256 + sub * 128 + 128],
                                    s_w2g[:, s_l,
                                          dc * 512:(dc + 1) * 512],
                                    start=(s_l == 0), stop=(s_l == 7)))
                cmo = p.chain("tensor", fns,
                              wait=[("sh", f1_done), ("wg2", cw2),
                                    ("vacc", p.val("vacc"))], inc="mmo")
                p.chain("vector", [
                    (lambda en, bi=sub * 2 + dc, tt_i=tb * 2 + sub, dc=dc:
                     en.tensor_tensor(
                         acc[:, tt_i, dc * 512:(dc + 1) * 512],
                         acc[:, tt_i, dc * 512:(dc + 1) * 512],
                         pO[bi][0:128, 0:512], ALU.add))
                    for sub in range(2) for dc in range(2)],
                    wait=[("mmo", cmo), ("vacc", p.val("vacc"))],
                    inc="vacc", amt=4)
        u_free_at = p.val("mmo")
        acc_done = p.val("vacc")

        for tt in range(NTT):
            p.do("sync", lambda e, tt=tt, hoff=hoff: e.dma_start(
                out=myout[hoff + tt * 128:hoff + (tt + 1) * 128, :],
                in_=acc[:, tt, :]),
                wait=[("vacc", acc_done), ("st3", p.val("st3"))],
                inc="st3", amt=16)
        p.wait_only("vector", [("st3", p.val("st3"))])


def _emit_attnb1(p, nc, ctx, C, W, xn_hbm, myout, DR, DR3):
    TB, NH = 4096, 8
    NRT = TB // 128
    POLY = [-0.037, 0.481, -2.3, 4.917, -1.791]
    sb = lambda n, sh, dt=BF16: ctx.enter_context(nc.sbuf_tensor(n, sh, dt))
    xt, scr, stats, cols = C["xt"], C["scr"], C["stats"], C["cols"]
    identb, consts, onesb = C["identb"], C["consts"], C["onesb"]
    pa, pb, pt1, pt2 = C["pa"], C["pb"], C["pt1"], C["pt2"]
    pav = C["pe_"]
    v_hbm, av_hbm = DR3["v_hbm1"], DR["av_hbm"]
    wbfs = {k: DR3[k + "_bf"] for k in ("wq", "wk", "wv", "wo")}
    wTs = {k: DR3[k + "T"] for k in ("wq", "wk", "wv", "wo")}

    kT = sb("a1_kT", [128, 4, TB])
    qT = sb("a1_qT", [128, 4, TB])
    ubig = sb("a1_ubig", [128, 49152])
    vhb = sb("a1_vhb", [128, 32, 64])
    avT = sb("a1_avT", [128, 4, 128])
    avsb = sb("a1_avsb", [64, 128])
    pTsb1 = sb("a1_pT1", [128, 128])
    pTsb2 = sb("a1_pT2", [128, 128])
    vstage = sb("a1_vst", [128, 512])
    mkbuf = sb("a1_mk", [128, 512], I32)
    jidx = sb("a1_jidx", [128, 512], F32)
    jlall = sb("a1_jlall", [128, 32], F32)
    bosb = sb("a1_bos", [1, 1024])
    wstage = sb("a1_wstage", [128, 1024])

    dots2 = [ubig[:, 0:8192].bitcast(F32),
             ubig[:, 8192:16384].bitcast(F32)]
    ebuf = ubig[:, 16384:24576].bitcast(F32)
    lnpb = ubig[:, 24576:32768].bitcast(F32)
    p2b2 = [ubig[:, 32768:36864 + 4096], ubig[:, 40960:45056]]
    wkTr = ubig[:, 0:4096].rearrange("p (kt f) -> p kt f", kt=8)
    wvTr = ubig[:, 4096:8192].rearrange("p (kt f) -> p kt f", kt=8)
    wqTr = ubig[:, 8192:12288].rearrange("p (kt f) -> p kt f", kt=8)
    xnTs = ubig[:, 12288:16384].rearrange("p (kt f) -> p kt f", kt=8)
    woTr = ubig[:, 45056:49152].rearrange("p (kt f) -> p kt f", kt=4)

    cjx = p.do("gpsimd", lambda e: e.iota(
        jidx[:, :], [[1, 512]], base=0, channel_multiplier=0,
        allow_small_or_imprecise_dtypes=True), inc="g")
    cjl = p.do("gpsimd", lambda e: e.iota(
        jlall[:, :], [[128, 32]], base=1, channel_multiplier=1,
        allow_small_or_imprecise_dtypes=True), inc="g")

    for nm in ("wq", "wk", "wv", "wo"):
        p.do("gpsimd", lambda e, nm=nm: e.dma_start(
            out=wbfs[nm], in_=W[nm]), inc="wld", amt=16)
    cwc = p.val("wld")
    p.do("gpsimd", lambda e: e.dma_start(out=bosb[:, :], in_=W["bo"]),
         inc="wld", amt=16)
    wld0 = p.val("wld")
    for nm, ncg, wd in (("wq", 8, 512), ("wk", 8, 512), ("wv", 8, 512),
                        ("wo", 4, 1024)):
        for cg in range(ncg):
            ctr = p.do("sync", lambda e, nm=nm, cg=cg, wd=wd: e.dma_start(
                out=wstage[:, 0:wd],
                in_=wbfs[nm][:, cg * 128:(cg + 1) * 128],
                transpose=True),
                wait=[("wld", cwc), ("st0", p.val("st0"))],
                inc="trl", amt=16)
            p.do("sync", lambda e, nm=nm, cg=cg, wd=wd: e.dma_start(
                out=wTs[nm][cg * 128:(cg + 1) * 128, :],
                in_=wstage[:, 0:wd]),
                wait=[("trl", ctr)], inc="st0", amt=16)
    wT_done = p.val("st0")

    ckl = p.do("gpsimd", lambda e: e.dma_start(
        out=wkTr[:, :, :],
        in_=wTs["wk"].rearrange("(kt kp) f -> kp kt f", kp=128)),
        wait=[("st0", wT_done)], inc="wg1", amt=16)
    cvl = p.do("gpsimd", lambda e: e.dma_start(
        out=wvTr[:, :, :],
        in_=wTs["wv"].rearrange("(kt kp) f -> kp kt f", kp=128)),
        inc="wg1", amt=16)
    cql = p.do("gpsimd", lambda e: e.dma_start(
        out=wqTr[:, :, :],
        in_=wTs["wq"].rearrange("(kt kp) f -> kp kt f", kp=128)),
        inc="wg1", amt=16)
    for tc in range(TB // 512):
        for kd in range(8):
            p.do("sync", lambda e, tc=tc, kd=kd: e.dma_start(
                out=xnTs[:, kd, :],
                in_=xn_hbm[tc * 512:(tc + 1) * 512, kd * 128:(kd + 1) * 128],
                transpose=True),
                wait=[("xnst", C["xnst_cur"]), ("mmh", p.val("mmh"))],
                inc="xnT", amt=16)
        xnT_c = p.val("xnT")
        for ft in range(4):
            cmh = p.chain("tensor", [
                (lambda e, ft=ft, kd=kd: e.matmul(
                    pa[0:128, 0:512],
                    wkTr[:, kd, ft * 128:(ft + 1) * 128],
                    xnTs[:, kd, :], start=(kd == 0), stop=(kd == 7)))
                for kd in range(8)],
                wait=[("wg1", cql), ("xnT", xnT_c), ("v", p.val("v"))],
                inc="mmh")
            p.do("vector", lambda e, ft=ft, tc=tc: e.tensor_copy(
                kT[:, ft, tc * 512:(tc + 1) * 512], pa[0:128, 0:512]),
                wait=[("mmh", cmh), ("v", p.val("v"))], inc="v")
        for sub in range(4):
            cmh = p.chain("tensor", [
                (lambda e, sub=sub, kd=kd: e.matmul(
                    pb[0:128, 0:512],
                    xnTs[:, kd, sub * 128:(sub + 1) * 128],
                    wvTr[:, kd, :], start=(kd == 0), stop=(kd == 7)))
                for kd in range(8)],
                wait=[("v", p.val("v"))], inc="mmh")
            cev = p.do("vector", lambda e: e.tensor_copy(
                vstage[:, :], pb[0:128, 0:512]),
                wait=[("mmh", cmh), ("v", p.val("v")),
                      ("st1", p.val("st1"))], inc="v")
            p.do("sync", lambda e, tc=tc, sub=sub, cev=cev: e.dma_start(
                out=v_hbm[tc * 512 + sub * 128:tc * 512 + (sub + 1) * 128,
                          :], in_=vstage[:, :]),
                wait=[("v", cev)], inc="st1", amt=16)
        for ft in range(4):
            cmh = p.chain("tensor", [
                (lambda e, ft=ft, kd=kd: e.matmul(
                    pa[0:128, 0:512],
                    wqTr[:, kd, ft * 128:(ft + 1) * 128],
                    xnTs[:, kd, :], start=(kd == 0), stop=(kd == 7)))
                for kd in range(8)],
                wait=[("v", p.val("v"))], inc="mmh")
            p.do("vector", lambda e, ft=ft, tc=tc: e.tensor_copy(
                qT[:, ft, tc * 512:(tc + 1) * 512], pa[0:128, 0:512]),
                wait=[("mmh", cmh), ("v", p.val("v"))], inc="v")
    proj_done = p.val("v")
    proj_mm_done = p.val("mmh")
    proj_st = p.val("st1")

    cwoT = p.do("gpsimd", lambda e: e.dma_start(
        out=woTr[:, :, :],
        in_=wTs["wo"].rearrange("(kt kp) f -> kp kt f", kp=128)),
        wait=[("st0", wT_done), ("mmh", proj_mm_done), ("v", proj_done)],
        inc="wg1", amt=16)

    dpp = [pa, pb, C["pc"], C["pd"]]
    ptp = [pt1, pt2]
    pTs = [pTsb1, pTsb2]
    bank_free = [0] * 4
    msk_hist = {}
    st = {}

    def a1_S0(h, rt):
        JC = rt // 4 + 1
        dots = dots2[rt % 2]
        for jc in range(JC):
            bank = dpp[jc % 4]
            cmd = p.do("tensor", lambda e, h=h, rt=rt, jc=jc, bank=bank:
                       e.matmul(
                           bank[0:128, 0:512],
                           qT[(h % 2) * 64:(h % 2) * 64 + 64, h // 2,
                              rt * 128:(rt + 1) * 128],
                           kT[(h % 2) * 64:(h % 2) * 64 + 64, h // 2,
                              jc * 512:(jc + 1) * 512],
                           start=True, stop=True),
                       wait=(([("v", proj_done), ("wg1", cwoT)]
                              if jc == 0 else [])
                             + [("v", bank_free[jc % 4])]),
                       inc="mmA")
            bank_free[jc % 4] = p.do(
                "vector", lambda e, jc=jc, bank=bank, dots=dots:
                e.tensor_scalar(
                    dots[:, jc * 512:(jc + 1) * 512],
                    bank[0:128, 0:512], 0.125, None, ALU.mult),
                wait=[("mmA", cmd), ("v", p.val("v")),
                      ("v", msk_hist.get(rt % 2, 0)),
                      ("s", st.get(("exp", rt % 2), 0))], inc="v")
        st[("ev", rt)] = bank_free[(JC - 1) % 4]
        st[("J", rt)] = JC

    def a1_S1(rt):
        JC = st[("J", rt)]
        dots = dots2[rt % 2]
        p2b = p2b2[rt % 2]
        Wd = JC * 512
        chm = p.chain("vector", [
            (lambda e, rt=rt, JC=JC: e.tensor_scalar(
                cols[:, 10:11], jlall[:, rt:rt + 1],
                float((JC - 1) * 512), None, ALU.subtract)),
            (lambda e, dots=dots: e.tensor_scalar(
                mkbuf[:, :], jidx[:, :], cols[:, 10:11], None,
                ALU.is_ge)),
            (lambda e, JC=JC, dots=dots: e.copy_predicated(
                dots[:, (JC - 1) * 512:JC * 512], mkbuf[:, :],
                consts[:, 3:4].to_broadcast([128, 512]))),
            (lambda e, Wd=Wd, dots=dots: e.reduce_max(
                cols[:, 0:1], dots[:, 0:Wd], axis=AX.X)),
            (lambda e: e.tensor_scalar(
                cols[:, 1:2], cols[:, 0:1], -1.0, None, ALU.mult)),
        ], wait=[("v", st[("ev", rt)]), ("g", max(cjx, cjl))], inc="v",
            linked=True)
        cex = p.do("scalar", lambda e, Wd=Wd, dots=dots: e.activation(
            ebuf[:, 0:Wd], dots[:, 0:Wd], AF.Exp, bias=cols[:, 1:2],
            accum_out=cols[:, 2:3]),
            wait=[("v", chm), ("s", p.val("s"))], inc="s")
        clnz = p.do("scalar", lambda e: e.activation(
            cols[:, 13:14], cols[:, 2:3], AF.Ln, bias=0.0),
            wait=[("s", cex)], inc="s")
        chE = [
            (lambda e: e.reciprocal(cols[:, 3:4], cols[:, 2:3])),
            (lambda e, Wd=Wd, dots=dots: e.scalar_tensor_tensor(
                lnpb[:, 0:Wd], dots[:, 0:Wd], cols[:, 1:2],
                ebuf[:, 0:Wd], ALU.add, ALU.mult,
                accum_out=cols[:, 4:5])),
            (lambda e: e.tensor_mul(
                cols[:, 4:5], cols[:, 4:5], cols[:, 3:4])),
            (lambda e: e.tensor_tensor(
                cols[:, 5:6], cols[:, 13:14], cols[:, 4:5],
                ALU.subtract)),
            (lambda e: e.tensor_scalar(
                cols[:, 6:7], cols[:, 5:6], POLY[0], POLY[1],
                ALU.mult, ALU.add)),
        ]
        for ci in (POLY[2], POLY[3], POLY[4]):
            chE.append(lambda e, ci=ci: e.tensor_scalar(
                cols[:, 6:7], cols[:, 6:7], cols[:, 5:6], ci,
                ALU.mult, ALU.add))
        chE += [
            (lambda e: e.tensor_scalar(
                cols[:, 7:8], cols[:, 6:7], 1.0, None, ALU.max)),
            (lambda e: e.tensor_scalar(
                cols[:, 8:9], cols[:, 5:6], 0.5, None, ALU.is_gt)),
            (lambda e: e.tensor_scalar(
                cols[:, 7:8], cols[:, 7:8], 1.0, None, ALU.subtract)),
            (lambda e: e.tensor_mul(
                cols[:, 7:8], cols[:, 7:8], cols[:, 8:9])),
            (lambda e: e.tensor_scalar(
                cols[:, 7:8], cols[:, 7:8], 1.0, None, ALU.add)),
            (lambda e: e.tensor_mul(
                cols[:, 9:10], cols[:, 7:8], cols[:, 1:2])),
        ]
        cbb = p.chain("vector", chE,
                      wait=[("s", clnz), ("v", p.val("v"))], inc="v",
                      linked=True)
        msk_hist[rt % 2] = cbb
        cp2 = p.do("scalar", lambda e, Wd=Wd, dots=dots, p2b=p2b:
                   e.activation(
                       p2b[:, 0:Wd], dots[:, 0:Wd], AF.Exp,
                       scale=cols[:, 7:8], bias=cols[:, 9:10],
                       accum_out=cols[:, 11:12]),
                   wait=[("v", cbb), ("s", p.val("s")),
                         ("mmA", st.get(("avm", rt % 2), 0))], inc="s")
        st[("exp", rt % 2)] = cp2
        cpn = p.chain("vector", [
            (lambda e: e.reciprocal(cols[:, 12:13], cols[:, 11:12])),
            (lambda e, Wd=Wd, p2b=p2b: e.tensor_scalar(
                p2b[:, 0:Wd], p2b[:, 0:Wd], cols[:, 12:13], None,
                ALU.mult)),
        ], wait=[("s", cp2), ("v", p.val("v"))], inc="v", linked=True)
        st[("cpn", rt)] = cpn

    def a1_S2(h, rt, cvh):
        JC = st.pop(("J", rt))
        cpn = st.pop(("cpn", rt))
        p2b = p2b2[rt % 2]
        njt = (rt + 1)
        ctt_c = p.chain("tensor", [
            (lambda e, p2b=p2b: e.transpose(
                ptp[0][0:128, 0:128], p2b[:, 0:128], identb[:, :]))],
            wait=[("v", cpn), ("sp", p.val("sp"))], inc="mmA")
        for jt in range(njt):
            cpt = p.do("vector", lambda e, jt=jt: e.tensor_copy(
                pTs[jt % 2][:, :], ptp[jt % 2][0:128, 0:128]),
                wait=[("mmA", ctt_c), ("v", p.val("v"))], inc="sp")
            fns = [(lambda e, jt=jt, njt=njt: e.matmul(
                pav[0:64, 0:128], vhb[:, jt, :], pTs[jt % 2][:, :],
                start=(jt == 0), stop=(jt == njt - 1),
                skip_group_check=True))]
            if jt + 1 < njt:
                fns.append(lambda e, jt=jt, p2b=p2b: e.transpose(
                    ptp[(jt + 1) % 2][0:128, 0:128],
                    p2b[:, (jt + 1) * 128:(jt + 2) * 128],
                    identb[:, :]))
            ctt_c = p.chain("tensor", fns,
                            wait=[("sp", cpt), ("wg2", cvh)],
                            inc="mmA")
        st[("avm", rt % 2)] = ctt_c
        cav = p.do("vector", lambda e, cavd=ctt_c: e.tensor_copy(
            avsb[0:64, :], pav[0:64, 0:128]),
            wait=[("mmA", ctt_c), ("v", p.val("v")),
                  ("st2", p.val("st2"))], inc="v")
        p.do("sync", lambda e, h=h, rt=rt, cav=cav: e.dma_start(
            out=av_hbm[h * 64:(h + 1) * 64,
                       rt * 128:(rt + 1) * 128], in_=avsb[0:64, :]),
            wait=[("v", cav)], inc="st2", amt=16)

    for h in range(NH):
        cvh = p.do("gpsimd", lambda e, h=h: e.dma_start(
            out=vhb[:, :, :],
            in_=v_hbm[:, h * 64:(h + 1) * 64]
            .rearrange("(jt jp) d2 -> jp jt d2", jp=128)),
            wait=[("st1", proj_st), ("mmA", p.val("mmA"))],
            inc="wg2", amt=16)
        for rt in range(NRT):
            a1_S0(h, rt)
            if rt >= 1:
                a1_S1(rt - 1)
            if rt >= 2:
                a1_S2(h, rt - 2, cvh)
        a1_S1(NRT - 1)
        a1_S2(h, NRT - 2, cvh)
        a1_S2(h, NRT - 1, cvh)
    attn_done = p.val("st2")

    for rt in range(NRT):
        cavl = p.do("sync", lambda e, rt=rt: e.dma_start(
            out=avT[:, :, :],
            in_=av_hbm[0:512, rt * 128:(rt + 1) * 128]
            .rearrange("(kt kp) r -> kp kt r", kp=128)),
            wait=[("st2", attn_done), ("mmo", p.val("mmo"))],
            inc="avl", amt=16)
        fns = []
        for dc in range(2):
            bank = C["pc"] if dc == 0 else C["pd"]
            for kt in range(4):
                fns.append(lambda e, rt=rt, dc=dc, kt=kt, bank=bank:
                           e.matmul(
                               bank[0:128, 0:512], avT[:, kt, :],
                               woTr[:, kt, dc * 512:(dc + 1) * 512],
                               start=(kt == 0), stop=False))
            fns.append(lambda e, dc=dc, bank=bank: e.matmul(
                bank[0:128, 0:512], onesb[0:1, :],
                bosb[0:1, dc * 512:(dc + 1) * 512],
                start=False, stop=True))
        cmo = p.chain("tensor", fns,
                      wait=[("avl", cavl), ("wg1", cwoT), ("wld", wld0),
                            ("vG", p.val("vG")), ("v", C["cconst"])],
                      inc="mmo")
        cv2 = p.chain("vector", [
            (lambda e: e.tensor_copy(scr[:, 0:512], C["pc"][0:128, 0:512])),
            (lambda e: e.tensor_copy(scr[:, 512:1024],
                                     C["pd"][0:128, 0:512])),
        ], wait=[("mmo", cmo), ("st3", p.val("st3")), ("v", p.val("v"))],
            inc="vG")
        p.do("sync", lambda e, rt=rt: e.dma_start(
            out=myout[rt * 128:(rt + 1) * 128, :], in_=scr[:, :]),
            wait=[("vG", cv2)], inc="st3", amt=16)
